# version 07
# baseline (speedup 1.0000x reference)
import numpy as np
import ml_dtypes
import concourse.bass as bass
import concourse.mybir as mybir
from concourse.bass import AP
from concourse.bass_utils import run_bass_kernel_spmd
from concourse.alu_op_type import AluOpType as ALU

AF = mybir.ActivationFunctionType
F32 = mybir.dt.float32
BF16 = mybir.dt.bfloat16
U8 = mybir.dt.uint8
U16 = mybir.dt.uint16
BF = ml_dtypes.bfloat16

P = 128
B, S, D = 4, 2048, 2048
HEADS, HD, CH = 16, 128, 128
MLP = 16384
T = 1024
NT = T // P
NF = D // P
HG = 2
NHG = HEADS // HG
EPS_LN = 1e-5
EPS_RMS = 1e-6
NWIN = 32
NS = 16
DBG = []


def _fperm():
    fp = np.empty(D, np.int64)
    for m in range(8):
        fp[256 * m:256 * m + 128] = 256 * m + 2 * np.arange(128)
        fp[256 * m + 128:256 * m + 256] = 256 * m + 2 * np.arange(128) + 1
    return fp


def _host_prep(inputs):
    x = np.asarray(inputs["L0_attn_x"])
    nw = np.asarray(inputs["L0_attn_norm_weight"])

    def pk_bytes(a):
        a = np.asarray(a)
        return np.ascontiguousarray(a.view(np.uint8).reshape(a.shape[0], a.shape[1], 4)[:, :, 0])

    g_pk = np.ascontiguousarray(pk_bytes(inputs["L0_ffn_gate_W_packed"]).T)
    u_pk = np.ascontiguousarray(pk_bytes(inputs["L0_ffn_up_W_packed"]).T)
    d_pk_t = pk_bytes(inputs["L0_ffn_down_W_packed"]).T

    fp = _fperm()
    col_order = np.empty(D, np.int64)
    col_order[0::2] = fp[:1024]
    col_order[1::2] = fp[1024:]
    d_pk = np.ascontiguousarray(d_pk_t[:, col_order])

    g_sc_t = np.asarray(inputs["L0_ffn_gate_scales"]).T
    u_sc_t = np.asarray(inputs["L0_ffn_up_scales"]).T
    d_sc_t = np.asarray(inputs["L0_ffn_down_scales"]).T
    rep16 = lambda a: np.ascontiguousarray(np.repeat(a, 16, axis=0))
    g_sc_e = rep16(g_sc_t[:, 0::2]); g_sc_o = rep16(g_sc_t[:, 1::2])
    u_sc_e = rep16(u_sc_t[:, 0::2]); u_sc_o = rep16(u_sc_t[:, 1::2])
    d_sc_a = rep16(d_sc_t[:, fp[:1024]])
    d_sc_b = rep16(d_sc_t[:, fp[1024:]])

    hh = np.arange(HEADS, dtype=np.float64)
    log_lam = -(2.0 ** (-8.0 * (hh + 1.0) / HEADS))
    i = np.arange(CH, dtype=np.float64)
    diff = i[:, None] - i[None, :]
    dT = np.zeros((P, HEADS * CH), np.float64)
    decq = np.zeros((P, HEADS * CH), np.float64)
    deck = np.zeros((P, HEADS * CH), np.float64)
    lamc = np.zeros((P, HEADS * CH), np.float64)
    for h in range(HEADS):
        Dh = np.where(diff >= 0, np.exp(log_lam[h] * diff), 0.0)
        dT[:, h * CH:(h + 1) * CH] = Dh.T
        decq[:, h * CH:(h + 1) * CH] = np.exp(log_lam[h] * (i + 1.0))[None, :]
        deck[:, h * CH:(h + 1) * CH] = np.exp(log_lam[h] * (CH - 1.0 - i))[:, None]
        lamc[:, h * CH:(h + 1) * CH] = np.exp(log_lam[h] * CH)

    half = HD // 2
    theta = 10000.0 ** (-2.0 * np.arange(half, dtype=np.float64) / HD)

    def ccss(pos0):
        ang = np.arange(pos0, pos0 + T, dtype=np.float64)[None, :] * theta[:, None]
        c, s = np.cos(ang), np.sin(ang)
        return (np.concatenate([c, c], 0).astype(BF),
                np.concatenate([-s, s], 0).astype(BF))

    cc0, ss0 = ccss(0)
    cc1, ss1 = ccss(T)

    swap = np.zeros((P, P), BF)
    for j in range(64):
        swap[j + 64, j] = 1
        swap[j, j + 64] = 1

    shared = dict(
        wq=np.asarray(inputs["L0_attn_Wq"]), wk=np.asarray(inputs["L0_attn_Wk"]),
        wv=np.asarray(inputs["L0_attn_Wv"]), wo=np.asarray(inputs["L0_attn_Wo"]),
        wg1=np.asarray(inputs["L0_attn_Wg1"]), wg2=np.asarray(inputs["L0_attn_Wg2"]),
        nw=np.ascontiguousarray(nw.reshape(HEADS, HD).T).astype(np.float32),
        g_pk=g_pk, u_pk=u_pk, d_pk=d_pk,
        g_sc_e=g_sc_e, g_sc_o=g_sc_o, u_sc_e=u_sc_e, u_sc_o=u_sc_o,
        d_sc_a=d_sc_a, d_sc_b=d_sc_b,
        dT=dT.astype(BF), decq=decq.astype(BF), deck=deck.astype(BF),
        lamc=lamc.astype(BF), swap=swap,
        ident_bf=np.eye(P, dtype=BF), ident_f32=np.eye(P, dtype=np.float32),
        ones_ln=np.full((P, 1), 1.0 / D, BF),
        ones_rms=np.full((P, 1), 1.0 / HD, BF),
        ones_row=np.ones((1, P), np.float32),
        actk=np.tile(np.array([[EPS_LN, EPS_RMS, -0.5, 0.0, float(2.0 ** 126)]], np.float32), (P, 1)),
    )
    in_maps = []
    for core in range(8):
        bb, h = core // 2, core % 2
        own = np.ascontiguousarray(x[bb, h * T:(h + 1) * T])
        ctx = np.ascontiguousarray(x[bb, 0:T]) if h == 1 else own
        m = dict(shared)
        m["xin"] = np.ascontiguousarray(np.concatenate([ctx, own], 0))
        m["smask"] = np.full((P, 1), float(h), np.float32)
        m["cc_c"], m["ss_c"] = cc0, ss0
        m["cc_o"], m["ss_o"] = (cc1, ss1) if h == 1 else (cc0, ss0)
        in_maps.append(m)
    return in_maps


class Bld:
    ENGS = ("sync", "vector", "scalar", "tensor", "gpsimd")

    def __init__(self, nc):
        self.nc = nc
        self.q = {e: [] for e in self.ENGS}
        self.sem = {}
        self.cnt = {}

    def new_sem(self, name, handle):
        self.sem[name] = handle
        self.cnt[name] = 0

    def op(self, eng, fn):
        self.q[eng].append(fn)

    def opi(self, eng, fn, sem, k=1):
        h = self.sem[sem]
        self.q[eng].append(lambda e, fn=fn, h=h, k=k: fn(e).then_inc(h, k))
        self.cnt[sem] += k
        return self.cnt[sem]

    def dma(self, eng, out, in_, sem):
        self.wait(eng, sem)
        return self.opi(eng, lambda e, o=out, i=in_: e.dma_start(out=o, in_=i), sem, 16)

    def wait(self, eng, sem, v=None):
        hv = self.cnt[sem] if v is None else v
        if hv > 0:
            h = self.sem[sem]
            self.q[eng].append(lambda e, h=h, hv=hv: e.wait_ge(h, hv))

    def barrier(self):
        h = self.sem["bar"]
        for e in self.ENGS:
            if e == "tensor":
                self.q[e].append(lambda en, h=h: en.nop().then_inc(h, 1))
            elif e in ("vector", "gpsimd"):
                self.q[e].append(lambda en, h=h: en.engine_nop().then_inc(h, 1))
            else:
                self.q[e].append(lambda en, h=h: en.nop().then_inc(h, 1))
        self.cnt["bar"] += 5
        tgt = self.cnt["bar"]
        for e in self.ENGS:
            self.q[e].append(lambda en, h=h, v=tgt: en.wait_ge(h, v))


def build_nc():
    nc = bass.Bass("TRN2", target_bir_lowering=False, debug=False,
                   detect_race_conditions=False)
    d = {}

    def din(name, shape, dt):
        d[name] = nc.dram_tensor(name, shape, dt, kind="ExternalInput").ap()

    din("xin", [2 * T, D], F32)
    din("smask", [P, 1], F32)
    for n in ("wq", "wk", "wv", "wo"):
        din(n, [D, D], F32)
    din("wg1", [D, HD], F32)
    din("wg2", [HD, D], F32)
    din("nw", [P, HEADS], F32)
    din("g_pk", [1024, MLP], U8)
    din("u_pk", [1024, MLP], U8)
    din("d_pk", [MLP // 2, D], U8)
    din("g_sc_e", [1024, MLP // 2], F32); din("g_sc_o", [1024, MLP // 2], F32)
    din("u_sc_e", [1024, MLP // 2], F32); din("u_sc_o", [1024, MLP // 2], F32)
    din("d_sc_a", [MLP // 2, 1024], F32); din("d_sc_b", [MLP // 2, 1024], F32)
    din("dT", [P, HEADS * CH], BF16); din("decq", [P, HEADS * CH], BF16)
    din("deck", [P, HEADS * CH], BF16); din("lamc", [P, HEADS * CH], BF16)
    din("swap", [P, P], BF16); din("ident_bf", [P, P], BF16); din("ident_f32", [P, P], F32)
    din("ones_ln", [P, 1], BF16); din("ones_rms", [P, 1], BF16); din("ones_row", [1, P], F32)
    din("actk", [P, 5], F32)
    for n in ("cc_c", "ss_c", "cc_o", "ss_o"):
        din(n, [P, T], BF16)
    d_out = nc.dram_tensor("out", [T, D], F32, kind="ExternalOutput").ap()
    d_dbg = {}
    for nm in DBG:
        d_dbg[nm] = nc.dram_tensor("dbg_" + nm, [P, 32 * 1024], F32, kind="ExternalOutput").ap()

    KB = 1024
    AB = 207 * KB

    with (
        nc.sbuf_tensor("arena", [P, AB], U8) as arena,
        nc.psum_tensor("PS", [P, 8, 512], F32) as PS,
        nc.semaphore("s_wl") as s_wl,
        nc.semaphore("s_gl") as s_gl,
        nc.semaphore("s_sl") as s_sl,
        nc.semaphore("s_pe") as s_pe,
        nc.semaphore("s_dv") as s_dv,
        nc.semaphore("s_ac") as s_ac,
        nc.semaphore("s_dv2") as s_dv2,
        nc.semaphore("s_pe2") as s_pe2,
        nc.semaphore("s_bar") as s_bar,
        nc.semaphore("s_st") as s_st,
        nc.Block() as block,
    ):
        b = Bld(nc)
        for nm, h in (("wl", s_wl), ("gl", s_gl), ("sl", s_sl), ("pe", s_pe), ("dv", s_dv),
                      ("ac", s_ac), ("dv2", s_dv2), ("pe2", s_pe2), ("bar", s_bar),
                      ("st", s_st)):
            b.new_sem(nm, h)

        def bank(i, dt=F32, n=None):
            v = PS[:, i, :]
            if dt == BF16:
                v = v.bitcast(BF16)
            if n is not None:
                v = v[:, 0:n]
            return v

        off = [0]

        def carve(nb, name=""):
            o = off[0]
            assert o + nb <= AB, f"SBUF overflow {name}: {o}+{nb}"
            off[0] += (nb + 63) & ~63
            return o

        def view(o, shape, dt):
            esz = {F32: 4, BF16: 2, U16: 2, U8: 1}[dt]
            nb = int(np.prod(shape[1:])) * esz
            a = arena[:, o:o + nb].bitcast(dt)
            if len(shape) == 3:
                a = a.rearrange("p (a c) -> p a c", a=shape[1])
            elif len(shape) == 4:
                a = a.rearrange("p (a b c) -> p a b c", a=shape[1], b=shape[2])
            return a

        c_dT = view(carve(4 * KB, "dT"), [P, HEADS, CH], BF16)
        c_decq = view(carve(4 * KB), [P, HEADS, CH], BF16)
        c_deck = view(carve(4 * KB), [P, HEADS, CH], BF16)
        c_lamc = view(carve(4 * KB), [P, HEADS, CH], BF16)
        c_cc = {"c": view(carve(2 * KB), [P, T], BF16)}
        c_ss = {"c": view(carve(2 * KB), [P, T], BF16)}
        c_cc["o"] = view(carve(2 * KB), [P, T], BF16)
        c_ss["o"] = view(carve(2 * KB), [P, T], BF16)
        c_swap = view(carve(256), [P, P], BF16)
        c_idb = view(carve(256), [P, P], BF16)
        c_idf = view(carve(512), [P, P], F32)
        c_1ln = view(carve(64), [P, 1], BF16)
        c_1rms = view(carve(64), [P, 1], BF16)
        c_1row = view(carve(512), [P, P], F32)
        c_nw = view(carve(64), [P, HEADS], F32)
        c_actk = view(carve(64), [P, 5], F32)
        c_mask = view(carve(64), [P, 1], F32)
        c_wg2 = view(carve(4 * KB), [P, D], BF16)
        c_wg1 = view(carve(4 * KB), [P, NF, HD], BF16)
        oonrm = carve(32 * KB, "o_nrm")
        o_nrm = view(oonrm, [P, HEADS, T], BF16)
        g1T = view(carve(2 * KB), [P, T], BF16)
        base0 = off[0]

        offA = [base0]

        def carveA(nb, name=""):
            o = offA[0]
            assert o + nb <= AB, f"A overflow {name}: {o}+{nb}"
            offA[0] += (nb + 63) & ~63
            return o

        xln = view(carveA(32 * KB, "xln"), [P, NF, T], BF16)
        oxcln = carveA(32 * KB, "xcln")
        xcln = view(oxcln, [P, NF, T], BF16)
        wstage = view(carveA(16 * KB, "wstage"), [P, 2, 2048], F32)
        wbf = view(carveA(16 * KB, "wbf"), [P, 2, NF, HG * HD], BF16)
        qsb = view(carveA(4 * KB, "q"), [P, HG, T], BF16)
        ksb = view(carveA(4 * KB, "k"), [P, HG, T], BF16)
        vsb = view(carveA(4 * KB, "v"), [P, NT, HG * HD], BF16)
        S32 = view(carveA(8 * KB, "S"), [P, HEADS, HD], F32)
        Sbf = view(carveA(4 * KB, "Sbf"), [P, HEADS, HD], BF16)
        xstg = view(carveA(8 * KB, "xstg"), [P, 1, D], F32)
        scr2 = view(carveA(4 * KB, "scr2"), [P, D], BF16)
        scrq = view(carveA(1 * KB, "scrq"), [P, HG * CH], BF16)
        scrk = view(carveA(1 * KB, "scrk"), [P, HG * CH], BF16)
        scrs = view(carveA(1 * KB, "scrs"), [P, HG * CH], BF16)
        stat = view(carveA(256, "stat"), [P, 64], F32)
        sgt = view(oxcln, [P, 2, T], BF16)
        rms1 = view(oxcln + 4 * KB, [P, T], F32)
        assert offA[0] <= AB, "A total %d" % offA[0]

        offB = [base0]

        def carveB(nb, name=""):
            o = offB[0]
            assert o + nb <= AB, f"B overflow {name}"
            offB[0] += (nb + 63) & ~63
            return o

        xT = view(carveB(64 * KB, "xT"), [P, NF, T], F32)
        wostg = view(carveB(32 * KB, "wostg"), [P, 2, NF, 256], F32)
        wobf = view(carveB(16 * KB, "wobf"), [P, 2, NF, 256], BF16)
        xstg2 = view(carveB(8 * KB, "xstg2"), [P, 1, D], F32)
        sqs = view(carveB(2 * KB, "sqs"), [P, T], BF16)
        xbs = view(carveB(2 * KB, "xbs"), [P, T], BF16)
        lnsc = view(carveB(4 * KB, "lnsc"), [P, T], F32)
        rms2 = view(carveB(4 * KB, "rms2"), [P, T], F32)
        assert offB[0] <= AB

        offC = [base0 + 64 * KB]

        def carveC(nb, name=""):
            o = offC[0]
            assert o + nb <= AB, f"C overflow {name}: {o}+{nb}"
            offC[0] += (nb + 63) & ~63
            return o

        xln2 = view(oonrm, [P, NF, T], BF16)
        inter = view(carveC(16 * KB, "inter"), [P, 8, T], BF16)
        ogdec = carveC(8 * KB, "gdec")
        gdec = view(ogdec, [P, 2, 8, 256], BF16)
        oudec = carveC(8 * KB, "udec")
        udec = view(oudec, [P, 2, 8, 256], BF16)
        wddec = view(carveC(4 * KB, "wddec"), [P, 8, 256], BF16)
        gpk = view(carveC(8 * KB, "gpk"), [P, 2, 8, 512], U8)
        upk = view(carveC(8 * KB, "upk"), [P, 2, 8, 512], U8)
        dpk = view(carveC(8 * KB, "dpk"), [P, 4, D], U8)
        gsc = view(carveC(8 * KB, "gsc"), [P, 8, 256], F32)
        gscb = view(carveC(4 * KB, "gscb"), [P, 8, 256], BF16)
        gl = view(0, [P, 2048], BF16)
        dsc = view(4 * KB, [P, 4, 256], F32)
        dscb = view(8 * KB, [P, 4, 256], BF16)
        scrC = view(10 * KB, [P, 8, 256], U16)
        up_sb = view(16 * KB, [P, 2048], BF16)
        wddec2 = view(20 * KB, [P, 8, 256], BF16)
        assert offC[0] <= AB, f"C total {offC[0]}"
        ostg = view(ogdec, [P, 2, D], F32)

        def dbg_dump(nm, ap2d):
            if nm not in d_dbg:
                return
            b.barrier()
            n = int(np.prod(ap2d.shape[1:]))
            if ap2d.dtype == BF16:
                for i in range(0, n, 2048):
                    w = min(2048, n - i)
                    b.op("vector", lambda e, i=i, w=w: e.tensor_copy(
                        out=xstg[:, 0, 0:w] if w <= 2048 else None,
                        in_=ap2d.rearrange("p a c -> p (a c)")[:, i:i + w] if len(ap2d.shape) == 3 else ap2d[:, i:i + w]))
                    b.opi("vector", lambda e: e.engine_nop(), "dv")
                    b.wait("scalar", "dv")
                    b.dma("scalar", d_dbg[nm][:, i:i + w], xstg[:, 0, 0:w], "sl")
                    b.wait("vector", "sl")
            else:
                a2 = ap2d.rearrange("p a c -> p (a c)") if len(ap2d.shape) == 3 else ap2d
                b.dma("scalar", d_dbg[nm][:, 0:n], a2, "sl")
                b.wait("scalar", "sl")
            b.barrier()

        for t_, n_ in ((c_dT.rearrange("p a c -> p (a c)"), "dT"),
                       (c_decq.rearrange("p a c -> p (a c)"), "decq"),
                       (c_deck.rearrange("p a c -> p (a c)"), "deck"),
                       (c_lamc.rearrange("p a c -> p (a c)"), "lamc"),
                       (c_cc["c"], "cc_c"), (c_ss["c"], "ss_c"),
                       (c_cc["o"], "cc_o"), (c_ss["o"], "ss_o"),
                       (c_swap, "swap"), (c_idb, "ident_bf"), (c_idf, "ident_f32"),
                       (c_1ln, "ones_ln"), (c_1rms, "ones_rms"),
                       (c_nw, "nw"), (c_mask, "smask"), (c_actk, "actk")):
            b.dma("gpsimd", t_, d[n_], "gl")
        b.dma("gpsimd", c_1row[0:1, :], d["ones_row"], "gl")
        b.dma("sync", wstage[:, 0, :], d["wg2"], "wl")
        for par in range(2):
            src = AP(tensor=d["wg1"].tensor, offset=par * HD,
                     ap=[[2 * HD, 128], [256 * HD, 8], [1, HD]])
            dst = wstage[:, 1, par * 1024:(par + 1) * 1024].rearrange("p (a c) -> p a c", a=8)
            b.dma("sync", dst, src, "wl")
        b.wait("vector", "wl")
        b.op("vector", lambda e: e.tensor_copy(out=c_wg2[:], in_=wstage[:, 0, :]))
        for par in range(2):
            for m in range(8):
                s_ = wstage[:, 1, par * 1024 + m * HD: par * 1024 + (m + 1) * HD]
                b.op("vector", lambda e, s_=s_, kc=2 * m + par: e.tensor_copy(
                    out=c_wg1[:, kc, :], in_=s_))
        b.op("vector", lambda e: e.memset(S32[:].rearrange("p a c -> p (a c)"), 0.0))
        b.opi("vector", lambda e: e.memset(Sbf[:].rearrange("p a c -> p (a c)"), 0.0), "dv")
        wgpre_dv = b.cnt["dv"]

        b.wait("vector", "gl")
        tr_free = {0: 0, 1: 0}
        ln_done = {}
        ln_tr_pe = [0]
        for ti in range(2 * NT):
            if ti >= 1:
                b.wait("gpsimd", "dv", ln_done[ti - 1])
            b.dma("gpsimd", xstg[:, 0, :], d["xin"][ti * P:(ti + 1) * P, :], "gl")
            b.wait("vector", "gl")
            xt = xstg[:, 0, :]
            b.op("vector", lambda e, xt=xt: e.reduce_sum(stat[:, 0:1], xt, axis=mybir.AxisListType.X))
            b.op("vector", lambda e, xt=xt: e.tensor_tensor(out=scr2[:], in0=xt, in1=xt, op=ALU.mult))
            b.opi("vector", lambda e: e.reduce_sum(stat[:, 1:2], scr2[:], axis=mybir.AxisListType.X), "st")
            b.wait("vector", "st")
            b.opi("vector", lambda e: e.tensor_scalar(out=stat[:, 2:3], in0=stat[:, 0:1],
                                                      scalar1=1.0 / D, scalar2=None, op0=ALU.mult), "st")
            b.wait("vector", "st")
            b.opi("vector", lambda e: e.tensor_tensor(out=stat[:, 4:5], in0=stat[:, 2:3],
                                                      in1=stat[:, 2:3], op=ALU.mult), "st")
            b.wait("vector", "st")
            b.opi("vector", lambda e: e.scalar_tensor_tensor(
                out=stat[:, 5:6], in0=stat[:, 1:2], scalar=1.0 / D, in1=stat[:, 4:5],
                op0=ALU.mult, op1=ALU.subtract), "dv")
            b.wait("vector", "dv")
            b.opi("vector", lambda e: e.engine_nop(), "st")
            b.wait("scalar", "st")
            b.opi("scalar", lambda e: e.activation(out=stat[:, 6:7], in_=stat[:, 5:6],
                                                   func=AF.Ln, bias=c_actk[:, 0:1]), "st")
            b.wait("scalar", "st")
            b.opi("scalar", lambda e: e.activation(out=stat[:, 7:8], in_=stat[:, 6:7],
                                                   func=AF.Exp, scale=c_actk[:, 2:3]), "ac")
            b.wait("scalar", "ac")
            b.opi("scalar", lambda e: e.nop(), "st")
            b.wait("vector", "st")
            b.wait("vector", "pe", ln_tr_pe[0])
            b.opi("vector", lambda e, xt=xt: e.tensor_scalar(
                out=scr2[:], in0=xt, scalar1=stat[:, 2:3], scalar2=stat[:, 7:8],
                op0=ALU.subtract, op1=ALU.mult), "dv")
            ln_done[ti] = b.cnt["dv"]
            dst = xcln if ti < NT else xln
            tc = ti % NT
            b.wait("tensor", "dv", ln_done[ti])
            for g4 in range(4):
                bk = g4 % 2
                b.wait("tensor", "dv2", tr_free[bk])
                for k4 in range(4):
                    fc = g4 * 4 + k4
                    m, par = fc // 2, fc % 2
                    srct = scr2[:, 256 * m + par: 256 * (m + 1): 2]
                    pv = bank(bk, BF16)[:, k4 * P:(k4 + 1) * P]
                    b.opi("tensor", lambda e, pv=pv, s_=srct: e.transpose(pv, s_, c_idb[:]), "pe")
                b.wait("vector", "pe")
                pvg = bank(bk, BF16)[:, 0:512].rearrange("p (a c) -> p a c", a=4)
                dstg = dst[:, g4 * 4:(g4 + 1) * 4, tc * P:(tc + 1) * P]
                tr_free[bk] = b.opi("vector", lambda e, pvg=pvg, dstg=dstg: e.tensor_copy(
                    out=dstg, in_=pvg), "dv2")
            ln_tr_pe[0] = b.cnt["pe"]
        xln_done = b.cnt["dv2"]
        dbg_dump("xln", xln)
        dbg_dump("xcln", xcln)

        b.wait("tensor", "dv2", xln_done)
        b.wait("tensor", "dv", wgpre_dv)
        for th in range(2):
            for kc in range(NF):
                b.opi("tensor", lambda e, kc=kc, th=th: e.matmul(
                    bank(2)[:, :], c_wg1[:, kc, :], xln[:, kc, th * 512:(th + 1) * 512],
                    start=(kc == 0), stop=(kc == NF - 1)), "pe")
            b.wait("vector", "pe")
            b.opi("vector", lambda e, th=th: e.tensor_copy(
                out=g1T[:, th * 512:(th + 1) * 512], in_=bank(2)[:, :]), "dv")
            b.wait("tensor", "dv")

        wload_state = {"idx": 0, "peuse": {0: 0, 1: 0},
                       "conv": {0: wgpre_dv, 1: wgpre_dv}}

        def wload(name, hg):
            st = wload_state
            wbuf = st["idx"] % 2
            st["idx"] += 1
            b.wait("sync", "dv", st["conv"][1 - wbuf])
            for par in range(2):
                src = AP(tensor=d[name].tensor, offset=par * D + hg * HG * HD,
                         ap=[[2 * D, 128], [256 * D, 8], [1, HG * HD]])
                dst = wstage[:, par, :].rearrange("p (a c) -> p a c", a=8)
                b.dma("sync", dst, src, "wl")
            b.wait("vector", "wl")
            b.wait("vector", "pe", st["peuse"][wbuf])
            for par in range(2):
                for m in range(8):
                    s_ = wstage[:, par, m * HG * HD:(m + 1) * HG * HD]
                    b.op("vector", lambda e, s_=s_, kc=2 * m + par, wbuf=wbuf: e.tensor_copy(
                        out=wbf[:, wbuf, kc, :], in_=s_))
            cv = b.opi("vector", lambda e: e.engine_nop(), "dv")
            st["conv"][wbuf] = cv
            return wbuf, cv

        def attn_pass(is_own):
            xl = xln if is_own else xcln
            key = "o" if is_own else "c"
            st = wload_state
            for hg in range(NHG):
                silu_bank = {0: 0, 1: 0}

                def proj_headmajor(wname, dstb):
                    wbuf, cv = wload(wname, hg)
                    b.wait("tensor", "dv", cv)
                    for hh in range(HG):
                        for th in range(2):
                            bk = (hh * 2 + th) % 2
                            b.wait("tensor", "ac", silu_bank[bk])
                            for kc in range(NF):
                                b.opi("tensor", lambda e, kc=kc, hh=hh, th=th, bk=bk, wbuf=wbuf: e.matmul(
                                    bank(bk)[:, :], wbf[:, wbuf, kc, hh * HD:(hh + 1) * HD],
                                    xl[:, kc, th * 512:(th + 1) * 512],
                                    start=(kc == 0), stop=(kc == NF - 1)), "pe")
                            b.wait("scalar", "pe")
                            silu_bank[bk] = b.opi("scalar", lambda e, hh=hh, th=th, bk=bk, dstb=dstb: e.activation(
                                out=dstb[:, hh, th * 512:(th + 1) * 512], in_=bank(bk)[:, :],
                                func=AF.Silu), "ac")
                    st["peuse"][wbuf] = b.cnt["pe"]
                    return b.cnt["ac"]

                if is_own:
                    q_ac = proj_headmajor("wq", qsb)
                    if hg == 0 and "qsnap1" in d_dbg:
                        b.wait("scalar", "ac", q_ac)
                        b.dma("scalar", d_dbg["qsnap1"][:, 0:1024],
                              qsb[:].rearrange("p a c -> p (a c)").bitcast(F32), "sl")
                    if hg == 0:
                        dbg_dump("qraw", qsb)
                        dbg_dump("kraw", ksb)
                k_ac = proj_headmajor("wk", ksb)
                if is_own and hg == 0 and "qsnap2" in d_dbg:
                    b.wait("scalar", "ac", k_ac)
                    b.dma("scalar", d_dbg["qsnap2"][:, 0:1024],
                          qsb[:].rearrange("p a c -> p (a c)").bitcast(F32), "sl")
                if hg == 0 and is_own:
                    dbg_dump("qpk", qsb)
                    dbg_dump("kpk", ksb)

                wbuf, cv = wload("wv", hg)
                b.wait("tensor", "dv", cv)
                b.wait("tensor", "ac", k_ac)
                v_free = {0: 0, 1: 0}
                for tc in range(NT):
                    bk = tc % 2
                    b.wait("tensor", "dv2", v_free[bk])
                    for kc in range(NF):
                        b.opi("tensor", lambda e, kc=kc, tc=tc, bk=bk, wbuf=wbuf: e.matmul(
                            bank(bk)[:, 0:HG * HD], xl[:, kc, tc * P:(tc + 1) * P],
                            wbf[:, wbuf, kc, :], start=(kc == 0), stop=(kc == NF - 1)), "pe")
                    b.wait("vector", "pe")
                    v_free[bk] = b.opi("vector", lambda e, tc=tc, bk=bk: e.tensor_copy(
                        out=vsb[:, tc, :], in_=bank(bk)[:, 0:HG * HD]), "dv2")
                st["peuse"][wbuf] = b.cnt["pe"]
                v_dv = b.cnt["dv2"]
                if hg == 0 and is_own:
                    dbg_dump("qpv", qsb)

                rot_bank = {4: 0, 5: 0}

                def rotate(buf3, key):
                    for hh in range(HG):
                        for th in range(2):
                            bk = 4 + th
                            sl = slice(th * 512, (th + 1) * 512)
                            b.wait("tensor", "dv", rot_bank[bk])
                            b.opi("tensor", lambda e, hh=hh, sl=sl, bk=bk, buf3=buf3: e.matmul(
                                bank(bk)[:, :], c_swap[:], buf3[:, hh, sl],
                                start=True, stop=True), "pe")
                            b.wait("vector", "pe")
                            b.op("vector", lambda e, hh=hh, sl=sl, buf3=buf3, key=key: e.tensor_tensor(
                                out=scr2[:, 0:512], in0=buf3[:, hh, sl], in1=c_cc[key][:, sl],
                                op=ALU.mult))
                            b.op("vector", lambda e, sl=sl, bk=bk, key=key: e.tensor_tensor(
                                out=scr2[:, 512:1024], in0=bank(bk)[:, :], in1=c_ss[key][:, sl],
                                op=ALU.mult))
                            rot_bank[bk] = b.opi("vector", lambda e, hh=hh, sl=sl, buf3=buf3: e.tensor_tensor(
                                out=buf3[:, hh, sl], in0=scr2[:, 0:512], in1=scr2[:, 512:1024],
                                op=ALU.add), "dv")

                if is_own:
                    rotate(qsb, key)
                    if hg == 0:
                        dbg_dump("qrot", qsb)
                rotate(ksb, key)
                rot_dv = b.cnt["dv"]
                if hg == 0 and is_own:
                    dbg_dump("q0", qsb)
                    dbg_dump("k0", ksb)
                    dbg_dump("v0", vsb)

                h0 = hg * HG
                Ssl32 = S32[:, h0:h0 + HG, :].rearrange("p a c -> p (a c)")
                Sslbf = Sbf[:, h0:h0 + HG, :].rearrange("p a c -> p (a c)")
                decq_sl = c_decq[:, h0:h0 + HG, :].rearrange("p a c -> p (a c)")
                deck_sl = c_deck[:, h0:h0 + HG, :].rearrange("p a c -> p (a c)")
                lamc_sl = c_lamc[:, h0:h0 + HG, :].rearrange("p a c -> p (a c)")
                dTsl = c_dT[:, h0:h0 + HG, :].rearrange("p a c -> p (a c)")
                b.wait("tensor", "dv", rot_dv)
                b.wait("tensor", "dv2", v_dv)
                sc_free = 0
                su_done = b.cnt["dv"]
                o_free = 0
                mm1_done = 0
                for c in range(NT):
                    csl = slice(c * CH, (c + 1) * CH)
                    if is_own:
                        b.wait("vector", "pe", mm1_done)
                        b.opi("vector", lambda e, csl=csl, decq_sl=decq_sl: e.tensor_tensor(
                            out=scrq[:].rearrange("p (a c) -> p a c", a=HG),
                            in0=qsb[:, :, csl],
                            in1=decq_sl.rearrange("p (a c) -> p a c", a=HG), op=ALU.mult), "dv")
                        qd_dv = b.cnt["dv"]
                        b.wait("tensor", "dv", qd_dv)
                        b.wait("tensor", "dv2", o_free)
                        for hh in range(HG):
                            b.opi("tensor", lambda e, hh=hh, h0=h0: e.matmul(
                                bank(hh)[:, 0:CH],
                                Sbf[:, h0 + hh, :],
                                scrq[:, hh * CH:(hh + 1) * CH],
                                start=True, stop=False), "pe")
                        mm1_done = b.cnt["pe"]
                        b.wait("tensor", "dv", sc_free)
                        for hh in range(HG):
                            b.opi("tensor", lambda e, hh=hh, csl=csl: e.matmul(
                                bank(2)[:, hh * CH:(hh + 1) * CH],
                                ksb[:, hh, csl], qsb[:, hh, csl], start=True, stop=True), "pe")
                        b.wait("vector", "pe")
                        b.opi("vector", lambda e, dTsl=dTsl: e.tensor_tensor(
                            out=scrs[:], in0=bank(2)[:, 0:HG * CH], in1=dTsl, op=ALU.mult), "dv")
                        sc_free = b.cnt["dv"]
                        b.wait("tensor", "dv", sc_free)
                        for hh in range(HG):
                            b.opi("tensor", lambda e, hh=hh, c=c: e.matmul(
                                bank(hh)[:, 0:CH],
                                vsb[:, c, hh * HD:(hh + 1) * HD], scrs[:, hh * CH:(hh + 1) * CH],
                                start=False, stop=True), "pe")
                        b.wait("vector", "pe")
                        o_free = b.opi("vector", lambda e, c=c, h0=h0: e.tensor_copy(
                            out=o_nrm[:, h0, c * CH:(c + 1) * CH], in_=bank(0)[:, 0:CH]), "dv2")
                        o_free = b.opi("vector", lambda e, c=c, h0=h0: e.tensor_copy(
                            out=o_nrm[:, h0 + 1, c * CH:(c + 1) * CH], in_=bank(1)[:, 0:CH]), "dv2")
                    b.wait("tensor", "dv", su_done)
                    for hh in range(HG):
                        b.opi("tensor", lambda e, hh=hh, csl=csl: e.transpose(
                            bank(3, BF16)[:, hh * CH:(hh + 1) * CH],
                            ksb[:, hh, csl], c_idb[:]), "pe")
                    b.wait("vector", "pe")
                    b.opi("vector", lambda e, deck_sl=deck_sl: e.tensor_tensor(
                        out=scrk[:], in0=bank(3, BF16)[:, 0:HG * CH], in1=deck_sl, op=ALU.mult), "dv")
                    kd_dv = b.cnt["dv"]
                    b.wait("tensor", "dv", kd_dv)
                    for hh in range(HG):
                        b.opi("tensor", lambda e, hh=hh, c=c: e.matmul(
                            bank(4)[:, hh * CH:(hh + 1) * CH],
                            scrk[:, hh * CH:(hh + 1) * CH], vsb[:, c, hh * HD:(hh + 1) * HD],
                            start=True, stop=True), "pe")
                    b.wait("vector", "pe")
                    b.op("vector", lambda e, Ssl32=Ssl32, lamc_sl=lamc_sl: e.tensor_tensor(
                        out=Ssl32, in0=Ssl32, in1=lamc_sl, op=ALU.mult))
                    b.op("vector", lambda e, Ssl32=Ssl32: e.tensor_tensor(
                        out=Ssl32, in0=Ssl32, in1=bank(4)[:, 0:HG * CH], op=ALU.add))
                    b.opi("vector", lambda e, Ssl32=Ssl32, Sslbf=Sslbf: e.tensor_copy(
                        out=Sslbf, in_=Ssl32), "dv")
                    su_done = b.cnt["dv"]
                b.wait("tensor", "dv", su_done)

        attn_pass(False)
        b.op("vector", lambda e: e.tensor_scalar(
            out=S32[:].rearrange("p a c -> p (a c)"), in0=S32[:].rearrange("p a c -> p (a c)"),
            scalar1=c_mask[:, 0:1], scalar2=None, op0=ALU.mult))
        b.opi("vector", lambda e: e.tensor_copy(
            out=Sbf[:].rearrange("p a c -> p (a c)"), in_=S32[:].rearrange("p a c -> p (a c)")), "dv")
        attn_pass(True)
        dbg_dump("o", o_nrm)
        dbg_dump("g1T", g1T)

        b.wait("tensor", "dv")
        b.wait("tensor", "dv2")
        p4_pe_gate = 0
        for h in range(HEADS):
            b.wait("tensor", "dv", p4_pe_gate)
            for th in range(2):
                b.opi("tensor", lambda e, h=h, th=th: e.matmul(
                    bank(th)[:, :], c_wg2[:, h * HD:(h + 1) * HD], g1T[:, th * 512:(th + 1) * 512],
                    start=True, stop=True), "pe")
            b.wait("scalar", "pe")
            for th in range(2):
                b.opi("scalar", lambda e, th=th: e.activation(
                    out=sgt[:, 0, th * 512:(th + 1) * 512], in_=bank(th)[:, :], func=AF.Sigmoid), "ac")
            b.wait("scalar", "ac")
            b.opi("scalar", lambda e: e.nop(), "st")
            b.wait("vector", "st")
            b.opi("vector", lambda e, h=h: e.tensor_tensor(
                out=sgt[:, 1, :], in0=o_nrm[:, h, :], in1=sgt[:, 0, :], op=ALU.mult), "st")
            b.wait("vector", "st")
            b.opi("vector", lambda e: e.tensor_tensor(
                out=scr2[:, 0:T], in0=sgt[:, 1, :], in1=sgt[:, 1, :], op=ALU.mult), "dv")
            sq_dv = b.cnt["dv"]
            p4_pe_gate = sq_dv
            b.wait("tensor", "dv", sq_dv)
            b.wait("tensor", "ac")
            for th in range(2):
                b.opi("tensor", lambda e, th=th: e.matmul(
                    PS[0:1, 2 + th, :], c_1rms[:], scr2[:, th * 512:(th + 1) * 512],
                    start=True, stop=True), "pe")
            b.wait("scalar", "pe")
            b.op("scalar", lambda e: e.activation(out=rms1[0:1, 0:512], in_=PS[0:1, 2, :],
                                                  func=AF.Ln, bias=c_actk[0:1, 1:2]))
            b.opi("scalar", lambda e: e.activation(out=rms1[0:1, 512:1024], in_=PS[0:1, 3, :],
                                                   func=AF.Ln, bias=c_actk[0:1, 1:2]), "st")
            b.wait("scalar", "st")
            b.opi("scalar", lambda e: e.activation(out=rms1[0:1, 0:1024], in_=rms1[0:1, 0:1024],
                                                   func=AF.Exp, scale=c_actk[0:1, 2:3]), "ac")
            b.wait("scalar", "ac")
            b.opi("scalar", lambda e: e.nop(), "st")
            b.wait("tensor", "st")
            for th in range(2):
                b.opi("tensor", lambda e, th=th: e.matmul(
                    bank(4 + th)[:, :], c_1row[0:1, :], rms1[0:1, th * 512:(th + 1) * 512],
                    start=True, stop=True), "pe")
            b.wait("vector", "pe")
            for th in range(2):
                b.opi("vector", lambda e, th=th: e.tensor_tensor(
                    out=scr2[:, T + th * 512:T + (th + 1) * 512],
                    in0=sgt[:, 1, th * 512:(th + 1) * 512],
                    in1=bank(4 + th)[:, :], op=ALU.mult), "st")
            b.wait("vector", "st")
            b.opi("vector", lambda e, h=h: e.tensor_scalar(
                out=o_nrm[:, h, :], in0=scr2[:, T:2 * T], scalar1=c_nw[:, h:h + 1], scalar2=None,
                op0=ALU.mult), "dv")
            p4_pe_gate = b.cnt["dv"]

        dbg_dump("normed", o_nrm)

        b.barrier()
        xre_free = [0]
        tr_free = {0: 0, 1: 0}
        for tc in range(NT):
            b.wait("scalar", "pe2", xre_free[0])
            b.dma("scalar", xstg2[:, 0, :], d["xin"][(NT + tc) * P:(NT + tc + 1) * P, :], "sl")
            b.wait("tensor", "sl")
            for g4 in range(4):
                bk = g4 % 2
                b.wait("tensor", "dv2", tr_free[bk])
                for k4 in range(4):
                    fc = g4 * 4 + k4
                    m, par = fc // 2, fc % 2
                    srct = xstg2[:, 0, 256 * m + par: 256 * (m + 1): 2]
                    b.opi("tensor", lambda e, bk=bk, k4=k4, s_=srct: e.transpose(
                        bank(bk)[:, k4 * P:(k4 + 1) * P], s_, c_idf[:]), "pe")
                b.wait("vector", "pe")
                tr_free[bk] = b.opi("vector", lambda e, bk=bk, g4=g4, tc=tc: e.tensor_copy(
                    out=xT[:, g4 * 4:(g4 + 1) * 4, tc * P:(tc + 1) * P],
                    in_=bank(bk)[:, :].rearrange("p (a c) -> p a c", a=4)), "dv2")
            xre_free[0] = b.opi("tensor", lambda e: e.nop(), "pe2")
        xT_dv = b.cnt["dv2"]

        wo_state = {"conv": {0: 0, 1: 0}, "peuse": {0: 0, 1: 0}}
        res_free = {6: 0, 7: 0}
        b.wait("tensor", "dv2", xT_dv)
        for m in range(8):
            buf = m % 2
            b.wait("sync", "dv", wo_state["conv"][buf])
            src = AP(tensor=d["wo"].tensor, offset=256 * m,
                     ap=[[D, 128], [P * D, NF], [1, 256]])
            b.dma("sync", wostg[:, buf, :, :], src, "wl")
            b.wait("vector", "wl")
            b.wait("vector", "pe", wo_state["peuse"][buf])
            cv = b.opi("vector", lambda e, buf=buf: e.tensor_copy(
                out=wobf[:, buf, :, :].rearrange("p a c -> p (a c)"),
                in_=wostg[:, buf, :, :].rearrange("p a c -> p (a c)")), "dv")
            wo_state["conv"][buf] = cv
            b.wait("tensor", "dv", cv)
            for par in range(2):
                fc = 2 * m + par
                for th in range(2):
                    bk = 6 + th
                    b.wait("tensor", "dv", res_free[bk])
                    for ec in range(NF):
                        b.opi("tensor", lambda e, buf=buf, ec=ec, par=par, th=th, bk=bk: e.matmul(
                            bank(bk)[:, :], wobf[:, buf, ec, par:256:2],
                            o_nrm[:, ec, th * 512:(th + 1) * 512],
                            start=(ec == 0), stop=(ec == NF - 1)), "pe")
                    b.wait("vector", "pe")
                    res_free[bk] = b.opi("vector", lambda e, fc=fc, th=th, bk=bk: e.tensor_tensor(
                        out=xT[:, fc, th * 512:(th + 1) * 512],
                        in0=xT[:, fc, th * 512:(th + 1) * 512], in1=bank(bk)[:, :], op=ALU.add), "dv")
            wo_state["peuse"][buf] = b.cnt["pe"]
        b.wait("tensor", "dv", max(res_free.values()))

        dbg_dump("xa", xT[:, 0:8, :])
        dbg_dump("xa2", xT[:, 8:16, :])

        lnmm = 0
        for fc in range(NF):
            b.wait("vector", "pe", lnmm)
            b.op("vector", lambda e, fc=fc: e.tensor_copy(out=xbs[:], in_=xT[:, fc, :]))
            b.opi("vector", lambda e: e.tensor_tensor(out=sqs[:], in0=xbs[:], in1=xbs[:],
                                                      op=ALU.mult), "dv")
            b.wait("tensor", "dv")
            for th in range(2):
                b.opi("tensor", lambda e, fc=fc, th=th: e.matmul(
                    PS[0:1, th, :], c_1ln[:], xbs[:, th * 512:(th + 1) * 512],
                    start=(fc == 0), stop=(fc == NF - 1)), "pe")
                b.opi("tensor", lambda e, fc=fc, th=th: e.matmul(
                    PS[0:1, 2 + th, :], c_1ln[:], sqs[:, th * 512:(th + 1) * 512],
                    start=(fc == 0), stop=(fc == NF - 1)), "pe")
            lnmm = b.cnt["pe"]
        b.wait("vector", "pe", lnmm)
        b.op("vector", lambda e: e.tensor_copy(out=lnsc[0:1, 0:512], in_=PS[0:1, 0, :]))
        b.opi("vector", lambda e: e.tensor_copy(out=lnsc[0:1, 512:1024], in_=PS[0:1, 1, :]), "st")
        b.wait("vector", "st")
        b.op("vector", lambda e: e.tensor_tensor(out=rms2[0:1, 0:512], in0=lnsc[0:1, 0:512],
                                                 in1=lnsc[0:1, 0:512], op=ALU.mult))
        b.opi("vector", lambda e: e.tensor_tensor(out=rms2[0:1, 512:1024], in0=lnsc[0:1, 512:1024],
                                                  in1=lnsc[0:1, 512:1024], op=ALU.mult), "st")
        b.wait("vector", "st")
        b.op("vector", lambda e: e.tensor_tensor(out=rms2[0:1, 0:512], in0=PS[0:1, 2, :],
                                                 in1=rms2[0:1, 0:512], op=ALU.subtract))
        b.opi("vector", lambda e: e.tensor_tensor(out=rms2[0:1, 512:1024], in0=PS[0:1, 3, :],
                                                  in1=rms2[0:1, 512:1024], op=ALU.subtract), "dv")
        b.wait("vector", "dv")
        b.opi("vector", lambda e: e.engine_nop(), "st")
        b.wait("scalar", "st")
        b.opi("scalar", lambda e: e.activation(out=rms2[0:1, 0:1024], in_=rms2[0:1, 0:1024],
                                               func=AF.Ln, bias=c_actk[0:1, 0:1]), "st")
        b.wait("scalar", "st")
        b.opi("scalar", lambda e: e.activation(out=rms2[0:1, 0:1024], in_=rms2[0:1, 0:1024],
                                               func=AF.Exp, scale=c_actk[0:1, 2:3]), "ac")
        b.wait("scalar", "ac")
        b.opi("scalar", lambda e: e.nop(), "st")
        b.wait("vector", "st")
        b.opi("vector", lambda e: e.scalar_tensor_tensor(
            out=lnsc[0:1, 0:1024], in0=lnsc[0:1, 0:1024], scalar=-1.0, in1=rms2[0:1, 0:1024],
            op0=ALU.mult, op1=ALU.mult), "dv")
        b.wait("vector", "dv")
        b.opi("vector", lambda e: e.engine_nop(), "st")
        b.wait("tensor", "st")
        b.wait("tensor", "dv")
        for th in range(2):
            b.opi("tensor", lambda e, th=th: e.matmul(
                bank(4 + th)[:, :], c_1row[0:1, :], rms2[0:1, th * 512:(th + 1) * 512],
                start=True, stop=True), "pe")
            b.opi("tensor", lambda e, th=th: e.matmul(
                bank(6 + th)[:, :], c_1row[0:1, :], lnsc[0:1, th * 512:(th + 1) * 512],
                start=True, stop=True), "pe")
        b.wait("vector", "pe")
        for fc in range(NF):
            for th in range(2):
                b.opi("vector", lambda e, fc=fc, th=th: e.tensor_tensor(
                    out=rms2[:, th * 512:(th + 1) * 512],
                    in0=xT[:, fc, th * 512:(th + 1) * 512], in1=bank(4 + th)[:, :], op=ALU.mult), "st")
                b.wait("vector", "st")
                b.opi("vector", lambda e, fc=fc, th=th: e.tensor_tensor(
                    out=xln2[:, fc, th * 512:(th + 1) * 512],
                    in0=rms2[:, th * 512:(th + 1) * 512], in1=bank(6 + th)[:, :], op=ALU.add), "dv")
        xln2_dv = b.cnt["dv"]
        dbg_dump("xln2", xln2)
        b.barrier()

        scb_war = {"gu": 0, "dn": 0}
        scb_cv = {"gu": 0, "dn": 0}

        def dec_batch(v16, shlo, dst_lo, dst_hi, scf, scb, nj, key, sem=None):
            b.wait("scalar", "dv", scb_war[key])
            cv = b.opi("scalar", lambda e, scf=scf, scb=scb: e.activation(
                out=scb, in_=scf, func=AF.Copy, scale=c_actk[:, 4:5]), "ac")
            scb_cv[key] = cv
            last = None
            for k, (sh, dst) in enumerate(((shlo, dst_lo), (shlo + 4, dst_hi))):
                t = scrC[:, 0:nj, :]
                if sh == 0:
                    b.op("vector", lambda e, v16=v16, t=t: e.tensor_scalar(
                        out=t, in0=v16, scalar1=0x000F, scalar2=None, op0=ALU.bitwise_and))
                else:
                    b.op("vector", lambda e, v16=v16, t=t, sh=sh: e.tensor_scalar(
                        out=t, in0=v16, scalar1=sh, scalar2=0x000F,
                        op0=ALU.logical_shift_right, op1=ALU.bitwise_and))
                b.op("vector", lambda e, t=t: e.tensor_scalar(
                    out=t, in0=t, scalar1=0x1040, scalar2=None, op0=ALU.mult))
                b.op("vector", lambda e, t=t: e.tensor_scalar(
                    out=t, in0=t, scalar1=0x81C0, scalar2=None, op0=ALU.bitwise_and))
                if k == 0:
                    b.wait("vector", "ac", cv)
                fn = lambda e, t=t, dst=dst, scb=scb: e.tensor_tensor(
                    out=dst, in0=t.bitcast(BF16), in1=scb, op=ALU.mult)
                if k == 1 and sem is not None:
                    last = b.opi("vector", fn, sem)
                else:
                    b.op("vector", fn)
            scb_war[key] = last if last is not None else b.cnt["dv"]
            return last

        pk_done = {}

        def pk_dma(mat, w):
            arr = gpk if mat == "g" else upk
            src = AP(tensor=d[mat + "_pk"].tensor, offset=512 * w,
                     ap=[[MLP, 128], [128 * MLP, 8], [1, 512]])
            pk_done[(mat, w)] = b.dma("sync", arr[:, w % 2, :, :], src, "wl")

        dpk_done = {}

        def dpk_dma(s):
            src = AP(tensor=d["d_pk"].tensor, offset=(4 * s) * 128 * D,
                     ap=[[D, 128], [128 * D, 4], [1, D]])
            dpk_done[s] = b.dma("sync", dpk[:], src, "wl")

        b.wait("sync", "dv", max(res_free.values()))
        pk_dma("g", 0)
        pk_dma("u", 0)
        decuse = {"g": 0, "u": 0}
        pkuse = {("g", 0): 0, ("g", 1): 0, ("u", 0): 0, ("u", 1): 0}
        gu_bank_free = {0: 0, 1: 0, 2: 0, 3: 0}
        dn_bank_free = {4: 0, 5: 0}
        wd_use = {0: 0, 1: 0}
        inter_tt = [0]
        dpk_decoded = xln2_dv
        dscuse = [0]

        for s_i in range(NS):
            b.wait("sync", "dv", dpk_decoded)
            dpk_dma(s_i)
            for ww in range(2):
                w = 2 * s_i + ww
                wbuf = w % 2
                if w + 1 < NWIN:
                    b.wait("sync", "dv", pkuse[("g", (w + 1) % 2)])
                    pk_dma("g", w + 1)
                    b.wait("sync", "dv", pkuse[("u", (w + 1) % 2)])
                    pk_dma("u", w + 1)
                for npar in range(2):
                    for mat in ("g", "u"):
                        dec = gdec if mat == "g" else udec
                        arr = gpk if mat == "g" else upk
                        nm = f"{mat}_sc_{'e' if npar == 0 else 'o'}"
                        b.wait("sync", "ac", scb_cv["gu"])
                        scd = b.dma("sync", gsc[:],
                                    AP(tensor=d[nm].tensor, offset=256 * w,
                                       ap=[[MLP // 2, 128], [128 * (MLP // 2), 8], [1, 256]]),
                                    "wl")
                        b.wait("scalar", "wl", scd)
                        b.wait("vector", "wl", pk_done[(mat, w)])
                        b.wait("vector", "pe", decuse[mat])
                        v16 = arr[:, wbuf, :, :].bitcast(U16)
                        dv_dec = dec_batch(v16, 8 * npar, dec[:, 0, :, :], dec[:, 1, :, :],
                                           gsc[:, :, :], gscb[:, :, :], 8, "gu", sem="dv")
                        pkuse[(mat, wbuf)] = dv_dec
                        b.wait("tensor", "dv", dv_dec)
                        if s_i == 0 and ww == 0 and mat == "g" and npar == 0:
                            b.wait("tensor", "dv", xln2_dv)
                        if mat == "g":
                            b.wait("scalar", "dv", inter_tt[0])
                        for ci in range(2):
                            islot = 4 * npar + 2 * ww + ci
                            for th in range(2):
                                bk = (ci * 2 + th) % 4
                                b.wait("tensor", "ac", gu_bank_free[bk])
                                for kc in range(NF):
                                    b.opi("tensor", lambda e, kc=kc, ci=ci, th=th, bk=bk, dec=dec: e.matmul(
                                        bank(bk)[:, :], dec[:, kc % 2, kc // 2, ci * 128:(ci + 1) * 128],
                                        xln2[:, kc, th * 512:(th + 1) * 512],
                                        start=(kc == 0), stop=(kc == NF - 1)), "pe")
                                mme = b.cnt["pe"]
                                sl = ci * 2 + th
                                b.wait("scalar", "pe", mme)
                                if mat == "g":
                                    gu_bank_free[bk] = b.opi("scalar", lambda e, bk=bk, sl=sl: e.activation(
                                        out=gl[:, sl * 512:(sl + 1) * 512], in_=bank(bk)[:, :],
                                        func=AF.Gelu_apprx_tanh), "ac")
                                else:
                                    gu_bank_free[bk] = b.opi("scalar", lambda e, bk=bk, sl=sl: e.activation(
                                        out=up_sb[:, sl * 512:(sl + 1) * 512], in_=bank(bk)[:, :],
                                        func=AF.Copy), "ac")
                        decuse[mat] = b.cnt["pe"]
                        if mat == "u":
                            base = 4 * npar + 2 * ww
                            b.wait("vector", "ac")
                            inter_tt[0] = b.opi("vector", lambda e, base=base: e.tensor_tensor(
                                out=inter[:, base:base + 2, :],
                                in0=gl[:].rearrange("p (a c) -> p a c", a=2),
                                in1=up_sb[:].rearrange("p (a c) -> p a c", a=2),
                                op=ALU.mult), "dv")
            inter_done = b.cnt["dv"]
            if s_i == 0:
                dbg_dump("inter0", inter)
            b.wait("vector", "wl", dpk_done[s_i])
            for q in range(4):
                for hf in range(2):
                    blk = q * 2 + hf
                    wdb = wddec if blk % 2 == 0 else wddec2
                    nm = "d_sc_a" if hf == 0 else "d_sc_b"
                    src = AP(tensor=d[nm].tensor, offset=(512 * s_i) * 1024 + 256 * q,
                             ap=[[1024, 128], [128 * 1024, 4], [1, 256]])
                    b.wait("sync", "ac", scb_cv["dn"])
                    dsd = b.dma("sync", dsc[:], src, "wl")
                    b.wait("scalar", "wl", dsd)
                    b.wait("vector", "pe", wd_use[blk % 2])
                    v16 = dpk[:, :, 512 * q:512 * q + 512].bitcast(U16)
                    wd_dv = dec_batch(v16, 8 * hf, wdb[:, 0:4, :], wdb[:, 4:8, :],
                                      dsc[:, :, :], dscb[:, :, :], 4, "dn", sem="dv")
                    b.wait("tensor", "dv", wd_dv)
                    if q == 0 and hf == 0:
                        b.wait("tensor", "dv", inter_done)
                    for hci in range(2):
                        hc = hf * 8 + 2 * q + hci
                        for th in range(2):
                            bk = 4 + ((hci * 2 + th) % 2)
                            b.wait("tensor", "dv", dn_bank_free[bk])
                            for sl in range(8):
                                jc, nib = sl % 4, sl // 4
                                islot = 4 * nib + jc
                                b.opi("tensor", lambda e, sl=sl, hci=hci, th=th, bk=bk, islot=islot, wdb=wdb: e.matmul(
                                    bank(bk)[:, :], wdb[:, sl, hci * 128:(hci + 1) * 128],
                                    inter[:, islot, th * 512:(th + 1) * 512],
                                    start=(sl == 0), stop=(sl == 7)), "pe")
                            b.wait("vector", "pe")
                            dn_bank_free[bk] = b.opi("vector", lambda e, hc=hc, th=th, bk=bk: e.tensor_tensor(
                                out=xT[:, hc, th * 512:(th + 1) * 512],
                                in0=xT[:, hc, th * 512:(th + 1) * 512], in1=bank(bk)[:, :],
                                op=ALU.add), "dv")
                    wd_use[blk % 2] = b.cnt["pe"]
            dpk_decoded = b.cnt["dv"]

        b.wait("tensor", "dv", max(dn_bank_free.values()))
        b.barrier()

        st_done = {0: 0, 1: 0}
        tr_free = {0: 0, 1: 0}
        for tc in range(NT):
            buf = tc % 2
            b.wait("vector", "sl", st_done[buf])
            evacs = []
            for g4 in range(4):
                bk = g4 % 2
                b.wait("tensor", "dv2", tr_free[bk])
                for k4 in range(4):
                    fc = g4 * 4 + k4
                    b.opi("tensor", lambda e, bk=bk, k4=k4, fc=fc, tc=tc: e.transpose(
                        bank(bk)[:, k4 * P:(k4 + 1) * P], xT[:, fc, tc * P:(tc + 1) * P],
                        c_idf[:]), "pe")
                b.wait("vector", "pe")
                last = 0
                for k4 in range(4):
                    fc = g4 * 4 + k4
                    m, par = fc // 2, fc % 2
                    dstv = ostg[:, buf, 256 * m + par: 256 * (m + 1): 2]
                    last = b.opi("vector", lambda e, bk=bk, k4=k4, dstv=dstv: e.tensor_copy(
                        out=dstv, in_=bank(bk)[:, k4 * P:(k4 + 1) * P]), "dv2")
                tr_free[bk] = last
            b.wait("scalar", "dv2")
            st_done[buf] = b.dma("scalar", d_out[tc * P:(tc + 1) * P, :], ostg[:, buf, :], "sl")
        b.wait("scalar", "sl")

        import os
        if os.environ.get("K_PRINT_CNT"):
            print("SEM COUNTS:", {k: v for k, v in b.cnt.items()})
            print("Q LENGTHS:", {k: len(v) for k, v in b.q.items()})
        @block.sync
        def _(eng):
            for f in b.q["sync"]:
                f(eng)

        @block.vector
        def _(eng):
            for f in b.q["vector"]:
                f(eng)

        @block.scalar
        def _(eng):
            for f in b.q["scalar"]:
                f(eng)

        @block.tensor
        def _(eng):
            for f in b.q["tensor"]:
                f(eng)

        @block.gpsimd
        def _(eng):
            for f in b.q["gpsimd"]:
                f(eng)

    return nc


_NC_CACHE = [None]


def kernel(**inputs):
    in_maps = _host_prep(inputs)
    if _NC_CACHE[0] is None:
        _NC_CACHE[0] = build_nc()
    nc = _NC_CACHE[0]
    res = run_bass_kernel_spmd(nc, in_maps, core_ids=list(range(8)))
    out = np.empty((B, S, D), np.float32)
    for core in range(8):
        bb, h = core // 2, core % 2
        out[bb, h * T:(h + 1) * T] = res.results[core]["out"]
    return out


# version 08
# speedup vs baseline: 1.0780x; 1.0780x over previous
import numpy as np
import ml_dtypes
import concourse.bass as bass
import concourse.mybir as mybir
from concourse.bass import AP
from concourse.bass_utils import run_bass_kernel_spmd
from concourse.alu_op_type import AluOpType as ALU

AF = mybir.ActivationFunctionType
F32 = mybir.dt.float32
BF16 = mybir.dt.bfloat16
U8 = mybir.dt.uint8
U16 = mybir.dt.uint16
BF = ml_dtypes.bfloat16

P = 128
B, S, D = 4, 2048, 2048
HEADS, HD, CH = 16, 128, 128
MLP = 16384
T = 1024
NT = T // P
NF = D // P
HG = 2
NHG = HEADS // HG
EPS_LN = 1e-5
EPS_RMS = 1e-6
NWIN = 32
NS = 16
DBG = []


def _fperm():
    fp = np.empty(D, np.int64)
    for m in range(8):
        fp[256 * m:256 * m + 128] = 256 * m + 2 * np.arange(128)
        fp[256 * m + 128:256 * m + 256] = 256 * m + 2 * np.arange(128) + 1
    return fp


def _host_prep(inputs):
    x = np.asarray(inputs["L0_attn_x"])
    nw = np.asarray(inputs["L0_attn_norm_weight"])

    def pk_bytes(a):
        a = np.asarray(a)
        return np.ascontiguousarray(a.view(np.uint8).reshape(a.shape[0], a.shape[1], 4)[:, :, 0])

    g_pk = np.ascontiguousarray(pk_bytes(inputs["L0_ffn_gate_W_packed"]).T)
    u_pk = np.ascontiguousarray(pk_bytes(inputs["L0_ffn_up_W_packed"]).T)
    d_pk_t = pk_bytes(inputs["L0_ffn_down_W_packed"]).T

    fp = _fperm()
    col_order = np.empty(D, np.int64)
    col_order[0::2] = fp[:1024]
    col_order[1::2] = fp[1024:]
    d_pk = np.ascontiguousarray(d_pk_t[:, col_order])

    g_sc_t = np.asarray(inputs["L0_ffn_gate_scales"]).T
    u_sc_t = np.asarray(inputs["L0_ffn_up_scales"]).T
    d_sc_t = np.asarray(inputs["L0_ffn_down_scales"]).T
    rep16 = lambda a: np.ascontiguousarray(np.repeat(a, 16, axis=0))
    g_sc_e = rep16(g_sc_t[:, 0::2]); g_sc_o = rep16(g_sc_t[:, 1::2])
    u_sc_e = rep16(u_sc_t[:, 0::2]); u_sc_o = rep16(u_sc_t[:, 1::2])
    d_sc_a = rep16(d_sc_t[:, fp[:1024]])
    d_sc_b = rep16(d_sc_t[:, fp[1024:]])

    hh = np.arange(HEADS, dtype=np.float64)
    log_lam = -(2.0 ** (-8.0 * (hh + 1.0) / HEADS))
    i = np.arange(CH, dtype=np.float64)
    diff = i[:, None] - i[None, :]
    dT = np.zeros((P, HEADS * CH), np.float64)
    decq = np.zeros((P, HEADS * CH), np.float64)
    deck = np.zeros((P, HEADS * CH), np.float64)
    lamc = np.zeros((P, HEADS * CH), np.float64)
    for h in range(HEADS):
        Dh = np.where(diff >= 0, np.exp(log_lam[h] * diff), 0.0)
        dT[:, h * CH:(h + 1) * CH] = Dh.T
        decq[:, h * CH:(h + 1) * CH] = np.exp(log_lam[h] * (i + 1.0))[None, :]
        deck[:, h * CH:(h + 1) * CH] = np.exp(log_lam[h] * (CH - 1.0 - i))[:, None]
        lamc[:, h * CH:(h + 1) * CH] = np.exp(log_lam[h] * CH)

    half = HD // 2
    theta = 10000.0 ** (-2.0 * np.arange(half, dtype=np.float64) / HD)

    def ccss(pos0):
        ang = np.arange(pos0, pos0 + T, dtype=np.float64)[None, :] * theta[:, None]
        c, s = np.cos(ang), np.sin(ang)
        return (np.concatenate([c, c], 0).astype(BF),
                np.concatenate([-s, s], 0).astype(BF))

    cc0, ss0 = ccss(0)
    cc1, ss1 = ccss(T)

    swap = np.zeros((P, P), BF)
    for j in range(64):
        swap[j + 64, j] = 1
        swap[j, j + 64] = 1

    shared = dict(
        wq=np.asarray(inputs["L0_attn_Wq"]), wk=np.asarray(inputs["L0_attn_Wk"]),
        wv=np.asarray(inputs["L0_attn_Wv"]), wo=np.asarray(inputs["L0_attn_Wo"]),
        wg1=np.asarray(inputs["L0_attn_Wg1"]), wg2=np.asarray(inputs["L0_attn_Wg2"]),
        nw=np.ascontiguousarray(nw.reshape(HEADS, HD).T).astype(np.float32),
        g_pk=g_pk, u_pk=u_pk, d_pk=d_pk,
        g_sc_e=g_sc_e, g_sc_o=g_sc_o, u_sc_e=u_sc_e, u_sc_o=u_sc_o,
        d_sc_a=d_sc_a, d_sc_b=d_sc_b,
        dT=dT.astype(BF), decq=decq.astype(BF), deck=deck.astype(BF),
        lamc=lamc.astype(BF), swap=swap,
        ident_bf=np.eye(P, dtype=BF), ident_f32=np.eye(P, dtype=np.float32),
        ones_ln=np.full((P, 1), 1.0 / D, BF),
        ones_rms=np.full((P, 1), 1.0 / HD, BF),
        ones_row=np.ones((1, P), np.float32),
        actk=np.tile(np.array([[EPS_LN, EPS_RMS, -0.5, 0.0, float(2.0 ** 126)]], np.float32), (P, 1)),
    )
    in_maps = []
    for core in range(8):
        bb, h = core // 2, core % 2
        own = np.ascontiguousarray(x[bb, h * T:(h + 1) * T])
        ctx = np.ascontiguousarray(x[bb, 0:T]) if h == 1 else own
        m = dict(shared)
        m["xin"] = np.ascontiguousarray(np.concatenate([ctx, own], 0))
        m["smask"] = np.full((P, 1), float(h), np.float32)
        m["cc_c"], m["ss_c"] = cc0, ss0
        m["cc_o"], m["ss_o"] = (cc1, ss1) if h == 1 else (cc0, ss0)
        in_maps.append(m)
    return in_maps


class Bld:
    ENGS = ("sync", "vector", "scalar", "tensor", "gpsimd")

    def __init__(self, nc):
        self.nc = nc
        self.q = {e: [] for e in self.ENGS}
        self.sem = {}
        self.cnt = {}

    def new_sem(self, name, handle):
        self.sem[name] = handle
        self.cnt[name] = 0

    def op(self, eng, fn):
        self.q[eng].append(fn)

    def opi(self, eng, fn, sem, k=1):
        h = self.sem[sem]
        self.q[eng].append(lambda e, fn=fn, h=h, k=k: fn(e).then_inc(h, k))
        self.cnt[sem] += k
        return self.cnt[sem]

    def dma(self, eng, out, in_, sem):
        self.wait(eng, sem)
        return self.opi(eng, lambda e, o=out, i=in_: e.dma_start(out=o, in_=i), sem, 16)

    def wait(self, eng, sem, v=None):
        hv = self.cnt[sem] if v is None else v
        if hv > 0:
            h = self.sem[sem]
            self.q[eng].append(lambda e, h=h, hv=hv: e.wait_ge(h, hv))

    def barrier(self):
        h = self.sem["bar"]
        for e in self.ENGS:
            if e == "tensor":
                self.q[e].append(lambda en, h=h: en.nop().then_inc(h, 1))
            elif e in ("vector", "gpsimd"):
                self.q[e].append(lambda en, h=h: en.engine_nop().then_inc(h, 1))
            else:
                self.q[e].append(lambda en, h=h: en.nop().then_inc(h, 1))
        self.cnt["bar"] += 5
        tgt = self.cnt["bar"]
        for e in self.ENGS:
            self.q[e].append(lambda en, h=h, v=tgt: en.wait_ge(h, v))


def build_nc():
    nc = bass.Bass("TRN2", target_bir_lowering=False, debug=False,
                   detect_race_conditions=False)
    d = {}

    def din(name, shape, dt):
        d[name] = nc.dram_tensor(name, shape, dt, kind="ExternalInput").ap()

    din("xin", [2 * T, D], F32)
    din("smask", [P, 1], F32)
    for n in ("wq", "wk", "wv", "wo"):
        din(n, [D, D], F32)
    din("wg1", [D, HD], F32)
    din("wg2", [HD, D], F32)
    din("nw", [P, HEADS], F32)
    din("g_pk", [1024, MLP], U8)
    din("u_pk", [1024, MLP], U8)
    din("d_pk", [MLP // 2, D], U8)
    din("g_sc_e", [1024, MLP // 2], F32); din("g_sc_o", [1024, MLP // 2], F32)
    din("u_sc_e", [1024, MLP // 2], F32); din("u_sc_o", [1024, MLP // 2], F32)
    din("d_sc_a", [MLP // 2, 1024], F32); din("d_sc_b", [MLP // 2, 1024], F32)
    din("dT", [P, HEADS * CH], BF16); din("decq", [P, HEADS * CH], BF16)
    din("deck", [P, HEADS * CH], BF16); din("lamc", [P, HEADS * CH], BF16)
    din("swap", [P, P], BF16); din("ident_bf", [P, P], BF16); din("ident_f32", [P, P], F32)
    din("ones_ln", [P, 1], BF16); din("ones_rms", [P, 1], BF16); din("ones_row", [1, P], F32)
    din("actk", [P, 5], F32)
    for n in ("cc_c", "ss_c", "cc_o", "ss_o"):
        din(n, [P, T], BF16)
    d_out = nc.dram_tensor("out", [T, D], F32, kind="ExternalOutput").ap()
    d_dbg = {}
    for nm in DBG:
        d_dbg[nm] = nc.dram_tensor("dbg_" + nm, [P, 32 * 1024], F32, kind="ExternalOutput").ap()

    KB = 1024
    AB = 207 * KB

    with (
        nc.sbuf_tensor("arena", [P, AB], U8) as arena,
        nc.psum_tensor("PS", [P, 8, 512], F32) as PS,
        nc.semaphore("s_wl") as s_wl,
        nc.semaphore("s_gl") as s_gl,
        nc.semaphore("s_sl") as s_sl,
        nc.semaphore("s_pe") as s_pe,
        nc.semaphore("s_dv") as s_dv,
        nc.semaphore("s_ac") as s_ac,
        nc.semaphore("s_dv2") as s_dv2,
        nc.semaphore("s_pe2") as s_pe2,
        nc.semaphore("s_bar") as s_bar,
        nc.semaphore("s_st") as s_st,
        nc.Block() as block,
    ):
        b = Bld(nc)
        for nm, h in (("wl", s_wl), ("gl", s_gl), ("sl", s_sl), ("pe", s_pe), ("dv", s_dv),
                      ("ac", s_ac), ("dv2", s_dv2), ("pe2", s_pe2), ("bar", s_bar),
                      ("st", s_st)):
            b.new_sem(nm, h)

        def bank(i, dt=F32, n=None):
            v = PS[:, i, :]
            if dt == BF16:
                v = v.bitcast(BF16)
            if n is not None:
                v = v[:, 0:n]
            return v

        off = [0]

        def carve(nb, name=""):
            o = off[0]
            assert o + nb <= AB, f"SBUF overflow {name}: {o}+{nb}"
            off[0] += (nb + 63) & ~63
            return o

        def view(o, shape, dt):
            esz = {F32: 4, BF16: 2, U16: 2, U8: 1}[dt]
            nb = int(np.prod(shape[1:])) * esz
            a = arena[:, o:o + nb].bitcast(dt)
            if len(shape) == 3:
                a = a.rearrange("p (a c) -> p a c", a=shape[1])
            elif len(shape) == 4:
                a = a.rearrange("p (a b c) -> p a b c", a=shape[1], b=shape[2])
            return a

        c_dT = view(carve(4 * KB, "dT"), [P, HEADS, CH], BF16)
        c_decq = view(carve(4 * KB), [P, HEADS, CH], BF16)
        c_deck = view(carve(4 * KB), [P, HEADS, CH], BF16)
        c_lamc = view(carve(4 * KB), [P, HEADS, CH], BF16)
        c_cc = {"c": view(carve(2 * KB), [P, T], BF16)}
        c_ss = {"c": view(carve(2 * KB), [P, T], BF16)}
        c_cc["o"] = view(carve(2 * KB), [P, T], BF16)
        c_ss["o"] = view(carve(2 * KB), [P, T], BF16)
        c_swap = view(carve(256), [P, P], BF16)
        c_idb = view(carve(256), [P, P], BF16)
        c_idf = view(carve(512), [P, P], F32)
        c_1ln = view(carve(64), [P, 1], BF16)
        c_1rms = view(carve(64), [P, 1], BF16)
        c_1row = view(carve(512), [P, P], F32)
        c_nw = view(carve(64), [P, HEADS], F32)
        c_actk = view(carve(64), [P, 5], F32)
        c_mask = view(carve(64), [P, 1], F32)
        c_wg2 = view(carve(4 * KB), [P, D], BF16)
        c_wg1 = view(carve(4 * KB), [P, NF, HD], BF16)
        oonrm = carve(32 * KB, "o_nrm")
        o_nrm = view(oonrm, [P, HEADS, T], BF16)
        g1T = view(carve(2 * KB), [P, T], BF16)
        base0 = off[0]

        offA = [base0]

        def carveA(nb, name=""):
            o = offA[0]
            assert o + nb <= AB, f"A overflow {name}: {o}+{nb}"
            offA[0] += (nb + 63) & ~63
            return o

        xln = view(carveA(32 * KB, "xln"), [P, NF, T], BF16)
        oxcln = carveA(32 * KB, "xcln")
        xcln = view(oxcln, [P, NF, T], BF16)
        wstage = view(carveA(16 * KB, "wstage"), [P, 2, 2048], F32)
        wbf = view(carveA(16 * KB, "wbf"), [P, 2, NF, HG * HD], BF16)
        qsb = view(carveA(4 * KB, "q"), [P, HG, T], BF16)
        ksb = view(carveA(4 * KB, "k"), [P, HG, T], BF16)
        vsb = view(carveA(4 * KB, "v"), [P, NT, HG * HD], BF16)
        S32 = view(carveA(8 * KB, "S"), [P, HEADS, HD], F32)
        Sbf = view(carveA(4 * KB, "Sbf"), [P, HEADS, HD], BF16)
        xstg = view(carveA(8 * KB, "xstg"), [P, 1, D], F32)
        scr2 = view(carveA(4 * KB, "scr2"), [P, D], BF16)
        scrq = view(carveA(1 * KB, "scrq"), [P, HG * CH], BF16)
        scrk = view(carveA(1 * KB, "scrk"), [P, HG * CH], BF16)
        scrs = view(carveA(1 * KB, "scrs"), [P, HG * CH], BF16)
        stat = view(carveA(256, "stat"), [P, 64], F32)
        sgt = view(oxcln, [P, 2, T], BF16)
        rms1 = view(oxcln + 4 * KB, [P, T], F32)
        assert offA[0] <= AB, "A total %d" % offA[0]

        offB = [base0]

        def carveB(nb, name=""):
            o = offB[0]
            assert o + nb <= AB, f"B overflow {name}"
            offB[0] += (nb + 63) & ~63
            return o

        xT = view(carveB(64 * KB, "xT"), [P, NF, T], F32)
        wostg = view(carveB(32 * KB, "wostg"), [P, 2, NF, 256], F32)
        wobf = view(carveB(16 * KB, "wobf"), [P, 2, NF, 256], BF16)
        xstg2 = view(carveB(8 * KB, "xstg2"), [P, 1, D], F32)
        sqs = view(carveB(2 * KB, "sqs"), [P, T], BF16)
        xbs = view(carveB(2 * KB, "xbs"), [P, T], BF16)
        lnsc = view(carveB(4 * KB, "lnsc"), [P, T], F32)
        rms2 = view(carveB(4 * KB, "rms2"), [P, T], F32)
        assert offB[0] <= AB

        offC = [base0 + 64 * KB]

        def carveC(nb, name=""):
            o = offC[0]
            assert o + nb <= AB, f"C overflow {name}: {o}+{nb}"
            offC[0] += (nb + 63) & ~63
            return o

        xln2 = view(oonrm, [P, NF, T], BF16)
        inter = view(carveC(16 * KB, "inter"), [P, 8, T], BF16)
        ogdec = carveC(8 * KB, "gdec")
        gdec = view(ogdec, [P, 2, 8, 256], BF16)
        oudec = carveC(8 * KB, "udec")
        udec = view(oudec, [P, 2, 8, 256], BF16)
        wddec = view(carveC(4 * KB, "wddec"), [P, 8, 256], BF16)
        gpk = view(carveC(8 * KB, "gpk"), [P, 2, 8, 512], U8)
        upk = view(carveC(8 * KB, "upk"), [P, 2, 8, 512], U8)
        dpk = view(carveC(8 * KB, "dpk"), [P, 4, D], U8)
        gsc = view(carveC(8 * KB, "gsc"), [P, 8, 256], F32)
        gscb = view(carveC(4 * KB, "gscb"), [P, 8, 256], BF16)
        gl = view(0, [P, 2048], BF16)
        dsc = view(4 * KB, [P, 4, 256], F32)
        dscb = view(8 * KB, [P, 4, 256], BF16)
        scrC = view(10 * KB, [P, 8, 256], U16)
        up_sb = view(16 * KB, [P, 2048], BF16)
        wddec2 = view(20 * KB, [P, 8, 256], BF16)
        assert offC[0] <= AB, f"C total {offC[0]}"
        ostg = view(ogdec, [P, 2, D], F32)

        def dbg_dump(nm, ap2d):
            if nm not in d_dbg:
                return
            b.barrier()
            n = int(np.prod(ap2d.shape[1:]))
            if ap2d.dtype == BF16:
                for i in range(0, n, 2048):
                    w = min(2048, n - i)
                    b.op("vector", lambda e, i=i, w=w: e.tensor_copy(
                        out=xstg[:, 0, 0:w] if w <= 2048 else None,
                        in_=ap2d.rearrange("p a c -> p (a c)")[:, i:i + w] if len(ap2d.shape) == 3 else ap2d[:, i:i + w]))
                    b.opi("vector", lambda e: e.engine_nop(), "dv")
                    b.wait("scalar", "dv")
                    b.dma("scalar", d_dbg[nm][:, i:i + w], xstg[:, 0, 0:w], "sl")
                    b.wait("vector", "sl")
            else:
                a2 = ap2d.rearrange("p a c -> p (a c)") if len(ap2d.shape) == 3 else ap2d
                b.dma("scalar", d_dbg[nm][:, 0:n], a2, "sl")
                b.wait("scalar", "sl")
            b.barrier()

        for t_, n_ in ((c_dT.rearrange("p a c -> p (a c)"), "dT"),
                       (c_decq.rearrange("p a c -> p (a c)"), "decq"),
                       (c_deck.rearrange("p a c -> p (a c)"), "deck"),
                       (c_lamc.rearrange("p a c -> p (a c)"), "lamc"),
                       (c_cc["c"], "cc_c"), (c_ss["c"], "ss_c"),
                       (c_cc["o"], "cc_o"), (c_ss["o"], "ss_o"),
                       (c_swap, "swap"), (c_idb, "ident_bf"), (c_idf, "ident_f32"),
                       (c_1ln, "ones_ln"), (c_1rms, "ones_rms"),
                       (c_nw, "nw"), (c_mask, "smask"), (c_actk, "actk")):
            b.dma("gpsimd", t_, d[n_], "gl")
        b.dma("gpsimd", c_1row[0:1, :], d["ones_row"], "gl")
        b.dma("sync", wstage[:, 0, :], d["wg2"], "wl")
        for par in range(2):
            src = AP(tensor=d["wg1"].tensor, offset=par * HD,
                     ap=[[2 * HD, 128], [256 * HD, 8], [1, HD]])
            dst = wstage[:, 1, par * 1024:(par + 1) * 1024].rearrange("p (a c) -> p a c", a=8)
            b.dma("sync", dst, src, "wl")
        b.wait("vector", "wl")
        b.op("vector", lambda e: e.tensor_copy(out=c_wg2[:], in_=wstage[:, 0, :]))
        for par in range(2):
            for m in range(8):
                s_ = wstage[:, 1, par * 1024 + m * HD: par * 1024 + (m + 1) * HD]
                b.op("vector", lambda e, s_=s_, kc=2 * m + par: e.tensor_copy(
                    out=c_wg1[:, kc, :], in_=s_))
        b.op("vector", lambda e: e.memset(S32[:].rearrange("p a c -> p (a c)"), 0.0))
        b.opi("vector", lambda e: e.memset(Sbf[:].rearrange("p a c -> p (a c)"), 0.0), "dv")
        wgpre_dv = b.cnt["dv"]

        b.wait("vector", "gl")
        tr_free = {0: 0, 1: 0}
        ln_done = {}
        ln_tr_pe = [0]
        for ti in range(2 * NT):
            if ti >= 1:
                b.wait("gpsimd", "dv", ln_done[ti - 1])
            b.dma("gpsimd", xstg[:, 0, :], d["xin"][ti * P:(ti + 1) * P, :], "gl")
            b.wait("vector", "gl")
            xt = xstg[:, 0, :]
            b.op("vector", lambda e, xt=xt: e.reduce_sum(stat[:, 0:1], xt, axis=mybir.AxisListType.X))
            b.op("vector", lambda e, xt=xt: e.tensor_tensor(out=scr2[:], in0=xt, in1=xt, op=ALU.mult))
            b.opi("vector", lambda e: e.reduce_sum(stat[:, 1:2], scr2[:], axis=mybir.AxisListType.X), "st")
            b.wait("vector", "st")
            b.opi("vector", lambda e: e.tensor_scalar(out=stat[:, 2:3], in0=stat[:, 0:1],
                                                      scalar1=1.0 / D, scalar2=None, op0=ALU.mult), "st")
            b.wait("vector", "st")
            b.opi("vector", lambda e: e.tensor_tensor(out=stat[:, 4:5], in0=stat[:, 2:3],
                                                      in1=stat[:, 2:3], op=ALU.mult), "st")
            b.wait("vector", "st")
            b.opi("vector", lambda e: e.scalar_tensor_tensor(
                out=stat[:, 5:6], in0=stat[:, 1:2], scalar=1.0 / D, in1=stat[:, 4:5],
                op0=ALU.mult, op1=ALU.subtract), "dv")
            b.wait("vector", "dv")
            b.opi("vector", lambda e: e.engine_nop(), "st")
            b.wait("scalar", "st")
            b.opi("scalar", lambda e: e.activation(out=stat[:, 6:7], in_=stat[:, 5:6],
                                                   func=AF.Ln, bias=c_actk[:, 0:1]), "st")
            b.wait("scalar", "st")
            b.opi("scalar", lambda e: e.activation(out=stat[:, 7:8], in_=stat[:, 6:7],
                                                   func=AF.Exp, scale=c_actk[:, 2:3]), "ac")
            b.wait("scalar", "ac")
            b.opi("scalar", lambda e: e.nop(), "st")
            b.wait("vector", "st")
            b.wait("vector", "pe", ln_tr_pe[0])
            b.opi("vector", lambda e, xt=xt: e.tensor_scalar(
                out=scr2[:], in0=xt, scalar1=stat[:, 2:3], scalar2=stat[:, 7:8],
                op0=ALU.subtract, op1=ALU.mult), "dv")
            ln_done[ti] = b.cnt["dv"]
            dst = xcln if ti < NT else xln
            tc = ti % NT
            b.wait("tensor", "dv", ln_done[ti])
            for g4 in range(4):
                bk = g4 % 2
                b.wait("tensor", "dv2", tr_free[bk])
                for k4 in range(4):
                    fc = g4 * 4 + k4
                    m, par = fc // 2, fc % 2
                    srct = scr2[:, 256 * m + par: 256 * (m + 1): 2]
                    pv = bank(bk, BF16)[:, k4 * P:(k4 + 1) * P]
                    b.opi("tensor", lambda e, pv=pv, s_=srct: e.transpose(pv, s_, c_idb[:]), "pe")
                b.wait("vector", "pe")
                pvg = bank(bk, BF16)[:, 0:512].rearrange("p (a c) -> p a c", a=4)
                dstg = dst[:, g4 * 4:(g4 + 1) * 4, tc * P:(tc + 1) * P]
                tr_free[bk] = b.opi("vector", lambda e, pvg=pvg, dstg=dstg: e.tensor_copy(
                    out=dstg, in_=pvg), "dv2")
            ln_tr_pe[0] = b.cnt["pe"]
        xln_done = b.cnt["dv2"]
        dbg_dump("xln", xln)
        dbg_dump("xcln", xcln)

        b.wait("tensor", "dv2", xln_done)
        b.wait("tensor", "dv", wgpre_dv)
        for th in range(2):
            for kc in range(NF):
                b.opi("tensor", lambda e, kc=kc, th=th: e.matmul(
                    bank(2)[:, :], c_wg1[:, kc, :], xln[:, kc, th * 512:(th + 1) * 512],
                    start=(kc == 0), stop=(kc == NF - 1)), "pe")
            b.wait("vector", "pe")
            b.opi("vector", lambda e, th=th: e.tensor_copy(
                out=g1T[:, th * 512:(th + 1) * 512], in_=bank(2)[:, :]), "dv")
            b.wait("tensor", "dv")

        wload_state = {"idx": 0, "peuse": {0: 0, 1: 0},
                       "conv": {0: wgpre_dv, 1: wgpre_dv}}

        def wload(name, hg):
            st = wload_state
            wbuf = st["idx"] % 2
            st["idx"] += 1
            b.wait("sync", "dv", st["conv"][1 - wbuf])
            for par in range(2):
                src = AP(tensor=d[name].tensor, offset=par * D + hg * HG * HD,
                         ap=[[2 * D, 128], [256 * D, 8], [1, HG * HD]])
                dst = wstage[:, par, :].rearrange("p (a c) -> p a c", a=8)
                b.dma("sync", dst, src, "wl")
            b.wait("vector", "wl")
            b.wait("vector", "pe", st["peuse"][wbuf])
            for par in range(2):
                for m in range(8):
                    s_ = wstage[:, par, m * HG * HD:(m + 1) * HG * HD]
                    b.op("vector", lambda e, s_=s_, kc=2 * m + par, wbuf=wbuf: e.tensor_copy(
                        out=wbf[:, wbuf, kc, :], in_=s_))
            cv = b.opi("vector", lambda e: e.engine_nop(), "dv")
            st["conv"][wbuf] = cv
            return wbuf, cv

        def attn_pass(is_own):
            xl = xln if is_own else xcln
            key = "o" if is_own else "c"
            st = wload_state
            for hg in range(NHG):
                silu_bank = {0: 0, 1: 0}

                def proj_headmajor(wname, dstb):
                    wbuf, cv = wload(wname, hg)
                    b.wait("tensor", "dv", cv)
                    for hh in range(HG):
                        for th in range(2):
                            bk = (hh * 2 + th) % 2
                            b.wait("tensor", "ac", silu_bank[bk])
                            for kc in range(NF):
                                b.opi("tensor", lambda e, kc=kc, hh=hh, th=th, bk=bk, wbuf=wbuf: e.matmul(
                                    bank(bk)[:, :], wbf[:, wbuf, kc, hh * HD:(hh + 1) * HD],
                                    xl[:, kc, th * 512:(th + 1) * 512],
                                    start=(kc == 0), stop=(kc == NF - 1)), "pe")
                            b.wait("scalar", "pe")
                            silu_bank[bk] = b.opi("scalar", lambda e, hh=hh, th=th, bk=bk, dstb=dstb: e.activation(
                                out=dstb[:, hh, th * 512:(th + 1) * 512], in_=bank(bk)[:, :],
                                func=AF.Silu), "ac")
                    st["peuse"][wbuf] = b.cnt["pe"]
                    return b.cnt["ac"]

                if is_own:
                    q_ac = proj_headmajor("wq", qsb)
                    if hg == 0 and "qsnap1" in d_dbg:
                        b.wait("scalar", "ac", q_ac)
                        b.dma("scalar", d_dbg["qsnap1"][:, 0:1024],
                              qsb[:].rearrange("p a c -> p (a c)").bitcast(F32), "sl")
                    if hg == 0:
                        dbg_dump("qraw", qsb)
                        dbg_dump("kraw", ksb)
                k_ac = proj_headmajor("wk", ksb)
                if is_own and hg == 0 and "qsnap2" in d_dbg:
                    b.wait("scalar", "ac", k_ac)
                    b.dma("scalar", d_dbg["qsnap2"][:, 0:1024],
                          qsb[:].rearrange("p a c -> p (a c)").bitcast(F32), "sl")
                if hg == 0 and is_own:
                    dbg_dump("qpk", qsb)
                    dbg_dump("kpk", ksb)

                wbuf, cv = wload("wv", hg)
                b.wait("tensor", "dv", cv)
                b.wait("tensor", "ac", k_ac)
                v_free = {0: 0, 1: 0}
                for tc in range(NT):
                    bk = tc % 2
                    b.wait("tensor", "dv2", v_free[bk])
                    for kc in range(NF):
                        b.opi("tensor", lambda e, kc=kc, tc=tc, bk=bk, wbuf=wbuf: e.matmul(
                            bank(bk)[:, 0:HG * HD], xl[:, kc, tc * P:(tc + 1) * P],
                            wbf[:, wbuf, kc, :], start=(kc == 0), stop=(kc == NF - 1)), "pe")
                    b.wait("vector", "pe")
                    v_free[bk] = b.opi("vector", lambda e, tc=tc, bk=bk: e.tensor_copy(
                        out=vsb[:, tc, :], in_=bank(bk)[:, 0:HG * HD]), "dv2")
                st["peuse"][wbuf] = b.cnt["pe"]
                v_dv = b.cnt["dv2"]
                if hg == 0 and is_own:
                    dbg_dump("qpv", qsb)

                rot_bank = {4: 0, 5: 0}

                def rotate(buf3, key):
                    for hh in range(HG):
                        for th in range(2):
                            bk = 4 + th
                            sl = slice(th * 512, (th + 1) * 512)
                            b.wait("tensor", "dv", rot_bank[bk])
                            b.opi("tensor", lambda e, hh=hh, sl=sl, bk=bk, buf3=buf3: e.matmul(
                                bank(bk)[:, :], c_swap[:], buf3[:, hh, sl],
                                start=True, stop=True), "pe")
                            b.wait("vector", "pe")
                            b.op("vector", lambda e, hh=hh, sl=sl, buf3=buf3, key=key: e.tensor_tensor(
                                out=scr2[:, 0:512], in0=buf3[:, hh, sl], in1=c_cc[key][:, sl],
                                op=ALU.mult))
                            b.op("vector", lambda e, sl=sl, bk=bk, key=key: e.tensor_tensor(
                                out=scr2[:, 512:1024], in0=bank(bk)[:, :], in1=c_ss[key][:, sl],
                                op=ALU.mult))
                            rot_bank[bk] = b.opi("vector", lambda e, hh=hh, sl=sl, buf3=buf3: e.tensor_tensor(
                                out=buf3[:, hh, sl], in0=scr2[:, 0:512], in1=scr2[:, 512:1024],
                                op=ALU.add), "dv")

                if is_own:
                    rotate(qsb, key)
                    if hg == 0:
                        dbg_dump("qrot", qsb)
                rotate(ksb, key)
                rot_dv = b.cnt["dv"]
                if hg == 0 and is_own:
                    dbg_dump("q0", qsb)
                    dbg_dump("k0", ksb)
                    dbg_dump("v0", vsb)

                h0 = hg * HG
                Ssl32 = S32[:, h0:h0 + HG, :].rearrange("p a c -> p (a c)")
                Sslbf = Sbf[:, h0:h0 + HG, :].rearrange("p a c -> p (a c)")
                decq_sl = c_decq[:, h0:h0 + HG, :].rearrange("p a c -> p (a c)")
                deck_sl = c_deck[:, h0:h0 + HG, :].rearrange("p a c -> p (a c)")
                lamc_sl = c_lamc[:, h0:h0 + HG, :].rearrange("p a c -> p (a c)")
                dTsl = c_dT[:, h0:h0 + HG, :].rearrange("p a c -> p (a c)")
                b.wait("tensor", "dv", rot_dv)
                b.wait("tensor", "dv2", v_dv)
                sc_free = 0
                su_done = b.cnt["dv"]
                o_free = 0
                mm1_done = 0
                for c in range(NT):
                    csl = slice(c * CH, (c + 1) * CH)
                    if is_own:
                        b.wait("vector", "pe", mm1_done)
                        b.opi("vector", lambda e, csl=csl, decq_sl=decq_sl: e.tensor_tensor(
                            out=scrq[:].rearrange("p (a c) -> p a c", a=HG),
                            in0=qsb[:, :, csl],
                            in1=decq_sl.rearrange("p (a c) -> p a c", a=HG), op=ALU.mult), "dv")
                        qd_dv = b.cnt["dv"]
                        b.wait("tensor", "dv", qd_dv)
                        b.wait("tensor", "dv2", o_free)
                        for hh in range(HG):
                            b.opi("tensor", lambda e, hh=hh, h0=h0: e.matmul(
                                bank(hh)[:, 0:CH],
                                Sbf[:, h0 + hh, :],
                                scrq[:, hh * CH:(hh + 1) * CH],
                                start=True, stop=False), "pe")
                        mm1_done = b.cnt["pe"]
                        b.wait("tensor", "dv", sc_free)
                        for hh in range(HG):
                            b.opi("tensor", lambda e, hh=hh, csl=csl: e.matmul(
                                bank(2)[:, hh * CH:(hh + 1) * CH],
                                ksb[:, hh, csl], qsb[:, hh, csl], start=True, stop=True), "pe")
                        b.wait("vector", "pe")
                        b.opi("vector", lambda e, dTsl=dTsl: e.tensor_tensor(
                            out=scrs[:], in0=bank(2)[:, 0:HG * CH], in1=dTsl, op=ALU.mult), "dv")
                        sc_free = b.cnt["dv"]
                        b.wait("tensor", "dv", sc_free)
                        for hh in range(HG):
                            b.opi("tensor", lambda e, hh=hh, c=c: e.matmul(
                                bank(hh)[:, 0:CH],
                                vsb[:, c, hh * HD:(hh + 1) * HD], scrs[:, hh * CH:(hh + 1) * CH],
                                start=False, stop=True), "pe")
                        b.wait("vector", "pe")
                        o_free = b.opi("vector", lambda e, c=c, h0=h0: e.tensor_copy(
                            out=o_nrm[:, h0, c * CH:(c + 1) * CH], in_=bank(0)[:, 0:CH]), "dv2")
                        o_free = b.opi("vector", lambda e, c=c, h0=h0: e.tensor_copy(
                            out=o_nrm[:, h0 + 1, c * CH:(c + 1) * CH], in_=bank(1)[:, 0:CH]), "dv2")
                    b.wait("tensor", "dv", su_done)
                    for hh in range(HG):
                        b.opi("tensor", lambda e, hh=hh, csl=csl: e.transpose(
                            bank(3, BF16)[:, hh * CH:(hh + 1) * CH],
                            ksb[:, hh, csl], c_idb[:]), "pe")
                    b.wait("vector", "pe")
                    b.opi("vector", lambda e, deck_sl=deck_sl: e.tensor_tensor(
                        out=scrk[:], in0=bank(3, BF16)[:, 0:HG * CH], in1=deck_sl, op=ALU.mult), "dv")
                    kd_dv = b.cnt["dv"]
                    b.wait("tensor", "dv", kd_dv)
                    for hh in range(HG):
                        b.opi("tensor", lambda e, hh=hh, c=c: e.matmul(
                            bank(4)[:, hh * CH:(hh + 1) * CH],
                            scrk[:, hh * CH:(hh + 1) * CH], vsb[:, c, hh * HD:(hh + 1) * HD],
                            start=True, stop=True), "pe")
                    b.wait("vector", "pe")
                    b.op("vector", lambda e, Ssl32=Ssl32, lamc_sl=lamc_sl: e.tensor_tensor(
                        out=Ssl32, in0=Ssl32, in1=lamc_sl, op=ALU.mult))
                    b.op("vector", lambda e, Ssl32=Ssl32: e.tensor_tensor(
                        out=Ssl32, in0=Ssl32, in1=bank(4)[:, 0:HG * CH], op=ALU.add))
                    b.opi("vector", lambda e, Ssl32=Ssl32, Sslbf=Sslbf: e.tensor_copy(
                        out=Sslbf, in_=Ssl32), "dv")
                    su_done = b.cnt["dv"]
                b.wait("tensor", "dv", su_done)

        attn_pass(False)
        b.op("vector", lambda e: e.tensor_scalar(
            out=S32[:].rearrange("p a c -> p (a c)"), in0=S32[:].rearrange("p a c -> p (a c)"),
            scalar1=c_mask[:, 0:1], scalar2=None, op0=ALU.mult))
        b.opi("vector", lambda e: e.tensor_copy(
            out=Sbf[:].rearrange("p a c -> p (a c)"), in_=S32[:].rearrange("p a c -> p (a c)")), "dv")
        attn_pass(True)
        dbg_dump("o", o_nrm)
        dbg_dump("g1T", g1T)

        b.wait("tensor", "dv")
        b.wait("tensor", "dv2")
        p4_pe_gate = 0
        for h in range(HEADS):
            b.wait("tensor", "dv", p4_pe_gate)
            for th in range(2):
                b.opi("tensor", lambda e, h=h, th=th: e.matmul(
                    bank(th)[:, :], c_wg2[:, h * HD:(h + 1) * HD], g1T[:, th * 512:(th + 1) * 512],
                    start=True, stop=True), "pe")
            b.wait("scalar", "pe")
            for th in range(2):
                b.opi("scalar", lambda e, th=th: e.activation(
                    out=sgt[:, 0, th * 512:(th + 1) * 512], in_=bank(th)[:, :], func=AF.Sigmoid), "ac")
            b.wait("scalar", "ac")
            b.opi("scalar", lambda e: e.nop(), "st")
            b.wait("vector", "st")
            b.opi("vector", lambda e, h=h: e.tensor_tensor(
                out=sgt[:, 1, :], in0=o_nrm[:, h, :], in1=sgt[:, 0, :], op=ALU.mult), "st")
            b.wait("vector", "st")
            b.opi("vector", lambda e: e.tensor_tensor(
                out=scr2[:, 0:T], in0=sgt[:, 1, :], in1=sgt[:, 1, :], op=ALU.mult), "dv")
            sq_dv = b.cnt["dv"]
            p4_pe_gate = sq_dv
            b.wait("tensor", "dv", sq_dv)
            b.wait("tensor", "ac")
            for th in range(2):
                b.opi("tensor", lambda e, th=th: e.matmul(
                    PS[0:1, 2 + th, :], c_1rms[:], scr2[:, th * 512:(th + 1) * 512],
                    start=True, stop=True), "pe")
            b.wait("scalar", "pe")
            b.op("scalar", lambda e: e.activation(out=rms1[0:1, 0:512], in_=PS[0:1, 2, :],
                                                  func=AF.Ln, bias=c_actk[0:1, 1:2]))
            b.opi("scalar", lambda e: e.activation(out=rms1[0:1, 512:1024], in_=PS[0:1, 3, :],
                                                   func=AF.Ln, bias=c_actk[0:1, 1:2]), "st")
            b.wait("scalar", "st")
            b.opi("scalar", lambda e: e.activation(out=rms1[0:1, 0:1024], in_=rms1[0:1, 0:1024],
                                                   func=AF.Exp, scale=c_actk[0:1, 2:3]), "ac")
            b.wait("scalar", "ac")
            b.opi("scalar", lambda e: e.nop(), "st")
            b.wait("tensor", "st")
            for th in range(2):
                b.opi("tensor", lambda e, th=th: e.matmul(
                    bank(4 + th)[:, :], c_1row[0:1, :], rms1[0:1, th * 512:(th + 1) * 512],
                    start=True, stop=True), "pe")
            b.wait("vector", "pe")
            for th in range(2):
                b.opi("vector", lambda e, th=th: e.tensor_tensor(
                    out=scr2[:, T + th * 512:T + (th + 1) * 512],
                    in0=sgt[:, 1, th * 512:(th + 1) * 512],
                    in1=bank(4 + th)[:, :], op=ALU.mult), "st")
            b.wait("vector", "st")
            b.opi("vector", lambda e, h=h: e.tensor_scalar(
                out=o_nrm[:, h, :], in0=scr2[:, T:2 * T], scalar1=c_nw[:, h:h + 1], scalar2=None,
                op0=ALU.mult), "dv")
            p4_pe_gate = b.cnt["dv"]

        dbg_dump("normed", o_nrm)

        b.barrier()
        xre_free = [0]
        tr_free = {0: 0, 1: 0}
        for tc in range(NT):
            b.wait("scalar", "pe2", xre_free[0])
            b.dma("scalar", xstg2[:, 0, :], d["xin"][(NT + tc) * P:(NT + tc + 1) * P, :], "sl")
            b.wait("tensor", "sl")
            for g4 in range(4):
                bk = g4 % 2
                b.wait("tensor", "dv2", tr_free[bk])
                for k4 in range(4):
                    fc = g4 * 4 + k4
                    m, par = fc // 2, fc % 2
                    srct = xstg2[:, 0, 256 * m + par: 256 * (m + 1): 2]
                    b.opi("tensor", lambda e, bk=bk, k4=k4, s_=srct: e.transpose(
                        bank(bk)[:, k4 * P:(k4 + 1) * P], s_, c_idf[:]), "pe")
                b.wait("vector", "pe")
                tr_free[bk] = b.opi("vector", lambda e, bk=bk, g4=g4, tc=tc: e.tensor_copy(
                    out=xT[:, g4 * 4:(g4 + 1) * 4, tc * P:(tc + 1) * P],
                    in_=bank(bk)[:, :].rearrange("p (a c) -> p a c", a=4)), "dv2")
            xre_free[0] = b.opi("tensor", lambda e: e.nop(), "pe2")
        xT_dv = b.cnt["dv2"]

        wo_state = {"conv": {0: 0, 1: 0}, "peuse": {0: 0, 1: 0}}
        res_free = {6: 0, 7: 0}
        b.wait("tensor", "dv2", xT_dv)
        for m in range(8):
            buf = m % 2
            b.wait("sync", "dv", wo_state["conv"][buf])
            src = AP(tensor=d["wo"].tensor, offset=256 * m,
                     ap=[[D, 128], [P * D, NF], [1, 256]])
            b.dma("sync", wostg[:, buf, :, :], src, "wl")
            b.wait("vector", "wl")
            b.wait("vector", "pe", wo_state["peuse"][buf])
            cv = b.opi("vector", lambda e, buf=buf: e.tensor_copy(
                out=wobf[:, buf, :, :].rearrange("p a c -> p (a c)"),
                in_=wostg[:, buf, :, :].rearrange("p a c -> p (a c)")), "dv")
            wo_state["conv"][buf] = cv
            b.wait("tensor", "dv", cv)
            for par in range(2):
                fc = 2 * m + par
                for th in range(2):
                    bk = 6 + th
                    b.wait("tensor", "dv", res_free[bk])
                    for ec in range(NF):
                        b.opi("tensor", lambda e, buf=buf, ec=ec, par=par, th=th, bk=bk: e.matmul(
                            bank(bk)[:, :], wobf[:, buf, ec, par:256:2],
                            o_nrm[:, ec, th * 512:(th + 1) * 512],
                            start=(ec == 0), stop=(ec == NF - 1)), "pe")
                    b.wait("vector", "pe")
                    res_free[bk] = b.opi("vector", lambda e, fc=fc, th=th, bk=bk: e.tensor_tensor(
                        out=xT[:, fc, th * 512:(th + 1) * 512],
                        in0=xT[:, fc, th * 512:(th + 1) * 512], in1=bank(bk)[:, :], op=ALU.add), "dv")
            wo_state["peuse"][buf] = b.cnt["pe"]
        b.wait("tensor", "dv", max(res_free.values()))

        dbg_dump("xa", xT[:, 0:8, :])
        dbg_dump("xa2", xT[:, 8:16, :])

        lnmm = 0
        for fc in range(NF):
            b.wait("vector", "pe", lnmm)
            b.op("vector", lambda e, fc=fc: e.tensor_copy(out=xbs[:], in_=xT[:, fc, :]))
            b.opi("vector", lambda e: e.tensor_tensor(out=sqs[:], in0=xbs[:], in1=xbs[:],
                                                      op=ALU.mult), "dv")
            b.wait("tensor", "dv")
            for th in range(2):
                b.opi("tensor", lambda e, fc=fc, th=th: e.matmul(
                    PS[0:1, th, :], c_1ln[:], xbs[:, th * 512:(th + 1) * 512],
                    start=(fc == 0), stop=(fc == NF - 1)), "pe")
                b.opi("tensor", lambda e, fc=fc, th=th: e.matmul(
                    PS[0:1, 2 + th, :], c_1ln[:], sqs[:, th * 512:(th + 1) * 512],
                    start=(fc == 0), stop=(fc == NF - 1)), "pe")
            lnmm = b.cnt["pe"]
        b.wait("vector", "pe", lnmm)
        b.op("vector", lambda e: e.tensor_copy(out=lnsc[0:1, 0:512], in_=PS[0:1, 0, :]))
        b.opi("vector", lambda e: e.tensor_copy(out=lnsc[0:1, 512:1024], in_=PS[0:1, 1, :]), "st")
        b.wait("vector", "st")
        b.op("vector", lambda e: e.tensor_tensor(out=rms2[0:1, 0:512], in0=lnsc[0:1, 0:512],
                                                 in1=lnsc[0:1, 0:512], op=ALU.mult))
        b.opi("vector", lambda e: e.tensor_tensor(out=rms2[0:1, 512:1024], in0=lnsc[0:1, 512:1024],
                                                  in1=lnsc[0:1, 512:1024], op=ALU.mult), "st")
        b.wait("vector", "st")
        b.op("vector", lambda e: e.tensor_tensor(out=rms2[0:1, 0:512], in0=PS[0:1, 2, :],
                                                 in1=rms2[0:1, 0:512], op=ALU.subtract))
        b.opi("vector", lambda e: e.tensor_tensor(out=rms2[0:1, 512:1024], in0=PS[0:1, 3, :],
                                                  in1=rms2[0:1, 512:1024], op=ALU.subtract), "dv")
        b.wait("vector", "dv")
        b.opi("vector", lambda e: e.engine_nop(), "st")
        b.wait("scalar", "st")
        b.opi("scalar", lambda e: e.activation(out=rms2[0:1, 0:1024], in_=rms2[0:1, 0:1024],
                                               func=AF.Ln, bias=c_actk[0:1, 0:1]), "st")
        b.wait("scalar", "st")
        b.opi("scalar", lambda e: e.activation(out=rms2[0:1, 0:1024], in_=rms2[0:1, 0:1024],
                                               func=AF.Exp, scale=c_actk[0:1, 2:3]), "ac")
        b.wait("scalar", "ac")
        b.opi("scalar", lambda e: e.nop(), "st")
        b.wait("vector", "st")
        b.opi("vector", lambda e: e.scalar_tensor_tensor(
            out=lnsc[0:1, 0:1024], in0=lnsc[0:1, 0:1024], scalar=-1.0, in1=rms2[0:1, 0:1024],
            op0=ALU.mult, op1=ALU.mult), "dv")
        b.wait("vector", "dv")
        b.opi("vector", lambda e: e.engine_nop(), "st")
        b.wait("tensor", "st")
        b.wait("tensor", "dv")
        for th in range(2):
            b.opi("tensor", lambda e, th=th: e.matmul(
                bank(4 + th)[:, :], c_1row[0:1, :], rms2[0:1, th * 512:(th + 1) * 512],
                start=True, stop=True), "pe")
            b.opi("tensor", lambda e, th=th: e.matmul(
                bank(6 + th)[:, :], c_1row[0:1, :], lnsc[0:1, th * 512:(th + 1) * 512],
                start=True, stop=True), "pe")
        b.wait("vector", "pe")
        for fc in range(NF):
            for th in range(2):
                b.opi("vector", lambda e, fc=fc, th=th: e.tensor_tensor(
                    out=rms2[:, th * 512:(th + 1) * 512],
                    in0=xT[:, fc, th * 512:(th + 1) * 512], in1=bank(4 + th)[:, :], op=ALU.mult), "st")
                b.wait("vector", "st")
                b.opi("vector", lambda e, fc=fc, th=th: e.tensor_tensor(
                    out=xln2[:, fc, th * 512:(th + 1) * 512],
                    in0=rms2[:, th * 512:(th + 1) * 512], in1=bank(6 + th)[:, :], op=ALU.add), "dv")
        xln2_dv = b.cnt["dv"]
        dbg_dump("xln2", xln2)
        b.barrier()

        scb_war = {"gu": 0, "dn": 0}

        def dec_batch(v16, shlo, dst_lo, dst_hi, scf, scb, nj, key, sem=None):
            b.op("vector", lambda e, scf=scf, scb=scb: e.tensor_scalar(
                out=scb, in0=scf, scalar1=float(2.0 ** 126), scalar2=None, op0=ALU.mult))
            last = None
            for k, (sh, dst) in enumerate(((shlo, dst_lo), (shlo + 4, dst_hi))):
                t = scrC[:, 0:nj, :]
                if sh == 0:
                    b.op("vector", lambda e, v16=v16, t=t: e.tensor_scalar(
                        out=t, in0=v16, scalar1=0x000F, scalar2=None, op0=ALU.bitwise_and))
                else:
                    b.op("vector", lambda e, v16=v16, t=t, sh=sh: e.tensor_scalar(
                        out=t, in0=v16, scalar1=sh, scalar2=0x000F,
                        op0=ALU.logical_shift_right, op1=ALU.bitwise_and))
                b.op("vector", lambda e, t=t: e.tensor_scalar(
                    out=t, in0=t, scalar1=0x1040, scalar2=None, op0=ALU.mult))
                b.op("vector", lambda e, t=t: e.tensor_scalar(
                    out=t, in0=t, scalar1=0x81C0, scalar2=None, op0=ALU.bitwise_and))
                fn = lambda e, t=t, dst=dst, scb=scb: e.tensor_tensor(
                    out=dst, in0=t.bitcast(BF16), in1=scb, op=ALU.mult)
                if k == 1 and sem is not None:
                    last = b.opi("vector", fn, sem)
                else:
                    b.op("vector", fn)
            scb_war[key] = last if last is not None else b.cnt["dv"]
            return last

        pk_done = {}

        def pk_dma(mat, w):
            arr = gpk if mat == "g" else upk
            src = AP(tensor=d[mat + "_pk"].tensor, offset=512 * w,
                     ap=[[MLP, 128], [128 * MLP, 8], [1, 512]])
            pk_done[(mat, w)] = b.dma("sync", arr[:, w % 2, :, :], src, "wl")

        dpk_done = {}

        def dpk_dma(s):
            src = AP(tensor=d["d_pk"].tensor, offset=(4 * s) * 128 * D,
                     ap=[[D, 128], [128 * D, 4], [1, D]])
            dpk_done[s] = b.dma("sync", dpk[:], src, "wl")

        b.wait("sync", "dv", max(res_free.values()))
        pk_dma("g", 0)
        pk_dma("u", 0)
        decuse = {"g": 0, "u": 0}
        pkuse = {("g", 0): 0, ("g", 1): 0, ("u", 0): 0, ("u", 1): 0}
        gu_bank_free = {0: 0, 1: 0, 2: 0, 3: 0}
        dn_bank_free = {4: 0, 5: 0}
        wd_use = {0: 0, 1: 0}
        inter_tt = [0]
        dpk_decoded = xln2_dv
        dscuse = [0]

        for s_i in range(NS):
            b.wait("sync", "dv", dpk_decoded)
            dpk_dma(s_i)
            for ww in range(2):
                w = 2 * s_i + ww
                wbuf = w % 2
                if w + 1 < NWIN:
                    b.wait("sync", "dv", pkuse[("g", (w + 1) % 2)])
                    pk_dma("g", w + 1)
                    b.wait("sync", "dv", pkuse[("u", (w + 1) % 2)])
                    pk_dma("u", w + 1)
                for npar in range(2):
                    for mat in ("g", "u"):
                        dec = gdec if mat == "g" else udec
                        arr = gpk if mat == "g" else upk
                        nm = f"{mat}_sc_{'e' if npar == 0 else 'o'}"
                        b.wait("sync", "dv", scb_war["gu"])
                        scd = b.dma("sync", gsc[:],
                                    AP(tensor=d[nm].tensor, offset=256 * w,
                                       ap=[[MLP // 2, 128], [128 * (MLP // 2), 8], [1, 256]]),
                                    "wl")
                        b.wait("vector", "wl", scd)
                        b.wait("vector", "wl", pk_done[(mat, w)])
                        b.wait("vector", "pe", decuse[mat])
                        v16 = arr[:, wbuf, :, :].bitcast(U16)
                        dv_dec = dec_batch(v16, 8 * npar, dec[:, 0, :, :], dec[:, 1, :, :],
                                           gsc[:, :, :], gscb[:, :, :], 8, "gu", sem="dv")
                        pkuse[(mat, wbuf)] = dv_dec
                        b.wait("tensor", "dv", dv_dec)
                        if s_i == 0 and ww == 0 and mat == "g" and npar == 0:
                            b.wait("tensor", "dv", xln2_dv)
                        if mat == "g":
                            b.wait("scalar", "dv", inter_tt[0])
                        for ci in range(2):
                            islot = 4 * npar + 2 * ww + ci
                            for th in range(2):
                                bk = (ci * 2 + th) % 4
                                b.wait("tensor", "ac", gu_bank_free[bk])
                                for kc in range(NF):
                                    b.opi("tensor", lambda e, kc=kc, ci=ci, th=th, bk=bk, dec=dec: e.matmul(
                                        bank(bk)[:, :], dec[:, kc % 2, kc // 2, ci * 128:(ci + 1) * 128],
                                        xln2[:, kc, th * 512:(th + 1) * 512],
                                        start=(kc == 0), stop=(kc == NF - 1)), "pe")
                                mme = b.cnt["pe"]
                                sl = ci * 2 + th
                                b.wait("scalar", "pe", mme)
                                if mat == "g":
                                    gu_bank_free[bk] = b.opi("scalar", lambda e, bk=bk, sl=sl: e.activation(
                                        out=gl[:, sl * 512:(sl + 1) * 512], in_=bank(bk)[:, :],
                                        func=AF.Gelu_apprx_tanh), "ac")
                                else:
                                    gu_bank_free[bk] = b.opi("scalar", lambda e, bk=bk, sl=sl: e.activation(
                                        out=up_sb[:, sl * 512:(sl + 1) * 512], in_=bank(bk)[:, :],
                                        func=AF.Copy), "ac")
                        decuse[mat] = b.cnt["pe"]
                        if mat == "u":
                            base = 4 * npar + 2 * ww
                            b.wait("vector", "ac")
                            inter_tt[0] = b.opi("vector", lambda e, base=base: e.tensor_tensor(
                                out=inter[:, base:base + 2, :],
                                in0=gl[:].rearrange("p (a c) -> p a c", a=2),
                                in1=up_sb[:].rearrange("p (a c) -> p a c", a=2),
                                op=ALU.mult), "dv")
            inter_done = b.cnt["dv"]
            if s_i == 0:
                dbg_dump("inter0", inter)
            b.wait("vector", "wl", dpk_done[s_i])
            for q in range(4):
                for hf in range(2):
                    blk = q * 2 + hf
                    wdb = wddec if blk % 2 == 0 else wddec2
                    nm = "d_sc_a" if hf == 0 else "d_sc_b"
                    src = AP(tensor=d[nm].tensor, offset=(512 * s_i) * 1024 + 256 * q,
                             ap=[[1024, 128], [128 * 1024, 4], [1, 256]])
                    b.wait("sync", "dv", scb_war["dn"])
                    dsd = b.dma("sync", dsc[:], src, "wl")
                    b.wait("vector", "wl", dsd)
                    b.wait("vector", "pe", wd_use[blk % 2])
                    v16 = dpk[:, :, 512 * q:512 * q + 512].bitcast(U16)
                    wd_dv = dec_batch(v16, 8 * hf, wdb[:, 0:4, :], wdb[:, 4:8, :],
                                      dsc[:, :, :], dscb[:, :, :], 4, "dn", sem="dv")
                    b.wait("tensor", "dv", wd_dv)
                    if q == 0 and hf == 0:
                        b.wait("tensor", "dv", inter_done)
                    for hci in range(2):
                        hc = hf * 8 + 2 * q + hci
                        for th in range(2):
                            bk = 4 + ((hci * 2 + th) % 2)
                            b.wait("tensor", "dv", dn_bank_free[bk])
                            for sl in range(8):
                                jc, nib = sl % 4, sl // 4
                                islot = 4 * nib + jc
                                b.opi("tensor", lambda e, sl=sl, hci=hci, th=th, bk=bk, islot=islot, wdb=wdb: e.matmul(
                                    bank(bk)[:, :], wdb[:, sl, hci * 128:(hci + 1) * 128],
                                    inter[:, islot, th * 512:(th + 1) * 512],
                                    start=(sl == 0), stop=(sl == 7)), "pe")
                            b.wait("vector", "pe")
                            dn_bank_free[bk] = b.opi("vector", lambda e, hc=hc, th=th, bk=bk: e.tensor_tensor(
                                out=xT[:, hc, th * 512:(th + 1) * 512],
                                in0=xT[:, hc, th * 512:(th + 1) * 512], in1=bank(bk)[:, :],
                                op=ALU.add), "dv")
                    wd_use[blk % 2] = b.cnt["pe"]
            dpk_decoded = b.cnt["dv"]

        b.wait("tensor", "dv", max(dn_bank_free.values()))
        b.barrier()

        st_done = {0: 0, 1: 0}
        tr_free = {0: 0, 1: 0}
        for tc in range(NT):
            buf = tc % 2
            b.wait("vector", "sl", st_done[buf])
            evacs = []
            for g4 in range(4):
                bk = g4 % 2
                b.wait("tensor", "dv2", tr_free[bk])
                for k4 in range(4):
                    fc = g4 * 4 + k4
                    b.opi("tensor", lambda e, bk=bk, k4=k4, fc=fc, tc=tc: e.transpose(
                        bank(bk)[:, k4 * P:(k4 + 1) * P], xT[:, fc, tc * P:(tc + 1) * P],
                        c_idf[:]), "pe")
                b.wait("vector", "pe")
                last = 0
                for k4 in range(4):
                    fc = g4 * 4 + k4
                    m, par = fc // 2, fc % 2
                    dstv = ostg[:, buf, 256 * m + par: 256 * (m + 1): 2]
                    last = b.opi("vector", lambda e, bk=bk, k4=k4, dstv=dstv: e.tensor_copy(
                        out=dstv, in_=bank(bk)[:, k4 * P:(k4 + 1) * P]), "dv2")
                tr_free[bk] = last
            b.wait("scalar", "dv2")
            st_done[buf] = b.dma("scalar", d_out[tc * P:(tc + 1) * P, :], ostg[:, buf, :], "sl")
        b.wait("scalar", "sl")

        import os
        if os.environ.get("K_PRINT_CNT"):
            print("SEM COUNTS:", {k: v for k, v in b.cnt.items()})
            print("Q LENGTHS:", {k: len(v) for k, v in b.q.items()})
        @block.sync
        def _(eng):
            for f in b.q["sync"]:
                f(eng)

        @block.vector
        def _(eng):
            for f in b.q["vector"]:
                f(eng)

        @block.scalar
        def _(eng):
            for f in b.q["scalar"]:
                f(eng)

        @block.tensor
        def _(eng):
            for f in b.q["tensor"]:
                f(eng)

        @block.gpsimd
        def _(eng):
            for f in b.q["gpsimd"]:
                f(eng)

    return nc


_NC_CACHE = [None]


def kernel(**inputs):
    in_maps = _host_prep(inputs)
    if _NC_CACHE[0] is None:
        _NC_CACHE[0] = build_nc()
    nc = _NC_CACHE[0]
    res = run_bass_kernel_spmd(nc, in_maps, core_ids=list(range(8)))
    out = np.empty((B, S, D), np.float32)
    for core in range(8):
        bb, h = core // 2, core % 2
        out[bb, h * T:(h + 1) * T] = res.results[core]["out"]
    return out


# version 09 (final)
# speedup vs baseline: 1.1272x; 1.0457x over previous
import numpy as np
import ml_dtypes
import concourse.bass as bass
import concourse.mybir as mybir
from concourse.bass import AP
from concourse.bass_utils import run_bass_kernel_spmd
from concourse.alu_op_type import AluOpType as ALU

AF = mybir.ActivationFunctionType
F32 = mybir.dt.float32
BF16 = mybir.dt.bfloat16
U8 = mybir.dt.uint8
U16 = mybir.dt.uint16
BF = ml_dtypes.bfloat16

P = 128
B, S, D = 4, 2048, 2048
HEADS, HD, CH = 16, 128, 128
MLP = 16384
T = 1024
NT = T // P
NF = D // P
HG = 2
NHG = HEADS // HG
EPS_LN = 1e-5
EPS_RMS = 1e-6
NWIN = 32
NS = 16
DBG = []


def _fperm():
    fp = np.empty(D, np.int64)
    for m in range(8):
        fp[256 * m:256 * m + 128] = 256 * m + 2 * np.arange(128)
        fp[256 * m + 128:256 * m + 256] = 256 * m + 2 * np.arange(128) + 1
    return fp


def _host_prep(inputs):
    x = np.asarray(inputs["L0_attn_x"])
    nw = np.asarray(inputs["L0_attn_norm_weight"])

    def pk_bytes(a):
        a = np.asarray(a)
        return np.ascontiguousarray(a.view(np.uint8).reshape(a.shape[0], a.shape[1], 4)[:, :, 0])

    g_pk = np.ascontiguousarray(pk_bytes(inputs["L0_ffn_gate_W_packed"]).T)
    u_pk = np.ascontiguousarray(pk_bytes(inputs["L0_ffn_up_W_packed"]).T)
    d_pk_t = pk_bytes(inputs["L0_ffn_down_W_packed"]).T

    fp = _fperm()
    col_order = np.empty(D, np.int64)
    col_order[0::2] = fp[:1024]
    col_order[1::2] = fp[1024:]
    d_pk = np.ascontiguousarray(d_pk_t[:, col_order])

    g_sc_t = np.asarray(inputs["L0_ffn_gate_scales"]).T
    u_sc_t = np.asarray(inputs["L0_ffn_up_scales"]).T
    d_sc_t = np.asarray(inputs["L0_ffn_down_scales"]).T
    rep16 = lambda a: np.ascontiguousarray(np.repeat(a, 16, axis=0))
    g_sc_e = rep16(g_sc_t[:, 0::2]); g_sc_o = rep16(g_sc_t[:, 1::2])
    u_sc_e = rep16(u_sc_t[:, 0::2]); u_sc_o = rep16(u_sc_t[:, 1::2])
    d_sc_a = rep16(d_sc_t[:, fp[:1024]])
    d_sc_b = rep16(d_sc_t[:, fp[1024:]])

    hh = np.arange(HEADS, dtype=np.float64)
    log_lam = -(2.0 ** (-8.0 * (hh + 1.0) / HEADS))
    i = np.arange(CH, dtype=np.float64)
    diff = i[:, None] - i[None, :]
    dT = np.zeros((P, HEADS * CH), np.float64)
    decq = np.zeros((P, HEADS * CH), np.float64)
    deck = np.zeros((P, HEADS * CH), np.float64)
    lamc = np.zeros((P, HEADS * CH), np.float64)
    for h in range(HEADS):
        Dh = np.where(diff >= 0, np.exp(log_lam[h] * diff), 0.0)
        dT[:, h * CH:(h + 1) * CH] = Dh.T
        decq[:, h * CH:(h + 1) * CH] = np.exp(log_lam[h] * (i + 1.0))[None, :]
        deck[:, h * CH:(h + 1) * CH] = np.exp(log_lam[h] * (CH - 1.0 - i))[:, None]
        lamc[:, h * CH:(h + 1) * CH] = np.exp(log_lam[h] * CH)

    half = HD // 2
    theta = 10000.0 ** (-2.0 * np.arange(half, dtype=np.float64) / HD)

    def ccss(pos0):
        ang = np.arange(pos0, pos0 + T, dtype=np.float64)[None, :] * theta[:, None]
        c, s = np.cos(ang), np.sin(ang)
        return (np.concatenate([c, c], 0).astype(BF),
                np.concatenate([-s, s], 0).astype(BF))

    cc0, ss0 = ccss(0)
    cc1, ss1 = ccss(T)

    swap = np.zeros((P, P), BF)
    for j in range(64):
        swap[j + 64, j] = 1
        swap[j, j + 64] = 1

    shared = dict(
        wq=np.asarray(inputs["L0_attn_Wq"]), wk=np.asarray(inputs["L0_attn_Wk"]),
        wv=np.asarray(inputs["L0_attn_Wv"]), wo=np.asarray(inputs["L0_attn_Wo"]),
        wg1=np.asarray(inputs["L0_attn_Wg1"]), wg2=np.asarray(inputs["L0_attn_Wg2"]),
        nw=np.ascontiguousarray(nw.reshape(HEADS, HD).T).astype(np.float32),
        g_pk=g_pk, u_pk=u_pk, d_pk=d_pk,
        g_sc_e=g_sc_e, g_sc_o=g_sc_o, u_sc_e=u_sc_e, u_sc_o=u_sc_o,
        d_sc_a=d_sc_a, d_sc_b=d_sc_b,
        dT=dT.astype(BF), decq=decq.astype(BF), deck=deck.astype(BF),
        lamc=lamc.astype(BF), swap=swap,
        ident_bf=np.eye(P, dtype=BF), ident_f32=np.eye(P, dtype=np.float32),
        ones_ln=np.full((P, 1), 1.0 / D, BF),
        ones_rms=np.full((P, 1), 1.0 / HD, BF),
        ones_row=np.ones((1, P), np.float32),
        actk=np.tile(np.array([[EPS_LN, EPS_RMS, -0.5, 0.0, float(2.0 ** 126)]], np.float32), (P, 1)),
    )
    in_maps = []
    for core in range(8):
        bb, h = core // 2, core % 2
        own = np.ascontiguousarray(x[bb, h * T:(h + 1) * T])
        ctx = np.ascontiguousarray(x[bb, 0:T]) if h == 1 else own
        m = dict(shared)
        m["xin"] = np.ascontiguousarray(np.concatenate([ctx, own], 0))
        m["smask"] = np.full((P, 1), float(h), np.float32)
        m["cc_c"], m["ss_c"] = cc0, ss0
        m["cc_o"], m["ss_o"] = (cc1, ss1) if h == 1 else (cc0, ss0)
        in_maps.append(m)
    return in_maps


class Bld:
    ENGS = ("sync", "vector", "scalar", "tensor", "gpsimd")

    def __init__(self, nc):
        self.nc = nc
        self.q = {e: [] for e in self.ENGS}
        self.sem = {}
        self.cnt = {}

    def new_sem(self, name, handle):
        self.sem[name] = handle
        self.cnt[name] = 0

    def op(self, eng, fn):
        self.q[eng].append(fn)

    def opi(self, eng, fn, sem, k=1):
        h = self.sem[sem]
        self.q[eng].append(lambda e, fn=fn, h=h, k=k: fn(e).then_inc(h, k))
        self.cnt[sem] += k
        return self.cnt[sem]

    def dma(self, eng, out, in_, sem):
        self.wait(eng, sem)
        return self.opi(eng, lambda e, o=out, i=in_: e.dma_start(out=o, in_=i), sem, 16)

    def wait(self, eng, sem, v=None):
        hv = self.cnt[sem] if v is None else v
        if hv > 0:
            h = self.sem[sem]
            self.q[eng].append(lambda e, h=h, hv=hv: e.wait_ge(h, hv))

    def barrier(self):
        h = self.sem["bar"]
        for e in self.ENGS:
            if e == "tensor":
                self.q[e].append(lambda en, h=h: en.nop().then_inc(h, 1))
            elif e in ("vector", "gpsimd"):
                self.q[e].append(lambda en, h=h: en.engine_nop().then_inc(h, 1))
            else:
                self.q[e].append(lambda en, h=h: en.nop().then_inc(h, 1))
        self.cnt["bar"] += 5
        tgt = self.cnt["bar"]
        for e in self.ENGS:
            self.q[e].append(lambda en, h=h, v=tgt: en.wait_ge(h, v))


def build_nc():
    nc = bass.Bass("TRN2", target_bir_lowering=False, debug=False,
                   detect_race_conditions=False)
    d = {}

    def din(name, shape, dt):
        d[name] = nc.dram_tensor(name, shape, dt, kind="ExternalInput").ap()

    din("xin", [2 * T, D], F32)
    din("smask", [P, 1], F32)
    for n in ("wq", "wk", "wv", "wo"):
        din(n, [D, D], F32)
    din("wg1", [D, HD], F32)
    din("wg2", [HD, D], F32)
    din("nw", [P, HEADS], F32)
    din("g_pk", [1024, MLP], U8)
    din("u_pk", [1024, MLP], U8)
    din("d_pk", [MLP // 2, D], U8)
    din("g_sc_e", [1024, MLP // 2], F32); din("g_sc_o", [1024, MLP // 2], F32)
    din("u_sc_e", [1024, MLP // 2], F32); din("u_sc_o", [1024, MLP // 2], F32)
    din("d_sc_a", [MLP // 2, 1024], F32); din("d_sc_b", [MLP // 2, 1024], F32)
    din("dT", [P, HEADS * CH], BF16); din("decq", [P, HEADS * CH], BF16)
    din("deck", [P, HEADS * CH], BF16); din("lamc", [P, HEADS * CH], BF16)
    din("swap", [P, P], BF16); din("ident_bf", [P, P], BF16); din("ident_f32", [P, P], F32)
    din("ones_ln", [P, 1], BF16); din("ones_rms", [P, 1], BF16); din("ones_row", [1, P], F32)
    din("actk", [P, 5], F32)
    for n in ("cc_c", "ss_c", "cc_o", "ss_o"):
        din(n, [P, T], BF16)
    d_out = nc.dram_tensor("out", [T, D], F32, kind="ExternalOutput").ap()
    d_dbg = {}
    for nm in DBG:
        d_dbg[nm] = nc.dram_tensor("dbg_" + nm, [P, 32 * 1024], F32, kind="ExternalOutput").ap()

    KB = 1024
    AB = 207 * KB

    with (
        nc.sbuf_tensor("arena", [P, AB], U8) as arena,
        nc.psum_tensor("PS", [P, 8, 512], F32) as PS,
        nc.semaphore("s_wl") as s_wl,
        nc.semaphore("s_gl") as s_gl,
        nc.semaphore("s_sl") as s_sl,
        nc.semaphore("s_pe") as s_pe,
        nc.semaphore("s_dv") as s_dv,
        nc.semaphore("s_ac") as s_ac,
        nc.semaphore("s_dv2") as s_dv2,
        nc.semaphore("s_pe2") as s_pe2,
        nc.semaphore("s_bar") as s_bar,
        nc.semaphore("s_st") as s_st,
        nc.Block() as block,
    ):
        b = Bld(nc)
        for nm, h in (("wl", s_wl), ("gl", s_gl), ("sl", s_sl), ("pe", s_pe), ("dv", s_dv),
                      ("ac", s_ac), ("dv2", s_dv2), ("pe2", s_pe2), ("bar", s_bar),
                      ("st", s_st)):
            b.new_sem(nm, h)

        def bank(i, dt=F32, n=None):
            v = PS[:, i, :]
            if dt == BF16:
                v = v.bitcast(BF16)
            if n is not None:
                v = v[:, 0:n]
            return v

        off = [0]

        def carve(nb, name=""):
            o = off[0]
            assert o + nb <= AB, f"SBUF overflow {name}: {o}+{nb}"
            off[0] += (nb + 63) & ~63
            return o

        def view(o, shape, dt):
            esz = {F32: 4, BF16: 2, U16: 2, U8: 1}[dt]
            nb = int(np.prod(shape[1:])) * esz
            a = arena[:, o:o + nb].bitcast(dt)
            if len(shape) == 3:
                a = a.rearrange("p (a c) -> p a c", a=shape[1])
            elif len(shape) == 4:
                a = a.rearrange("p (a b c) -> p a b c", a=shape[1], b=shape[2])
            return a

        c_dT = view(carve(4 * KB, "dT"), [P, HEADS, CH], BF16)
        c_decq = view(carve(4 * KB), [P, HEADS, CH], BF16)
        c_deck = view(carve(4 * KB), [P, HEADS, CH], BF16)
        c_lamc = view(carve(4 * KB), [P, HEADS, CH], BF16)
        c_cc = {"c": view(carve(2 * KB), [P, T], BF16)}
        c_ss = {"c": view(carve(2 * KB), [P, T], BF16)}
        c_cc["o"] = view(carve(2 * KB), [P, T], BF16)
        c_ss["o"] = view(carve(2 * KB), [P, T], BF16)
        c_swap = view(carve(256), [P, P], BF16)
        c_idb = view(carve(256), [P, P], BF16)
        c_idf = view(carve(512), [P, P], F32)
        c_1ln = view(carve(64), [P, 1], BF16)
        c_1rms = view(carve(64), [P, 1], BF16)
        c_1row = view(carve(512), [P, P], F32)
        c_nw = view(carve(64), [P, HEADS], F32)
        c_actk = view(carve(64), [P, 5], F32)
        c_mask = view(carve(64), [P, 1], F32)
        c_wg2 = view(carve(4 * KB), [P, D], BF16)
        c_wg1 = view(carve(4 * KB), [P, NF, HD], BF16)
        oonrm = carve(32 * KB, "o_nrm")
        o_nrm = view(oonrm, [P, HEADS, T], BF16)
        g1T = view(carve(2 * KB), [P, T], BF16)
        base0 = off[0]

        offA = [base0]

        def carveA(nb, name=""):
            o = offA[0]
            assert o + nb <= AB, f"A overflow {name}: {o}+{nb}"
            offA[0] += (nb + 63) & ~63
            return o

        xln = view(carveA(32 * KB, "xln"), [P, NF, T], BF16)
        oxcln = carveA(32 * KB, "xcln")
        xcln = view(oxcln, [P, NF, T], BF16)
        wstage = view(carveA(16 * KB, "wstage"), [P, 2, 2048], F32)
        wbf = view(carveA(16 * KB, "wbf"), [P, 2, NF, HG * HD], BF16)
        qsb = view(carveA(4 * KB, "q"), [P, HG, T], BF16)
        ksb = view(carveA(4 * KB, "k"), [P, HG, T], BF16)
        vsb = view(carveA(4 * KB, "v"), [P, NT, HG * HD], BF16)
        S32 = view(carveA(8 * KB, "S"), [P, HEADS, HD], F32)
        Sbf = view(carveA(4 * KB, "Sbf"), [P, HEADS, HD], BF16)
        xstg = view(carveA(8 * KB, "xstg"), [P, 1, D], F32)
        scr2 = view(carveA(4 * KB, "scr2"), [P, D], BF16)
        scrq = view(carveA(1 * KB, "scrq"), [P, HG * CH], BF16)
        scrk = view(carveA(1 * KB, "scrk"), [P, HG * CH], BF16)
        scrs = view(carveA(1 * KB, "scrs"), [P, HG * CH], BF16)
        stat = view(carveA(256, "stat"), [P, 64], F32)
        sgt = view(oxcln, [P, 2, T], BF16)
        rms1 = view(oxcln + 4 * KB, [P, T], F32)
        assert offA[0] <= AB, "A total %d" % offA[0]

        offB = [base0]

        def carveB(nb, name=""):
            o = offB[0]
            assert o + nb <= AB, f"B overflow {name}"
            offB[0] += (nb + 63) & ~63
            return o

        xT = view(carveB(64 * KB, "xT"), [P, NF, T], F32)
        wostg = view(carveB(32 * KB, "wostg"), [P, 2, NF, 256], F32)
        wobf = view(carveB(16 * KB, "wobf"), [P, 2, NF, 256], BF16)
        xstg2 = view(carveB(8 * KB, "xstg2"), [P, 1, D], F32)
        sqs = view(carveB(2 * KB, "sqs"), [P, T], BF16)
        xbs = view(carveB(2 * KB, "xbs"), [P, T], BF16)
        lnsc = view(carveB(4 * KB, "lnsc"), [P, T], F32)
        rms2 = view(carveB(4 * KB, "rms2"), [P, T], F32)
        assert offB[0] <= AB

        offC = [base0 + 64 * KB]

        def carveC(nb, name=""):
            o = offC[0]
            assert o + nb <= AB, f"C overflow {name}: {o}+{nb}"
            offC[0] += (nb + 63) & ~63
            return o

        xln2 = view(oonrm, [P, NF, T], BF16)
        inter = view(carveC(16 * KB, "inter"), [P, 8, T], BF16)
        ogdec = carveC(8 * KB, "gdec")
        gdec = view(ogdec, [P, 2, 8, 256], BF16)
        oudec = carveC(8 * KB, "udec")
        udec = view(oudec, [P, 2, 8, 256], BF16)
        wddec = view(carveC(4 * KB, "wddec"), [P, 8, 256], BF16)
        gpk = view(carveC(8 * KB, "gpk"), [P, 2, 8, 512], U8)
        upk = view(carveC(8 * KB, "upk"), [P, 2, 8, 512], U8)
        dpk = view(carveC(8 * KB, "dpk"), [P, 4, D], U8)
        gsc = view(carveC(8 * KB, "gsc"), [P, 8, 256], F32)
        gscb = view(carveC(4 * KB, "gscb"), [P, 8, 256], BF16)
        gl = view(0, [P, 2048], BF16)
        dsc = view(4 * KB, [P, 4, 256], F32)
        dscb = view(8 * KB, [P, 4, 256], BF16)
        scrC = view(10 * KB, [P, 8, 256], U16)
        up_sb = view(16 * KB, [P, 2048], BF16)
        wddec2 = view(20 * KB, [P, 8, 256], BF16)
        assert offC[0] <= AB, f"C total {offC[0]}"
        ostg = view(ogdec, [P, 2, D], F32)

        def dbg_dump(nm, ap2d):
            if nm not in d_dbg:
                return
            b.barrier()
            n = int(np.prod(ap2d.shape[1:]))
            if ap2d.dtype == BF16:
                for i in range(0, n, 2048):
                    w = min(2048, n - i)
                    b.op("vector", lambda e, i=i, w=w: e.tensor_copy(
                        out=xstg[:, 0, 0:w] if w <= 2048 else None,
                        in_=ap2d.rearrange("p a c -> p (a c)")[:, i:i + w] if len(ap2d.shape) == 3 else ap2d[:, i:i + w]))
                    b.opi("vector", lambda e: e.engine_nop(), "dv")
                    b.wait("scalar", "dv")
                    b.dma("scalar", d_dbg[nm][:, i:i + w], xstg[:, 0, 0:w], "sl")
                    b.wait("vector", "sl")
            else:
                a2 = ap2d.rearrange("p a c -> p (a c)") if len(ap2d.shape) == 3 else ap2d
                b.dma("scalar", d_dbg[nm][:, 0:n], a2, "sl")
                b.wait("scalar", "sl")
            b.barrier()

        for t_, n_ in ((c_dT.rearrange("p a c -> p (a c)"), "dT"),
                       (c_decq.rearrange("p a c -> p (a c)"), "decq"),
                       (c_deck.rearrange("p a c -> p (a c)"), "deck"),
                       (c_lamc.rearrange("p a c -> p (a c)"), "lamc"),
                       (c_cc["c"], "cc_c"), (c_ss["c"], "ss_c"),
                       (c_cc["o"], "cc_o"), (c_ss["o"], "ss_o"),
                       (c_swap, "swap"), (c_idb, "ident_bf"), (c_idf, "ident_f32"),
                       (c_1ln, "ones_ln"), (c_1rms, "ones_rms"),
                       (c_nw, "nw"), (c_mask, "smask"), (c_actk, "actk")):
            b.dma("gpsimd", t_, d[n_], "gl")
        b.dma("gpsimd", c_1row[0:1, :], d["ones_row"], "gl")
        b.dma("sync", wstage[:, 0, :], d["wg2"], "wl")
        for par in range(2):
            src = AP(tensor=d["wg1"].tensor, offset=par * HD,
                     ap=[[2 * HD, 128], [256 * HD, 8], [1, HD]])
            dst = wstage[:, 1, par * 1024:(par + 1) * 1024].rearrange("p (a c) -> p a c", a=8)
            b.dma("sync", dst, src, "wl")
        b.wait("vector", "wl")
        b.op("vector", lambda e: e.tensor_copy(out=c_wg2[:], in_=wstage[:, 0, :]))
        for par in range(2):
            for m in range(8):
                s_ = wstage[:, 1, par * 1024 + m * HD: par * 1024 + (m + 1) * HD]
                b.op("vector", lambda e, s_=s_, kc=2 * m + par: e.tensor_copy(
                    out=c_wg1[:, kc, :], in_=s_))
        b.op("vector", lambda e: e.memset(S32[:].rearrange("p a c -> p (a c)"), 0.0))
        b.opi("vector", lambda e: e.memset(Sbf[:].rearrange("p a c -> p (a c)"), 0.0), "dv")
        wgpre_dv = b.cnt["dv"]

        b.wait("vector", "gl")
        tr_free = {0: 0, 1: 0}
        ln_done = {}
        ln_tr_pe = [0]
        for ti in range(2 * NT):
            if ti >= 1:
                b.wait("gpsimd", "dv", ln_done[ti - 1])
            b.dma("gpsimd", xstg[:, 0, :], d["xin"][ti * P:(ti + 1) * P, :], "gl")
            b.wait("vector", "gl")
            xt = xstg[:, 0, :]
            b.op("vector", lambda e, xt=xt: e.reduce_sum(stat[:, 0:1], xt, axis=mybir.AxisListType.X))
            b.op("vector", lambda e, xt=xt: e.tensor_tensor(out=scr2[:], in0=xt, in1=xt, op=ALU.mult))
            b.opi("vector", lambda e: e.reduce_sum(stat[:, 1:2], scr2[:], axis=mybir.AxisListType.X), "st")
            b.wait("vector", "st")
            b.opi("vector", lambda e: e.tensor_scalar(out=stat[:, 2:3], in0=stat[:, 0:1],
                                                      scalar1=1.0 / D, scalar2=None, op0=ALU.mult), "st")
            b.wait("vector", "st")
            b.opi("vector", lambda e: e.tensor_tensor(out=stat[:, 4:5], in0=stat[:, 2:3],
                                                      in1=stat[:, 2:3], op=ALU.mult), "st")
            b.wait("vector", "st")
            b.opi("vector", lambda e: e.scalar_tensor_tensor(
                out=stat[:, 5:6], in0=stat[:, 1:2], scalar=1.0 / D, in1=stat[:, 4:5],
                op0=ALU.mult, op1=ALU.subtract), "dv")
            b.wait("vector", "dv")
            b.opi("vector", lambda e: e.engine_nop(), "st")
            b.wait("scalar", "st")
            b.opi("scalar", lambda e: e.activation(out=stat[:, 6:7], in_=stat[:, 5:6],
                                                   func=AF.Ln, bias=c_actk[:, 0:1]), "st")
            b.wait("scalar", "st")
            b.opi("scalar", lambda e: e.activation(out=stat[:, 7:8], in_=stat[:, 6:7],
                                                   func=AF.Exp, scale=c_actk[:, 2:3]), "ac")
            b.wait("scalar", "ac")
            b.opi("scalar", lambda e: e.nop(), "st")
            b.wait("vector", "st")
            b.wait("vector", "pe", ln_tr_pe[0])
            b.opi("vector", lambda e, xt=xt: e.tensor_scalar(
                out=scr2[:], in0=xt, scalar1=stat[:, 2:3], scalar2=stat[:, 7:8],
                op0=ALU.subtract, op1=ALU.mult), "dv")
            ln_done[ti] = b.cnt["dv"]
            dst = xcln if ti < NT else xln
            tc = ti % NT
            b.wait("tensor", "dv", ln_done[ti])
            for g4 in range(4):
                bk = g4 % 2
                b.wait("tensor", "dv2", tr_free[bk])
                for k4 in range(4):
                    fc = g4 * 4 + k4
                    m, par = fc // 2, fc % 2
                    srct = scr2[:, 256 * m + par: 256 * (m + 1): 2]
                    pv = bank(bk, BF16)[:, k4 * P:(k4 + 1) * P]
                    b.opi("tensor", lambda e, pv=pv, s_=srct: e.transpose(pv, s_, c_idb[:]), "pe")
                b.wait("vector", "pe")
                pvg = bank(bk, BF16)[:, 0:512].rearrange("p (a c) -> p a c", a=4)
                dstg = dst[:, g4 * 4:(g4 + 1) * 4, tc * P:(tc + 1) * P]
                tr_free[bk] = b.opi("vector", lambda e, pvg=pvg, dstg=dstg: e.tensor_copy(
                    out=dstg, in_=pvg), "dv2")
            ln_tr_pe[0] = b.cnt["pe"]
        xln_done = b.cnt["dv2"]
        dbg_dump("xln", xln)
        dbg_dump("xcln", xcln)

        b.wait("tensor", "dv2", xln_done)
        b.wait("tensor", "dv", wgpre_dv)
        for th in range(2):
            for kc in range(NF):
                b.opi("tensor", lambda e, kc=kc, th=th: e.matmul(
                    bank(2)[:, :], c_wg1[:, kc, :], xln[:, kc, th * 512:(th + 1) * 512],
                    start=(kc == 0), stop=(kc == NF - 1)), "pe")
            b.wait("vector", "pe")
            b.opi("vector", lambda e, th=th: e.tensor_copy(
                out=g1T[:, th * 512:(th + 1) * 512], in_=bank(2)[:, :]), "dv")
            b.wait("tensor", "dv")

        wload_state = {"idx": 0, "peuse": {0: 0, 1: 0},
                       "conv": {0: wgpre_dv, 1: wgpre_dv}}

        def wload(name, hg):
            st = wload_state
            wbuf = st["idx"] % 2
            st["idx"] += 1
            b.wait("sync", "dv", st["conv"][1 - wbuf])
            for par in range(2):
                src = AP(tensor=d[name].tensor, offset=par * D + hg * HG * HD,
                         ap=[[2 * D, 128], [256 * D, 8], [1, HG * HD]])
                dst = wstage[:, par, :].rearrange("p (a c) -> p a c", a=8)
                b.dma("sync", dst, src, "wl")
            b.wait("vector", "wl")
            b.wait("vector", "pe", st["peuse"][wbuf])
            for par in range(2):
                for m in range(8):
                    s_ = wstage[:, par, m * HG * HD:(m + 1) * HG * HD]
                    b.op("vector", lambda e, s_=s_, kc=2 * m + par, wbuf=wbuf: e.tensor_copy(
                        out=wbf[:, wbuf, kc, :], in_=s_))
            cv = b.opi("vector", lambda e: e.engine_nop(), "dv")
            st["conv"][wbuf] = cv
            return wbuf, cv

        def attn_pass(is_own):
            xl = xln if is_own else xcln
            key = "o" if is_own else "c"
            st = wload_state
            for hg in range(NHG):
                silu_bank = {0: 0, 1: 0}

                def proj_headmajor(wname, dstb):
                    wbuf, cv = wload(wname, hg)
                    b.wait("tensor", "dv", cv)
                    for hh in range(HG):
                        for th in range(2):
                            bk = (hh * 2 + th) % 2
                            b.wait("tensor", "ac", silu_bank[bk])
                            for kc in range(NF):
                                b.opi("tensor", lambda e, kc=kc, hh=hh, th=th, bk=bk, wbuf=wbuf: e.matmul(
                                    bank(bk)[:, :], wbf[:, wbuf, kc, hh * HD:(hh + 1) * HD],
                                    xl[:, kc, th * 512:(th + 1) * 512],
                                    start=(kc == 0), stop=(kc == NF - 1)), "pe")
                            b.wait("scalar", "pe")
                            silu_bank[bk] = b.opi("scalar", lambda e, hh=hh, th=th, bk=bk, dstb=dstb: e.activation(
                                out=dstb[:, hh, th * 512:(th + 1) * 512], in_=bank(bk)[:, :],
                                func=AF.Silu), "ac")
                    st["peuse"][wbuf] = b.cnt["pe"]
                    return b.cnt["ac"]

                if is_own:
                    q_ac = proj_headmajor("wq", qsb)
                    if hg == 0 and "qsnap1" in d_dbg:
                        b.wait("scalar", "ac", q_ac)
                        b.dma("scalar", d_dbg["qsnap1"][:, 0:1024],
                              qsb[:].rearrange("p a c -> p (a c)").bitcast(F32), "sl")
                    if hg == 0:
                        dbg_dump("qraw", qsb)
                        dbg_dump("kraw", ksb)
                k_ac = proj_headmajor("wk", ksb)
                if is_own and hg == 0 and "qsnap2" in d_dbg:
                    b.wait("scalar", "ac", k_ac)
                    b.dma("scalar", d_dbg["qsnap2"][:, 0:1024],
                          qsb[:].rearrange("p a c -> p (a c)").bitcast(F32), "sl")
                if hg == 0 and is_own:
                    dbg_dump("qpk", qsb)
                    dbg_dump("kpk", ksb)

                wbuf, cv = wload("wv", hg)
                b.wait("tensor", "dv", cv)
                b.wait("tensor", "ac", k_ac)
                v_free = {0: 0, 1: 0}
                for tc in range(NT):
                    bk = tc % 2
                    b.wait("tensor", "dv2", v_free[bk])
                    for kc in range(NF):
                        b.opi("tensor", lambda e, kc=kc, tc=tc, bk=bk, wbuf=wbuf: e.matmul(
                            bank(bk)[:, 0:HG * HD], xl[:, kc, tc * P:(tc + 1) * P],
                            wbf[:, wbuf, kc, :], start=(kc == 0), stop=(kc == NF - 1)), "pe")
                    b.wait("vector", "pe")
                    v_free[bk] = b.opi("vector", lambda e, tc=tc, bk=bk: e.tensor_copy(
                        out=vsb[:, tc, :], in_=bank(bk)[:, 0:HG * HD]), "dv2")
                st["peuse"][wbuf] = b.cnt["pe"]
                v_dv = b.cnt["dv2"]
                if hg == 0 and is_own:
                    dbg_dump("qpv", qsb)

                rot_bank = {4: 0, 5: 0}

                def rotate(buf3, key):
                    for hh in range(HG):
                        for th in range(2):
                            bk = 4 + th
                            sl = slice(th * 512, (th + 1) * 512)
                            b.wait("tensor", "dv", rot_bank[bk])
                            b.opi("tensor", lambda e, hh=hh, sl=sl, bk=bk, buf3=buf3: e.matmul(
                                bank(bk)[:, :], c_swap[:], buf3[:, hh, sl],
                                start=True, stop=True), "pe")
                            b.wait("vector", "pe")
                            b.op("vector", lambda e, hh=hh, sl=sl, buf3=buf3, key=key: e.tensor_tensor(
                                out=scr2[:, 0:512], in0=buf3[:, hh, sl], in1=c_cc[key][:, sl],
                                op=ALU.mult))
                            b.op("vector", lambda e, sl=sl, bk=bk, key=key: e.tensor_tensor(
                                out=scr2[:, 512:1024], in0=bank(bk)[:, :], in1=c_ss[key][:, sl],
                                op=ALU.mult))
                            rot_bank[bk] = b.opi("vector", lambda e, hh=hh, sl=sl, buf3=buf3: e.tensor_tensor(
                                out=buf3[:, hh, sl], in0=scr2[:, 0:512], in1=scr2[:, 512:1024],
                                op=ALU.add), "dv")

                if is_own:
                    rotate(qsb, key)
                    if hg == 0:
                        dbg_dump("qrot", qsb)
                rotate(ksb, key)
                rot_dv = b.cnt["dv"]
                if hg == 0 and is_own:
                    dbg_dump("q0", qsb)
                    dbg_dump("k0", ksb)
                    dbg_dump("v0", vsb)

                h0 = hg * HG
                Ssl32 = S32[:, h0:h0 + HG, :].rearrange("p a c -> p (a c)")
                Sslbf = Sbf[:, h0:h0 + HG, :].rearrange("p a c -> p (a c)")
                decq_sl = c_decq[:, h0:h0 + HG, :].rearrange("p a c -> p (a c)")
                deck_sl = c_deck[:, h0:h0 + HG, :].rearrange("p a c -> p (a c)")
                lamc_sl = c_lamc[:, h0:h0 + HG, :].rearrange("p a c -> p (a c)")
                dTsl = c_dT[:, h0:h0 + HG, :].rearrange("p a c -> p (a c)")
                b.wait("tensor", "dv", rot_dv)
                b.wait("tensor", "dv2", v_dv)
                sc_free = 0
                su_done = b.cnt["dv"]
                o_free = 0
                mm1_done = 0
                for c in range(NT):
                    csl = slice(c * CH, (c + 1) * CH)
                    if is_own:
                        b.wait("vector", "pe", mm1_done)
                        b.opi("vector", lambda e, csl=csl, decq_sl=decq_sl: e.tensor_tensor(
                            out=scrq[:].rearrange("p (a c) -> p a c", a=HG),
                            in0=qsb[:, :, csl],
                            in1=decq_sl.rearrange("p (a c) -> p a c", a=HG), op=ALU.mult), "dv")
                        qd_dv = b.cnt["dv"]
                        b.wait("tensor", "dv", qd_dv)
                        b.wait("tensor", "dv2", o_free)
                        for hh in range(HG):
                            b.opi("tensor", lambda e, hh=hh, h0=h0: e.matmul(
                                bank(hh)[:, 0:CH],
                                Sbf[:, h0 + hh, :],
                                scrq[:, hh * CH:(hh + 1) * CH],
                                start=True, stop=False), "pe")
                        mm1_done = b.cnt["pe"]
                        b.wait("tensor", "dv", sc_free)
                        for hh in range(HG):
                            b.opi("tensor", lambda e, hh=hh, csl=csl: e.matmul(
                                bank(2)[:, hh * CH:(hh + 1) * CH],
                                ksb[:, hh, csl], qsb[:, hh, csl], start=True, stop=True), "pe")
                        b.wait("vector", "pe")
                        b.opi("vector", lambda e, dTsl=dTsl: e.tensor_tensor(
                            out=scrs[:], in0=bank(2)[:, 0:HG * CH], in1=dTsl, op=ALU.mult), "dv")
                        sc_free = b.cnt["dv"]
                        b.wait("tensor", "dv", sc_free)
                        for hh in range(HG):
                            b.opi("tensor", lambda e, hh=hh, c=c: e.matmul(
                                bank(hh)[:, 0:CH],
                                vsb[:, c, hh * HD:(hh + 1) * HD], scrs[:, hh * CH:(hh + 1) * CH],
                                start=False, stop=True), "pe")
                        b.wait("vector", "pe")
                        o_free = b.opi("vector", lambda e, c=c, h0=h0: e.tensor_copy(
                            out=o_nrm[:, h0, c * CH:(c + 1) * CH], in_=bank(0)[:, 0:CH]), "dv2")
                        o_free = b.opi("vector", lambda e, c=c, h0=h0: e.tensor_copy(
                            out=o_nrm[:, h0 + 1, c * CH:(c + 1) * CH], in_=bank(1)[:, 0:CH]), "dv2")
                    b.wait("tensor", "dv", su_done)
                    for hh in range(HG):
                        b.opi("tensor", lambda e, hh=hh, csl=csl: e.transpose(
                            bank(3, BF16)[:, hh * CH:(hh + 1) * CH],
                            ksb[:, hh, csl], c_idb[:]), "pe")
                    b.wait("vector", "pe")
                    b.opi("vector", lambda e, deck_sl=deck_sl: e.tensor_tensor(
                        out=scrk[:], in0=bank(3, BF16)[:, 0:HG * CH], in1=deck_sl, op=ALU.mult), "dv")
                    kd_dv = b.cnt["dv"]
                    b.wait("tensor", "dv", kd_dv)
                    for hh in range(HG):
                        b.opi("tensor", lambda e, hh=hh, c=c: e.matmul(
                            bank(4)[:, hh * CH:(hh + 1) * CH],
                            scrk[:, hh * CH:(hh + 1) * CH], vsb[:, c, hh * HD:(hh + 1) * HD],
                            start=True, stop=True), "pe")
                    b.wait("vector", "pe")
                    b.op("vector", lambda e, Ssl32=Ssl32, lamc_sl=lamc_sl: e.tensor_tensor(
                        out=Ssl32, in0=Ssl32, in1=lamc_sl, op=ALU.mult))
                    b.op("vector", lambda e, Ssl32=Ssl32: e.tensor_tensor(
                        out=Ssl32, in0=Ssl32, in1=bank(4)[:, 0:HG * CH], op=ALU.add))
                    b.opi("vector", lambda e, Ssl32=Ssl32, Sslbf=Sslbf: e.tensor_copy(
                        out=Sslbf, in_=Ssl32), "dv")
                    su_done = b.cnt["dv"]
                b.wait("tensor", "dv", su_done)

        attn_pass(False)
        b.op("vector", lambda e: e.tensor_scalar(
            out=S32[:].rearrange("p a c -> p (a c)"), in0=S32[:].rearrange("p a c -> p (a c)"),
            scalar1=c_mask[:, 0:1], scalar2=None, op0=ALU.mult))
        b.opi("vector", lambda e: e.tensor_copy(
            out=Sbf[:].rearrange("p a c -> p (a c)"), in_=S32[:].rearrange("p a c -> p (a c)")), "dv")
        attn_pass(True)
        dbg_dump("o", o_nrm)
        dbg_dump("g1T", g1T)

        b.wait("tensor", "dv")
        b.wait("tensor", "dv2")
        p4_pe_gate = 0
        for h in range(HEADS):
            b.wait("tensor", "dv", p4_pe_gate)
            for th in range(2):
                b.opi("tensor", lambda e, h=h, th=th: e.matmul(
                    bank(th)[:, :], c_wg2[:, h * HD:(h + 1) * HD], g1T[:, th * 512:(th + 1) * 512],
                    start=True, stop=True), "pe")
            b.wait("scalar", "pe")
            for th in range(2):
                b.opi("scalar", lambda e, th=th: e.activation(
                    out=sgt[:, 0, th * 512:(th + 1) * 512], in_=bank(th)[:, :], func=AF.Sigmoid), "ac")
            b.wait("scalar", "ac")
            b.opi("scalar", lambda e: e.nop(), "st")
            b.wait("vector", "st")
            b.opi("vector", lambda e, h=h: e.tensor_tensor(
                out=sgt[:, 1, :], in0=o_nrm[:, h, :], in1=sgt[:, 0, :], op=ALU.mult), "st")
            b.wait("vector", "st")
            b.opi("vector", lambda e: e.tensor_tensor(
                out=scr2[:, 0:T], in0=sgt[:, 1, :], in1=sgt[:, 1, :], op=ALU.mult), "dv")
            sq_dv = b.cnt["dv"]
            p4_pe_gate = sq_dv
            b.wait("tensor", "dv", sq_dv)
            b.wait("tensor", "ac")
            for th in range(2):
                b.opi("tensor", lambda e, th=th: e.matmul(
                    PS[0:1, 2 + th, :], c_1rms[:], scr2[:, th * 512:(th + 1) * 512],
                    start=True, stop=True), "pe")
            b.wait("scalar", "pe")
            b.op("scalar", lambda e: e.activation(out=rms1[0:1, 0:512], in_=PS[0:1, 2, :],
                                                  func=AF.Ln, bias=c_actk[0:1, 1:2]))
            b.opi("scalar", lambda e: e.activation(out=rms1[0:1, 512:1024], in_=PS[0:1, 3, :],
                                                   func=AF.Ln, bias=c_actk[0:1, 1:2]), "st")
            b.wait("scalar", "st")
            b.opi("scalar", lambda e: e.activation(out=rms1[0:1, 0:1024], in_=rms1[0:1, 0:1024],
                                                   func=AF.Exp, scale=c_actk[0:1, 2:3]), "ac")
            b.wait("scalar", "ac")
            b.opi("scalar", lambda e: e.nop(), "st")
            b.wait("tensor", "st")
            for th in range(2):
                b.opi("tensor", lambda e, th=th: e.matmul(
                    bank(4 + th)[:, :], c_1row[0:1, :], rms1[0:1, th * 512:(th + 1) * 512],
                    start=True, stop=True), "pe")
            b.wait("vector", "pe")
            for th in range(2):
                b.opi("vector", lambda e, th=th: e.tensor_tensor(
                    out=scr2[:, T + th * 512:T + (th + 1) * 512],
                    in0=sgt[:, 1, th * 512:(th + 1) * 512],
                    in1=bank(4 + th)[:, :], op=ALU.mult), "st")
            b.wait("vector", "st")
            b.opi("vector", lambda e, h=h: e.tensor_scalar(
                out=o_nrm[:, h, :], in0=scr2[:, T:2 * T], scalar1=c_nw[:, h:h + 1], scalar2=None,
                op0=ALU.mult), "dv")
            p4_pe_gate = b.cnt["dv"]

        dbg_dump("normed", o_nrm)

        b.barrier()
        xre_free = [0]
        tr_free = {0: 0, 1: 0}
        for tc in range(NT):
            b.wait("scalar", "pe2", xre_free[0])
            b.dma("scalar", xstg2[:, 0, :], d["xin"][(NT + tc) * P:(NT + tc + 1) * P, :], "sl")
            b.wait("tensor", "sl")
            for g4 in range(4):
                bk = g4 % 2
                b.wait("tensor", "dv2", tr_free[bk])
                for k4 in range(4):
                    fc = g4 * 4 + k4
                    m, par = fc // 2, fc % 2
                    srct = xstg2[:, 0, 256 * m + par: 256 * (m + 1): 2]
                    b.opi("tensor", lambda e, bk=bk, k4=k4, s_=srct: e.transpose(
                        bank(bk)[:, k4 * P:(k4 + 1) * P], s_, c_idf[:]), "pe")
                b.wait("vector", "pe")
                tr_free[bk] = b.opi("vector", lambda e, bk=bk, g4=g4, tc=tc: e.tensor_copy(
                    out=xT[:, g4 * 4:(g4 + 1) * 4, tc * P:(tc + 1) * P],
                    in_=bank(bk)[:, :].rearrange("p (a c) -> p a c", a=4)), "dv2")
            xre_free[0] = b.opi("tensor", lambda e: e.nop(), "pe2")
        xT_dv = b.cnt["dv2"]

        wo_state = {"conv": {0: 0, 1: 0}, "peuse": {0: 0, 1: 0}}
        res_free = {6: 0, 7: 0}
        b.wait("tensor", "dv2", xT_dv)
        for m in range(8):
            buf = m % 2
            b.wait("sync", "dv", wo_state["conv"][buf])
            src = AP(tensor=d["wo"].tensor, offset=256 * m,
                     ap=[[D, 128], [P * D, NF], [1, 256]])
            b.dma("sync", wostg[:, buf, :, :], src, "wl")
            b.wait("vector", "wl")
            b.wait("vector", "pe", wo_state["peuse"][buf])
            cv = b.opi("vector", lambda e, buf=buf: e.tensor_copy(
                out=wobf[:, buf, :, :].rearrange("p a c -> p (a c)"),
                in_=wostg[:, buf, :, :].rearrange("p a c -> p (a c)")), "dv")
            wo_state["conv"][buf] = cv
            b.wait("tensor", "dv", cv)
            for par in range(2):
                fc = 2 * m + par
                for th in range(2):
                    bk = 6 + th
                    b.wait("tensor", "dv", res_free[bk])
                    for ec in range(NF):
                        b.opi("tensor", lambda e, buf=buf, ec=ec, par=par, th=th, bk=bk: e.matmul(
                            bank(bk)[:, :], wobf[:, buf, ec, par:256:2],
                            o_nrm[:, ec, th * 512:(th + 1) * 512],
                            start=(ec == 0), stop=(ec == NF - 1)), "pe")
                    b.wait("vector", "pe")
                    res_free[bk] = b.opi("vector", lambda e, fc=fc, th=th, bk=bk: e.tensor_tensor(
                        out=xT[:, fc, th * 512:(th + 1) * 512],
                        in0=xT[:, fc, th * 512:(th + 1) * 512], in1=bank(bk)[:, :], op=ALU.add), "dv")
            wo_state["peuse"][buf] = b.cnt["pe"]
        b.wait("tensor", "dv", max(res_free.values()))

        dbg_dump("xa", xT[:, 0:8, :])
        dbg_dump("xa2", xT[:, 8:16, :])

        lnmm = 0
        for fc in range(NF):
            b.wait("vector", "pe", lnmm)
            b.op("vector", lambda e, fc=fc: e.tensor_copy(out=xbs[:], in_=xT[:, fc, :]))
            b.opi("vector", lambda e: e.tensor_tensor(out=sqs[:], in0=xbs[:], in1=xbs[:],
                                                      op=ALU.mult), "dv")
            b.wait("tensor", "dv")
            for th in range(2):
                b.opi("tensor", lambda e, fc=fc, th=th: e.matmul(
                    PS[0:1, th, :], c_1ln[:], xbs[:, th * 512:(th + 1) * 512],
                    start=(fc == 0), stop=(fc == NF - 1)), "pe")
                b.opi("tensor", lambda e, fc=fc, th=th: e.matmul(
                    PS[0:1, 2 + th, :], c_1ln[:], sqs[:, th * 512:(th + 1) * 512],
                    start=(fc == 0), stop=(fc == NF - 1)), "pe")
            lnmm = b.cnt["pe"]
        b.wait("vector", "pe", lnmm)
        b.op("vector", lambda e: e.tensor_copy(out=lnsc[0:1, 0:512], in_=PS[0:1, 0, :]))
        b.opi("vector", lambda e: e.tensor_copy(out=lnsc[0:1, 512:1024], in_=PS[0:1, 1, :]), "st")
        b.wait("vector", "st")
        b.op("vector", lambda e: e.tensor_tensor(out=rms2[0:1, 0:512], in0=lnsc[0:1, 0:512],
                                                 in1=lnsc[0:1, 0:512], op=ALU.mult))
        b.opi("vector", lambda e: e.tensor_tensor(out=rms2[0:1, 512:1024], in0=lnsc[0:1, 512:1024],
                                                  in1=lnsc[0:1, 512:1024], op=ALU.mult), "st")
        b.wait("vector", "st")
        b.op("vector", lambda e: e.tensor_tensor(out=rms2[0:1, 0:512], in0=PS[0:1, 2, :],
                                                 in1=rms2[0:1, 0:512], op=ALU.subtract))
        b.opi("vector", lambda e: e.tensor_tensor(out=rms2[0:1, 512:1024], in0=PS[0:1, 3, :],
                                                  in1=rms2[0:1, 512:1024], op=ALU.subtract), "dv")
        b.wait("vector", "dv")
        b.opi("vector", lambda e: e.engine_nop(), "st")
        b.wait("scalar", "st")
        b.opi("scalar", lambda e: e.activation(out=rms2[0:1, 0:1024], in_=rms2[0:1, 0:1024],
                                               func=AF.Ln, bias=c_actk[0:1, 0:1]), "st")
        b.wait("scalar", "st")
        b.opi("scalar", lambda e: e.activation(out=rms2[0:1, 0:1024], in_=rms2[0:1, 0:1024],
                                               func=AF.Exp, scale=c_actk[0:1, 2:3]), "ac")
        b.wait("scalar", "ac")
        b.opi("scalar", lambda e: e.nop(), "st")
        b.wait("vector", "st")
        b.opi("vector", lambda e: e.scalar_tensor_tensor(
            out=lnsc[0:1, 0:1024], in0=lnsc[0:1, 0:1024], scalar=-1.0, in1=rms2[0:1, 0:1024],
            op0=ALU.mult, op1=ALU.mult), "dv")
        b.wait("vector", "dv")
        b.opi("vector", lambda e: e.engine_nop(), "st")
        b.wait("tensor", "st")
        b.wait("tensor", "dv")
        for th in range(2):
            b.opi("tensor", lambda e, th=th: e.matmul(
                bank(4 + th)[:, :], c_1row[0:1, :], rms2[0:1, th * 512:(th + 1) * 512],
                start=True, stop=True), "pe")
            b.opi("tensor", lambda e, th=th: e.matmul(
                bank(6 + th)[:, :], c_1row[0:1, :], lnsc[0:1, th * 512:(th + 1) * 512],
                start=True, stop=True), "pe")
        b.wait("vector", "pe")
        for fc in range(NF):
            for th in range(2):
                b.opi("vector", lambda e, fc=fc, th=th: e.tensor_tensor(
                    out=rms2[:, th * 512:(th + 1) * 512],
                    in0=xT[:, fc, th * 512:(th + 1) * 512], in1=bank(4 + th)[:, :], op=ALU.mult), "st")
                b.wait("vector", "st")
                b.opi("vector", lambda e, fc=fc, th=th: e.tensor_tensor(
                    out=xln2[:, fc, th * 512:(th + 1) * 512],
                    in0=rms2[:, th * 512:(th + 1) * 512], in1=bank(6 + th)[:, :], op=ALU.add), "dv")
        xln2_dv = b.cnt["dv"]
        dbg_dump("xln2", xln2)
        b.barrier()

        scb_war = {"gu": 0, "dn": 0}

        def dec_batch(v16, shlo, dst_lo, dst_hi, scf, scb, nj, key, sem=None):
            b.op("vector", lambda e, scf=scf, scb=scb: e.tensor_scalar(
                out=scb, in0=scf, scalar1=float(2.0 ** 126), scalar2=None, op0=ALU.mult))
            last = None
            for k, (sh, dst) in enumerate(((shlo, dst_lo), (shlo + 4, dst_hi))):
                t = scrC[:, 0:nj, :]
                if sh == 0:
                    b.op("vector", lambda e, v16=v16, t=t: e.tensor_scalar(
                        out=t, in0=v16, scalar1=0x000F, scalar2=None, op0=ALU.bitwise_and))
                else:
                    b.op("vector", lambda e, v16=v16, t=t, sh=sh: e.tensor_scalar(
                        out=t, in0=v16, scalar1=sh, scalar2=0x000F,
                        op0=ALU.logical_shift_right, op1=ALU.bitwise_and))
                b.op("vector", lambda e, t=t: e.tensor_scalar(
                    out=t, in0=t, scalar1=0x1040, scalar2=None, op0=ALU.mult))
                b.op("vector", lambda e, t=t: e.tensor_scalar(
                    out=t, in0=t, scalar1=0x81C0, scalar2=None, op0=ALU.bitwise_and))
                fn = lambda e, t=t, dst=dst, scb=scb: e.tensor_tensor(
                    out=dst, in0=t.bitcast(BF16), in1=scb, op=ALU.mult)
                if k == 1 and sem is not None:
                    last = b.opi("vector", fn, sem)
                else:
                    b.op("vector", fn)
            scb_war[key] = last if last is not None else b.cnt["dv"]
            return last

        pk_done = {}

        def pk_dma(mat, w):
            arr = gpk if mat == "g" else upk
            src = AP(tensor=d[mat + "_pk"].tensor, offset=512 * w,
                     ap=[[MLP, 128], [128 * MLP, 8], [1, 512]])
            pk_done[(mat, w)] = b.dma("sync", arr[:, w % 2, :, :], src, "wl")

        dpk_done = {}

        def dpk_dma(s, q):
            src = AP(tensor=d["d_pk"].tensor, offset=(4 * s) * 128 * D + 512 * q,
                     ap=[[D, 128], [128 * D, 4], [1, 512]])
            dpk_done[(s, q)] = b.dma("gpsimd", dpk[:, :, 512 * q:512 * q + 512], src, "gl")

        b.wait("sync", "dv", max(res_free.values()))
        pk_dma("g", 0)
        pk_dma("u", 0)
        decuse = {"g": 0, "u": 0}
        pkuse = {("g", 0): 0, ("g", 1): 0, ("u", 0): 0, ("u", 1): 0}
        gu_bank_free = {0: 0, 1: 0, 2: 0, 3: 0}
        dn_bank_free = {4: 0, 5: 0}
        wd_use = {0: 0, 1: 0}
        inter_tt = [0]
        dscuse = [0]

        for s_i in range(NS):
            if s_i == 0:
                b.wait("gpsimd", "dv", xln2_dv)
                for q in range(4):
                    dpk_dma(0, q)
            for ww in range(2):
                w = 2 * s_i + ww
                wbuf = w % 2
                if w + 1 < NWIN:
                    b.wait("sync", "dv", pkuse[("g", (w + 1) % 2)])
                    pk_dma("g", w + 1)
                    b.wait("sync", "dv", pkuse[("u", (w + 1) % 2)])
                    pk_dma("u", w + 1)
                for npar in range(2):
                    for mat in ("g", "u"):
                        dec = gdec if mat == "g" else udec
                        arr = gpk if mat == "g" else upk
                        nm = f"{mat}_sc_{'e' if npar == 0 else 'o'}"
                        b.wait("sync", "dv", scb_war["gu"])
                        scd = b.dma("sync", gsc[:],
                                    AP(tensor=d[nm].tensor, offset=256 * w,
                                       ap=[[MLP // 2, 128], [128 * (MLP // 2), 8], [1, 256]]),
                                    "wl")
                        b.wait("vector", "wl", scd)
                        b.wait("vector", "wl", pk_done[(mat, w)])
                        b.wait("vector", "pe", decuse[mat])
                        v16 = arr[:, wbuf, :, :].bitcast(U16)
                        dv_dec = dec_batch(v16, 8 * npar, dec[:, 0, :, :], dec[:, 1, :, :],
                                           gsc[:, :, :], gscb[:, :, :], 8, "gu", sem="dv")
                        pkuse[(mat, wbuf)] = dv_dec
                        b.wait("tensor", "dv", dv_dec)
                        if s_i == 0 and ww == 0 and mat == "g" and npar == 0:
                            b.wait("tensor", "dv", xln2_dv)
                        if mat == "g":
                            b.wait("scalar", "dv", inter_tt[0])
                        for ci in range(2):
                            islot = 4 * npar + 2 * ww + ci
                            for th in range(2):
                                bk = (ci * 2 + th) % 4
                                b.wait("tensor", "ac", gu_bank_free[bk])
                                for kc in range(NF):
                                    b.opi("tensor", lambda e, kc=kc, ci=ci, th=th, bk=bk, dec=dec: e.matmul(
                                        bank(bk)[:, :], dec[:, kc % 2, kc // 2, ci * 128:(ci + 1) * 128],
                                        xln2[:, kc, th * 512:(th + 1) * 512],
                                        start=(kc == 0), stop=(kc == NF - 1)), "pe")
                                mme = b.cnt["pe"]
                                sl = ci * 2 + th
                                b.wait("scalar", "pe", mme)
                                if mat == "g":
                                    gu_bank_free[bk] = b.opi("scalar", lambda e, bk=bk, sl=sl: e.activation(
                                        out=gl[:, sl * 512:(sl + 1) * 512], in_=bank(bk)[:, :],
                                        func=AF.Gelu_apprx_tanh), "ac")
                                else:
                                    gu_bank_free[bk] = b.opi("scalar", lambda e, bk=bk, sl=sl: e.activation(
                                        out=up_sb[:, sl * 512:(sl + 1) * 512], in_=bank(bk)[:, :],
                                        func=AF.Copy), "ac")
                        decuse[mat] = b.cnt["pe"]
                        if mat == "u":
                            base = 4 * npar + 2 * ww
                            b.wait("vector", "ac")
                            inter_tt[0] = b.opi("vector", lambda e, base=base: e.tensor_tensor(
                                out=inter[:, base:base + 2, :],
                                in0=gl[:].rearrange("p (a c) -> p a c", a=2),
                                in1=up_sb[:].rearrange("p (a c) -> p a c", a=2),
                                op=ALU.mult), "dv")
            inter_done = b.cnt["dv"]
            if s_i == 0:
                dbg_dump("inter0", inter)
            for q in range(4):
                for hf in range(2):
                    blk = q * 2 + hf
                    wdb = wddec if blk % 2 == 0 else wddec2
                    nm = "d_sc_a" if hf == 0 else "d_sc_b"
                    src = AP(tensor=d[nm].tensor, offset=(512 * s_i) * 1024 + 256 * q,
                             ap=[[1024, 128], [128 * 1024, 4], [1, 256]])
                    b.wait("sync", "dv", scb_war["dn"])
                    dsd = b.dma("sync", dsc[:], src, "wl")
                    b.wait("vector", "wl", dsd)
                    b.wait("vector", "gl", dpk_done[(s_i, q)])
                    b.wait("vector", "pe", wd_use[blk % 2])
                    v16 = dpk[:, :, 512 * q:512 * q + 512].bitcast(U16)
                    wd_dv = dec_batch(v16, 8 * hf, wdb[:, 0:4, :], wdb[:, 4:8, :],
                                      dsc[:, :, :], dscb[:, :, :], 4, "dn", sem="dv")
                    if hf == 1 and s_i + 1 < NS:
                        b.wait("gpsimd", "dv", wd_dv)
                        dpk_dma(s_i + 1, q)
                    b.wait("tensor", "dv", wd_dv)
                    if q == 0 and hf == 0:
                        b.wait("tensor", "dv", inter_done)
                    for hci in range(2):
                        hc = hf * 8 + 2 * q + hci
                        for th in range(2):
                            bk = 4 + ((hci * 2 + th) % 2)
                            b.wait("tensor", "dv", dn_bank_free[bk])
                            for sl in range(8):
                                jc, nib = sl % 4, sl // 4
                                islot = 4 * nib + jc
                                b.opi("tensor", lambda e, sl=sl, hci=hci, th=th, bk=bk, islot=islot, wdb=wdb: e.matmul(
                                    bank(bk)[:, :], wdb[:, sl, hci * 128:(hci + 1) * 128],
                                    inter[:, islot, th * 512:(th + 1) * 512],
                                    start=(sl == 0), stop=(sl == 7)), "pe")
                            b.wait("vector", "pe")
                            dn_bank_free[bk] = b.opi("vector", lambda e, hc=hc, th=th, bk=bk: e.tensor_tensor(
                                out=xT[:, hc, th * 512:(th + 1) * 512],
                                in0=xT[:, hc, th * 512:(th + 1) * 512], in1=bank(bk)[:, :],
                                op=ALU.add), "dv")
                    wd_use[blk % 2] = b.cnt["pe"]

        b.wait("tensor", "dv", max(dn_bank_free.values()))
        b.barrier()

        st_done = {0: 0, 1: 0}
        tr_free = {0: 0, 1: 0}
        for tc in range(NT):
            buf = tc % 2
            b.wait("vector", "sl", st_done[buf])
            evacs = []
            for g4 in range(4):
                bk = g4 % 2
                b.wait("tensor", "dv2", tr_free[bk])
                for k4 in range(4):
                    fc = g4 * 4 + k4
                    b.opi("tensor", lambda e, bk=bk, k4=k4, fc=fc, tc=tc: e.transpose(
                        bank(bk)[:, k4 * P:(k4 + 1) * P], xT[:, fc, tc * P:(tc + 1) * P],
                        c_idf[:]), "pe")
                b.wait("vector", "pe")
                last = 0
                for k4 in range(4):
                    fc = g4 * 4 + k4
                    m, par = fc // 2, fc % 2
                    dstv = ostg[:, buf, 256 * m + par: 256 * (m + 1): 2]
                    last = b.opi("vector", lambda e, bk=bk, k4=k4, dstv=dstv: e.tensor_copy(
                        out=dstv, in_=bank(bk)[:, k4 * P:(k4 + 1) * P]), "dv2")
                tr_free[bk] = last
            b.wait("scalar", "dv2")
            st_done[buf] = b.dma("scalar", d_out[tc * P:(tc + 1) * P, :], ostg[:, buf, :], "sl")
        b.wait("scalar", "sl")

        import os
        if os.environ.get("K_PRINT_CNT"):
            print("SEM COUNTS:", {k: v for k, v in b.cnt.items()})
            print("Q LENGTHS:", {k: len(v) for k, v in b.q.items()})
        @block.sync
        def _(eng):
            for f in b.q["sync"]:
                f(eng)

        @block.vector
        def _(eng):
            for f in b.q["vector"]:
                f(eng)

        @block.scalar
        def _(eng):
            for f in b.q["scalar"]:
                f(eng)

        @block.tensor
        def _(eng):
            for f in b.q["tensor"]:
                f(eng)

        @block.gpsimd
        def _(eng):
            for f in b.q["gpsimd"]:
                f(eng)

    return nc


_NC_CACHE = [None]


def kernel(**inputs):
    in_maps = _host_prep(inputs)
    if _NC_CACHE[0] is None:
        _NC_CACHE[0] = build_nc()
    nc = _NC_CACHE[0]
    res = run_bass_kernel_spmd(nc, in_maps, core_ids=list(range(8)))
    out = np.empty((B, S, D), np.float32)
    for core in range(8):
        bb, h = core // 2, core % 2
        out[bb, h * T:(h + 1) * T] = res.results[core]["out"]
    return out


# version 10
# speedup vs baseline: 1.1363x; 1.0080x over previous
import numpy as np
import ml_dtypes
import concourse.bass as bass
import concourse.mybir as mybir
from concourse.bass import AP
from concourse.bass_utils import run_bass_kernel_spmd
from concourse.alu_op_type import AluOpType as ALU

AF = mybir.ActivationFunctionType
F32 = mybir.dt.float32
BF16 = mybir.dt.bfloat16
U8 = mybir.dt.uint8
U16 = mybir.dt.uint16
BF = ml_dtypes.bfloat16

P = 128
B, S, D = 4, 2048, 2048
HEADS, HD, CH = 16, 128, 128
MLP = 16384
T = 1024
NT = T // P
NF = D // P
HG = 2
NHG = HEADS // HG
EPS_LN = 1e-5
EPS_RMS = 1e-6
NWIN = 32
NS = 16
CBLOB = 26432
DBG = []


def _fperm():
    fp = np.empty(D, np.int64)
    for m in range(8):
        fp[256 * m:256 * m + 128] = 256 * m + 2 * np.arange(128)
        fp[256 * m + 128:256 * m + 256] = 256 * m + 2 * np.arange(128) + 1
    return fp


def _host_prep(inputs):
    x = np.asarray(inputs["L0_attn_x"])
    nw = np.asarray(inputs["L0_attn_norm_weight"])

    def pk_bytes(a):
        a = np.asarray(a)
        return np.ascontiguousarray(a.view(np.uint8).reshape(a.shape[0], a.shape[1], 4)[:, :, 0])

    g_pk = np.ascontiguousarray(pk_bytes(inputs["L0_ffn_gate_W_packed"]).T)
    u_pk = np.ascontiguousarray(pk_bytes(inputs["L0_ffn_up_W_packed"]).T)
    d_pk_t = pk_bytes(inputs["L0_ffn_down_W_packed"]).T

    fp = _fperm()
    col_order = np.empty(D, np.int64)
    col_order[0::2] = fp[:1024]
    col_order[1::2] = fp[1024:]
    d_pk = np.ascontiguousarray(d_pk_t[:, col_order])

    g_sc_t = np.asarray(inputs["L0_ffn_gate_scales"]).T
    u_sc_t = np.asarray(inputs["L0_ffn_up_scales"]).T
    d_sc_t = np.asarray(inputs["L0_ffn_down_scales"]).T
    rep16 = lambda a: np.ascontiguousarray(np.repeat(a, 16, axis=0))
    g_sc_e = rep16(g_sc_t[:, 0::2]); g_sc_o = rep16(g_sc_t[:, 1::2])
    u_sc_e = rep16(u_sc_t[:, 0::2]); u_sc_o = rep16(u_sc_t[:, 1::2])
    d_sc_a = rep16(d_sc_t[:, fp[:1024]])
    d_sc_b = rep16(d_sc_t[:, fp[1024:]])

    hh = np.arange(HEADS, dtype=np.float64)
    log_lam = -(2.0 ** (-8.0 * (hh + 1.0) / HEADS))
    i = np.arange(CH, dtype=np.float64)
    diff = i[:, None] - i[None, :]
    dT = np.zeros((P, HEADS * CH), np.float64)
    decq = np.zeros((P, HEADS * CH), np.float64)
    deck = np.zeros((P, HEADS * CH), np.float64)
    lamc = np.zeros((P, HEADS * CH), np.float64)
    for h in range(HEADS):
        Dh = np.where(diff >= 0, np.exp(log_lam[h] * diff), 0.0)
        dT[:, h * CH:(h + 1) * CH] = Dh.T
        decq[:, h * CH:(h + 1) * CH] = np.exp(log_lam[h] * (i + 1.0))[None, :]
        deck[:, h * CH:(h + 1) * CH] = np.exp(log_lam[h] * (CH - 1.0 - i))[:, None]
        lamc[:, h * CH:(h + 1) * CH] = np.exp(log_lam[h] * CH)

    half = HD // 2
    theta = 10000.0 ** (-2.0 * np.arange(half, dtype=np.float64) / HD)

    def ccss(pos0):
        ang = np.arange(pos0, pos0 + T, dtype=np.float64)[None, :] * theta[:, None]
        c, s = np.cos(ang), np.sin(ang)
        return (np.concatenate([c, c], 0).astype(BF),
                np.concatenate([-s, s], 0).astype(BF))

    cc0, ss0 = ccss(0)
    cc1, ss1 = ccss(T)

    swap = np.zeros((P, P), BF)
    for j in range(64):
        swap[j + 64, j] = 1
        swap[j, j + 64] = 1

    shared = dict(
        wq=np.asarray(inputs["L0_attn_Wq"]), wk=np.asarray(inputs["L0_attn_Wk"]),
        wv=np.asarray(inputs["L0_attn_Wv"]), wo=np.asarray(inputs["L0_attn_Wo"]),
        wg1=np.asarray(inputs["L0_attn_Wg1"]), wg2=np.asarray(inputs["L0_attn_Wg2"]),
        g_pk=g_pk, u_pk=u_pk, d_pk=d_pk,
        g_sc_e=g_sc_e, g_sc_o=g_sc_o, u_sc_e=u_sc_e, u_sc_o=u_sc_o,
        d_sc_a=d_sc_a, d_sc_b=d_sc_b,
    )

    nw_t = np.ascontiguousarray(nw.reshape(HEADS, HD).T).astype(np.float32)
    actk = np.tile(np.array([[EPS_LN, EPS_RMS, -0.5, 0.0, float(2.0 ** 126)]], np.float32), (P, 1))

    def blob_for(h):
        bl = np.zeros((P, CBLOB), np.uint8)

        def put2d(off, arr):
            a = np.ascontiguousarray(arr)
            bl[:, off:off + a.shape[1] * a.dtype.itemsize] = a.view(np.uint8)

        put2d(0, dT.astype(BF)); put2d(4096, decq.astype(BF))
        put2d(8192, deck.astype(BF)); put2d(12288, lamc.astype(BF))
        put2d(16384, cc0); put2d(18432, ss0)
        cco, sso = (cc1, ss1) if h == 1 else (cc0, ss0)
        put2d(20480, cco); put2d(22528, sso)
        put2d(24576, swap); put2d(24832, np.eye(P, dtype=BF))
        put2d(25088, np.eye(P, dtype=np.float32))
        put2d(25600, np.full((P, 1), 1.0 / D, BF))
        put2d(25664, np.full((P, 1), 1.0 / HD, BF))
        bl[0, 25728:25728 + 512] = np.ones(P, np.float32).view(np.uint8)
        put2d(26240, nw_t)
        put2d(26304, actk)
        bl[:, 26368:26372] = np.full((P, 1), float(h), np.float32).view(np.uint8)
        return bl

    blobs = [blob_for(0), blob_for(1)]
    in_maps = []
    for core in range(8):
        bb, h = core // 2, core % 2
        own = np.ascontiguousarray(x[bb, h * T:(h + 1) * T])
        ctx = np.ascontiguousarray(x[bb, 0:T]) if h == 1 else own
        m = dict(shared)
        m["xin"] = np.ascontiguousarray(np.concatenate([ctx, own], 0))
        m["cblob"] = blobs[h]
        in_maps.append(m)
    return in_maps


class Bld:
    ENGS = ("sync", "vector", "scalar", "tensor", "gpsimd")

    def __init__(self, nc):
        self.nc = nc
        self.q = {e: [] for e in self.ENGS}
        self.sem = {}
        self.cnt = {}

    def new_sem(self, name, handle):
        self.sem[name] = handle
        self.cnt[name] = 0

    def op(self, eng, fn):
        self.q[eng].append(fn)

    def opi(self, eng, fn, sem, k=1):
        h = self.sem[sem]
        self.q[eng].append(lambda e, fn=fn, h=h, k=k: fn(e).then_inc(h, k))
        self.cnt[sem] += k
        return self.cnt[sem]

    def dma(self, eng, out, in_, sem):
        self.wait(eng, sem)
        return self.opi(eng, lambda e, o=out, i=in_: e.dma_start(out=o, in_=i), sem, 16)

    def wait(self, eng, sem, v=None):
        hv = self.cnt[sem] if v is None else v
        if hv > 0:
            h = self.sem[sem]
            self.q[eng].append(lambda e, h=h, hv=hv: e.wait_ge(h, hv))

    def barrier(self):
        h = self.sem["bar"]
        for e in self.ENGS:
            if e == "tensor":
                self.q[e].append(lambda en, h=h: en.nop().then_inc(h, 1))
            elif e in ("vector", "gpsimd"):
                self.q[e].append(lambda en, h=h: en.engine_nop().then_inc(h, 1))
            else:
                self.q[e].append(lambda en, h=h: en.nop().then_inc(h, 1))
        self.cnt["bar"] += 5
        tgt = self.cnt["bar"]
        for e in self.ENGS:
            self.q[e].append(lambda en, h=h, v=tgt: en.wait_ge(h, v))


def build_nc():
    nc = bass.Bass("TRN2", target_bir_lowering=False, debug=False,
                   detect_race_conditions=False)
    d = {}

    def din(name, shape, dt):
        d[name] = nc.dram_tensor(name, shape, dt, kind="ExternalInput").ap()

    din("xin", [2 * T, D], F32)
    for n in ("wq", "wk", "wv", "wo"):
        din(n, [D, D], F32)
    din("wg1", [D, HD], F32)
    din("wg2", [HD, D], F32)
    din("g_pk", [1024, MLP], U8)
    din("u_pk", [1024, MLP], U8)
    din("d_pk", [MLP // 2, D], U8)
    din("g_sc_e", [1024, MLP // 2], F32); din("g_sc_o", [1024, MLP // 2], F32)
    din("u_sc_e", [1024, MLP // 2], F32); din("u_sc_o", [1024, MLP // 2], F32)
    din("d_sc_a", [MLP // 2, 1024], F32); din("d_sc_b", [MLP // 2, 1024], F32)
    din("cblob", [P, CBLOB], U8)
    d_out = nc.dram_tensor("out", [T, D], F32, kind="ExternalOutput").ap()
    d_dbg = {}
    for nm in DBG:
        d_dbg[nm] = nc.dram_tensor("dbg_" + nm, [P, 32 * 1024], F32, kind="ExternalOutput").ap()

    KB = 1024
    AB = 207 * KB

    with (
        nc.sbuf_tensor("arena", [P, AB], U8) as arena,
        nc.psum_tensor("PS", [P, 8, 512], F32) as PS,
        nc.semaphore("s_wl") as s_wl,
        nc.semaphore("s_gl") as s_gl,
        nc.semaphore("s_sl") as s_sl,
        nc.semaphore("s_pe") as s_pe,
        nc.semaphore("s_dv") as s_dv,
        nc.semaphore("s_ac") as s_ac,
        nc.semaphore("s_dv2") as s_dv2,
        nc.semaphore("s_pe2") as s_pe2,
        nc.semaphore("s_bar") as s_bar,
        nc.semaphore("s_st") as s_st,
        nc.Block() as block,
    ):
        b = Bld(nc)
        for nm, h in (("wl", s_wl), ("gl", s_gl), ("sl", s_sl), ("pe", s_pe), ("dv", s_dv),
                      ("ac", s_ac), ("dv2", s_dv2), ("pe2", s_pe2), ("bar", s_bar),
                      ("st", s_st)):
            b.new_sem(nm, h)

        def bank(i, dt=F32, n=None):
            v = PS[:, i, :]
            if dt == BF16:
                v = v.bitcast(BF16)
            if n is not None:
                v = v[:, 0:n]
            return v

        off = [0]

        def carve(nb, name=""):
            o = off[0]
            assert o + nb <= AB, f"SBUF overflow {name}: {o}+{nb}"
            off[0] += (nb + 63) & ~63
            return o

        def view(o, shape, dt):
            esz = {F32: 4, BF16: 2, U16: 2, U8: 1}[dt]
            nb = int(np.prod(shape[1:])) * esz
            a = arena[:, o:o + nb].bitcast(dt)
            if len(shape) == 3:
                a = a.rearrange("p (a c) -> p a c", a=shape[1])
            elif len(shape) == 4:
                a = a.rearrange("p (a b c) -> p a b c", a=shape[1], b=shape[2])
            return a

        c_dT = view(carve(4 * KB, "dT"), [P, HEADS, CH], BF16)
        c_decq = view(carve(4 * KB), [P, HEADS, CH], BF16)
        c_deck = view(carve(4 * KB), [P, HEADS, CH], BF16)
        c_lamc = view(carve(4 * KB), [P, HEADS, CH], BF16)
        c_cc = {"c": view(carve(2 * KB), [P, T], BF16)}
        c_ss = {"c": view(carve(2 * KB), [P, T], BF16)}
        c_cc["o"] = view(carve(2 * KB), [P, T], BF16)
        c_ss["o"] = view(carve(2 * KB), [P, T], BF16)
        c_swap = view(carve(256), [P, P], BF16)
        c_idb = view(carve(256), [P, P], BF16)
        c_idf = view(carve(512), [P, P], F32)
        c_1ln = view(carve(64), [P, 1], BF16)
        c_1rms = view(carve(64), [P, 1], BF16)
        c_1row = view(carve(512), [P, P], F32)
        c_nw = view(carve(64), [P, HEADS], F32)
        c_actk = view(carve(64), [P, 5], F32)
        c_mask = view(carve(64), [P, 1], F32)
        assert off[0] == CBLOB, f"const blob layout mismatch: {off[0]} != {CBLOB}"
        c_wg2 = view(carve(4 * KB), [P, D], BF16)
        c_wg1 = view(carve(4 * KB), [P, NF, HD], BF16)
        oonrm = carve(32 * KB, "o_nrm")
        o_nrm = view(oonrm, [P, HEADS, T], BF16)
        g1T = view(carve(2 * KB), [P, T], BF16)
        base0 = off[0]

        offA = [base0]

        def carveA(nb, name=""):
            o = offA[0]
            assert o + nb <= AB, f"A overflow {name}: {o}+{nb}"
            offA[0] += (nb + 63) & ~63
            return o

        xln = view(carveA(32 * KB, "xln"), [P, NF, T], BF16)
        oxcln = carveA(32 * KB, "xcln")
        xcln = view(oxcln, [P, NF, T], BF16)
        wstage = view(carveA(16 * KB, "wstage"), [P, 2, 2048], F32)
        wbf = view(carveA(16 * KB, "wbf"), [P, 2, NF, HG * HD], BF16)
        qsb = view(carveA(4 * KB, "q"), [P, HG, T], BF16)
        ksb = view(carveA(4 * KB, "k"), [P, HG, T], BF16)
        vsb = view(carveA(4 * KB, "v"), [P, NT, HG * HD], BF16)
        oS32 = carveA(8 * KB, "S")
        S32 = view(oS32, [P, HEADS, HD], F32)
        oSbf = carveA(4 * KB, "Sbf")
        Sbf = view(oSbf, [P, HEADS, HD], BF16)
        xstgB = view(oS32, [P, 1, D], F32)
        scr2B = view(oSbf, [P, D], BF16)
        xstg = view(carveA(8 * KB, "xstg"), [P, 1, D], F32)
        scr2 = view(carveA(4 * KB, "scr2"), [P, D], BF16)
        scrq = view(carveA(1 * KB, "scrq"), [P, HG * CH], BF16)
        scrk = view(carveA(1 * KB, "scrk"), [P, HG * CH], BF16)
        scrs = view(carveA(1 * KB, "scrs"), [P, HG * CH], BF16)
        stat = view(carveA(256, "stat"), [P, 64], F32)
        sgt = view(oxcln, [P, 2, T], BF16)
        rms1 = view(oxcln + 4 * KB, [P, T], F32)
        assert offA[0] <= AB, "A total %d" % offA[0]

        offB = [base0]

        def carveB(nb, name=""):
            o = offB[0]
            assert o + nb <= AB, f"B overflow {name}"
            offB[0] += (nb + 63) & ~63
            return o

        xT = view(carveB(64 * KB, "xT"), [P, NF, T], F32)
        wostg = view(carveB(32 * KB, "wostg"), [P, 2, NF, 256], F32)
        wobf = view(carveB(16 * KB, "wobf"), [P, 2, NF, 256], BF16)
        xstg2 = view(carveB(8 * KB, "xstg2"), [P, 1, D], F32)
        sqs = view(carveB(2 * KB, "sqs"), [P, T], BF16)
        xbs = view(carveB(2 * KB, "xbs"), [P, T], BF16)
        lnsc = view(carveB(4 * KB, "lnsc"), [P, T], F32)
        rms2 = view(carveB(4 * KB, "rms2"), [P, T], F32)
        assert offB[0] <= AB

        offC = [base0 + 64 * KB]

        def carveC(nb, name=""):
            o = offC[0]
            assert o + nb <= AB, f"C overflow {name}: {o}+{nb}"
            offC[0] += (nb + 63) & ~63
            return o

        xln2 = view(oonrm, [P, NF, T], BF16)
        inter = view(carveC(16 * KB, "inter"), [P, 8, T], BF16)
        ogdec = carveC(8 * KB, "gdec")
        gdec = view(ogdec, [P, 2, 8, 256], BF16)
        oudec = carveC(8 * KB, "udec")
        udec = view(oudec, [P, 2, 8, 256], BF16)
        wddec = view(carveC(4 * KB, "wddec"), [P, 8, 256], BF16)
        gpk = view(carveC(8 * KB, "gpk"), [P, 2, 8, 512], U8)
        upk = view(carveC(8 * KB, "upk"), [P, 2, 8, 512], U8)
        dpk = view(carveC(8 * KB, "dpk"), [P, 4, D], U8)
        gsc = view(carveC(8 * KB, "gsc"), [P, 8, 256], F32)
        gscb = view(carveC(4 * KB, "gscb"), [P, 8, 256], BF16)
        gl = view(0, [P, 2048], BF16)
        dsc = view(4 * KB, [P, 4, 256], F32)
        dscb = view(8 * KB, [P, 4, 256], BF16)
        scrC = view(10 * KB, [P, 8, 256], U16)
        up_sb = view(16 * KB, [P, 2048], BF16)
        wddec2 = view(20 * KB, [P, 8, 256], BF16)
        assert offC[0] <= AB, f"C total {offC[0]}"
        ostg = view(ogdec, [P, 2, D], F32)

        def dbg_dump(nm, ap2d):
            if nm not in d_dbg:
                return
            b.barrier()
            n = int(np.prod(ap2d.shape[1:]))
            if ap2d.dtype == BF16:
                for i in range(0, n, 2048):
                    w = min(2048, n - i)
                    b.op("vector", lambda e, i=i, w=w: e.tensor_copy(
                        out=xstg[:, 0, 0:w] if w <= 2048 else None,
                        in_=ap2d.rearrange("p a c -> p (a c)")[:, i:i + w] if len(ap2d.shape) == 3 else ap2d[:, i:i + w]))
                    b.opi("vector", lambda e: e.engine_nop(), "dv")
                    b.wait("scalar", "dv")
                    b.dma("scalar", d_dbg[nm][:, i:i + w], xstg[:, 0, 0:w], "sl")
                    b.wait("vector", "sl")
            else:
                a2 = ap2d.rearrange("p a c -> p (a c)") if len(ap2d.shape) == 3 else ap2d
                b.dma("scalar", d_dbg[nm][:, 0:n], a2, "sl")
                b.wait("scalar", "sl")
            b.barrier()

        b.dma("gpsimd", arena[:, 0:CBLOB], d["cblob"], "gl")
        b.dma("sync", wstage[:, 0, :], d["wg2"], "wl")
        for par in range(2):
            src = AP(tensor=d["wg1"].tensor, offset=par * HD,
                     ap=[[2 * HD, 128], [256 * HD, 8], [1, HD]])
            dst = wstage[:, 1, par * 1024:(par + 1) * 1024].rearrange("p (a c) -> p a c", a=8)
            b.dma("sync", dst, src, "wl")
        b.wait("vector", "wl")
        b.op("vector", lambda e: e.tensor_copy(out=c_wg2[:], in_=wstage[:, 0, :]))
        for par in range(2):
            for m in range(8):
                s_ = wstage[:, 1, par * 1024 + m * HD: par * 1024 + (m + 1) * HD]
                b.opi("vector", lambda e, s_=s_, kc=2 * m + par: e.tensor_copy(
                    out=c_wg1[:, kc, :], in_=s_), "dv")
        wgpre_dv = b.cnt["dv"]

        b.wait("vector", "gl")
        tr_free = {0: 0, 1: 0}
        ln_done = {}
        ln_tr_pe = {0: 0, 1: 0}
        stA = {}

        def ln_A(ti):
            par = ti % 2
            xs = (xstg if par == 0 else xstgB)[:, 0, :]
            sc = scr2 if par == 0 else scr2B
            so = 8 * par
            if ti >= 2:
                b.wait("gpsimd", "dv", ln_done[ti - 2])
            b.dma("gpsimd", (xstg if par == 0 else xstgB)[:, 0, :],
                  d["xin"][ti * P:(ti + 1) * P, :], "gl")
            b.wait("vector", "gl")
            b.wait("vector", "pe", ln_tr_pe[par])
            b.op("vector", lambda e, xs=xs, so=so: e.reduce_sum(
                stat[:, so:so + 1], xs, axis=mybir.AxisListType.X))
            b.op("vector", lambda e, xs=xs, sc=sc: e.tensor_tensor(
                out=sc[:], in0=xs, in1=xs, op=ALU.mult))
            b.opi("vector", lambda e, sc=sc, so=so: e.reduce_sum(
                stat[:, so + 1:so + 2], sc[:], axis=mybir.AxisListType.X), "st")
            b.wait("vector", "st")
            b.opi("vector", lambda e, so=so: e.tensor_scalar(
                out=stat[:, so + 2:so + 3], in0=stat[:, so:so + 1],
                scalar1=1.0 / D, scalar2=None, op0=ALU.mult), "st")
            b.wait("vector", "st")
            b.opi("vector", lambda e, so=so: e.tensor_tensor(
                out=stat[:, so + 4:so + 5], in0=stat[:, so + 2:so + 3],
                in1=stat[:, so + 2:so + 3], op=ALU.mult), "st")
            b.wait("vector", "st")
            b.opi("vector", lambda e, so=so: e.scalar_tensor_tensor(
                out=stat[:, so + 5:so + 6], in0=stat[:, so + 1:so + 2], scalar=1.0 / D,
                in1=stat[:, so + 4:so + 5], op0=ALU.mult, op1=ALU.subtract), "dv")
            b.wait("vector", "dv")
            b.opi("vector", lambda e: e.engine_nop(), "st")
            b.wait("scalar", "st")
            b.opi("scalar", lambda e, so=so: e.activation(
                out=stat[:, so + 6:so + 7], in_=stat[:, so + 5:so + 6],
                func=AF.Ln, bias=c_actk[:, 0:1]), "st")
            b.wait("scalar", "st")
            b.opi("scalar", lambda e, so=so: e.activation(
                out=stat[:, so + 7:so + 8], in_=stat[:, so + 6:so + 7],
                func=AF.Exp, scale=c_actk[:, 2:3]), "ac")
            b.wait("scalar", "ac")
            b.opi("scalar", lambda e: e.nop(), "st")
            stA[ti] = b.cnt["st"]

        def ln_B(ti):
            par = ti % 2
            xs = (xstg if par == 0 else xstgB)[:, 0, :]
            sc = scr2 if par == 0 else scr2B
            so = 8 * par
            b.wait("vector", "st", stA[ti])
            b.opi("vector", lambda e, xs=xs, sc=sc, so=so: e.tensor_scalar(
                out=sc[:], in0=xs, scalar1=stat[:, so + 2:so + 3],
                scalar2=stat[:, so + 7:so + 8],
                op0=ALU.subtract, op1=ALU.mult), "dv")
            ln_done[ti] = b.cnt["dv"]
            dst = xcln if ti < NT else xln
            tc = ti % NT
            b.wait("tensor", "dv", ln_done[ti])
            for g4 in range(4):
                bk = g4 % 2
                b.wait("tensor", "dv2", tr_free[bk])
                for k4 in range(4):
                    fc = g4 * 4 + k4
                    m, par2 = fc // 2, fc % 2
                    srct = sc[:, 256 * m + par2: 256 * (m + 1): 2]
                    pv = bank(bk, BF16)[:, k4 * P:(k4 + 1) * P]
                    b.opi("tensor", lambda e, pv=pv, s_=srct: e.transpose(pv, s_, c_idb[:]), "pe")
                b.wait("vector", "pe")
                pvg = bank(bk, BF16)[:, 0:512].rearrange("p (a c) -> p a c", a=4)
                dstg = dst[:, g4 * 4:(g4 + 1) * 4, tc * P:(tc + 1) * P]
                tr_free[bk] = b.opi("vector", lambda e, pvg=pvg, dstg=dstg: e.tensor_copy(
                    out=dstg, in_=pvg), "dv2")
            ln_tr_pe[par] = b.cnt["pe"]

        ln_A(0)
        ln_A(1)
        for ti in range(2 * NT):
            ln_B(ti)
            if ti + 2 < 2 * NT:
                ln_A(ti + 2)
        b.op("vector", lambda e: e.memset(S32[:].rearrange("p a c -> p (a c)"), 0.0))
        b.opi("vector", lambda e: e.memset(Sbf[:].rearrange("p a c -> p (a c)"), 0.0), "dv")
        xln_done = b.cnt["dv2"]
        dbg_dump("xln", xln)
        dbg_dump("xcln", xcln)

        b.wait("tensor", "dv2", xln_done)
        b.wait("tensor", "dv", wgpre_dv)
        for th in range(2):
            for kc in range(NF):
                b.opi("tensor", lambda e, kc=kc, th=th: e.matmul(
                    bank(2)[:, :], c_wg1[:, kc, :], xln[:, kc, th * 512:(th + 1) * 512],
                    start=(kc == 0), stop=(kc == NF - 1)), "pe")
            b.wait("vector", "pe")
            b.opi("vector", lambda e, th=th: e.tensor_copy(
                out=g1T[:, th * 512:(th + 1) * 512], in_=bank(2)[:, :]), "dv")
            b.wait("tensor", "dv")

        wload_state = {"idx": 0, "peuse": {0: 0, 1: 0},
                       "conv": {0: wgpre_dv, 1: wgpre_dv}}

        def wload(name, hg):
            st = wload_state
            wbuf = st["idx"] % 2
            st["idx"] += 1
            b.wait("sync", "dv", st["conv"][1 - wbuf])
            for par in range(2):
                src = AP(tensor=d[name].tensor, offset=par * D + hg * HG * HD,
                         ap=[[2 * D, 128], [256 * D, 8], [1, HG * HD]])
                dst = wstage[:, par, :].rearrange("p (a c) -> p a c", a=8)
                b.dma("sync", dst, src, "wl")
            b.wait("vector", "wl")
            b.wait("vector", "pe", st["peuse"][wbuf])
            for par in range(2):
                for m in range(8):
                    s_ = wstage[:, par, m * HG * HD:(m + 1) * HG * HD]
                    b.op("vector", lambda e, s_=s_, kc=2 * m + par, wbuf=wbuf: e.tensor_copy(
                        out=wbf[:, wbuf, kc, :], in_=s_))
            cv = b.opi("vector", lambda e: e.engine_nop(), "dv")
            st["conv"][wbuf] = cv
            return wbuf, cv

        def attn_pass(is_own):
            xl = xln if is_own else xcln
            key = "o" if is_own else "c"
            st = wload_state
            for hg in range(NHG):
                silu_bank = {0: 0, 1: 0}

                def proj_headmajor(wname, dstb):
                    wbuf, cv = wload(wname, hg)
                    b.wait("tensor", "dv", cv)
                    for hh in range(HG):
                        for th in range(2):
                            bk = (hh * 2 + th) % 2
                            b.wait("tensor", "ac", silu_bank[bk])
                            for kc in range(NF):
                                b.opi("tensor", lambda e, kc=kc, hh=hh, th=th, bk=bk, wbuf=wbuf: e.matmul(
                                    bank(bk)[:, :], wbf[:, wbuf, kc, hh * HD:(hh + 1) * HD],
                                    xl[:, kc, th * 512:(th + 1) * 512],
                                    start=(kc == 0), stop=(kc == NF - 1)), "pe")
                            b.wait("scalar", "pe")
                            silu_bank[bk] = b.opi("scalar", lambda e, hh=hh, th=th, bk=bk, dstb=dstb: e.activation(
                                out=dstb[:, hh, th * 512:(th + 1) * 512], in_=bank(bk)[:, :],
                                func=AF.Silu), "ac")
                    st["peuse"][wbuf] = b.cnt["pe"]
                    return b.cnt["ac"]

                if is_own:
                    q_ac = proj_headmajor("wq", qsb)
                    if hg == 0 and "qsnap1" in d_dbg:
                        b.wait("scalar", "ac", q_ac)
                        b.dma("scalar", d_dbg["qsnap1"][:, 0:1024],
                              qsb[:].rearrange("p a c -> p (a c)").bitcast(F32), "sl")
                    if hg == 0:
                        dbg_dump("qraw", qsb)
                        dbg_dump("kraw", ksb)
                k_ac = proj_headmajor("wk", ksb)
                if is_own and hg == 0 and "qsnap2" in d_dbg:
                    b.wait("scalar", "ac", k_ac)
                    b.dma("scalar", d_dbg["qsnap2"][:, 0:1024],
                          qsb[:].rearrange("p a c -> p (a c)").bitcast(F32), "sl")
                if hg == 0 and is_own:
                    dbg_dump("qpk", qsb)
                    dbg_dump("kpk", ksb)

                wbuf, cv = wload("wv", hg)
                b.wait("tensor", "dv", cv)
                b.wait("tensor", "ac", k_ac)
                v_free = {0: 0, 1: 0}
                for tc in range(NT):
                    bk = tc % 2
                    b.wait("tensor", "dv2", v_free[bk])
                    for kc in range(NF):
                        b.opi("tensor", lambda e, kc=kc, tc=tc, bk=bk, wbuf=wbuf: e.matmul(
                            bank(bk)[:, 0:HG * HD], xl[:, kc, tc * P:(tc + 1) * P],
                            wbf[:, wbuf, kc, :], start=(kc == 0), stop=(kc == NF - 1)), "pe")
                    b.wait("vector", "pe")
                    v_free[bk] = b.opi("vector", lambda e, tc=tc, bk=bk: e.tensor_copy(
                        out=vsb[:, tc, :], in_=bank(bk)[:, 0:HG * HD]), "dv2")
                st["peuse"][wbuf] = b.cnt["pe"]
                v_dv = b.cnt["dv2"]
                if hg == 0 and is_own:
                    dbg_dump("qpv", qsb)

                rot_bank = {4: 0, 5: 0}

                def rotate(buf3, key):
                    for hh in range(HG):
                        for th in range(2):
                            bk = 4 + th
                            sl = slice(th * 512, (th + 1) * 512)
                            b.wait("tensor", "dv", rot_bank[bk])
                            b.opi("tensor", lambda e, hh=hh, sl=sl, bk=bk, buf3=buf3: e.matmul(
                                bank(bk)[:, :], c_swap[:], buf3[:, hh, sl],
                                start=True, stop=True), "pe")
                            b.wait("vector", "pe")
                            b.op("vector", lambda e, hh=hh, sl=sl, buf3=buf3, key=key: e.tensor_tensor(
                                out=scr2[:, 0:512], in0=buf3[:, hh, sl], in1=c_cc[key][:, sl],
                                op=ALU.mult))
                            b.op("vector", lambda e, sl=sl, bk=bk, key=key: e.tensor_tensor(
                                out=scr2[:, 512:1024], in0=bank(bk)[:, :], in1=c_ss[key][:, sl],
                                op=ALU.mult))
                            rot_bank[bk] = b.opi("vector", lambda e, hh=hh, sl=sl, buf3=buf3: e.tensor_tensor(
                                out=buf3[:, hh, sl], in0=scr2[:, 0:512], in1=scr2[:, 512:1024],
                                op=ALU.add), "dv")

                if is_own:
                    rotate(qsb, key)
                    if hg == 0:
                        dbg_dump("qrot", qsb)
                rotate(ksb, key)
                rot_dv = b.cnt["dv"]
                if hg == 0 and is_own:
                    dbg_dump("q0", qsb)
                    dbg_dump("k0", ksb)
                    dbg_dump("v0", vsb)

                h0 = hg * HG
                Ssl32 = S32[:, h0:h0 + HG, :].rearrange("p a c -> p (a c)")
                Sslbf = Sbf[:, h0:h0 + HG, :].rearrange("p a c -> p (a c)")
                decq_sl = c_decq[:, h0:h0 + HG, :].rearrange("p a c -> p (a c)")
                deck_sl = c_deck[:, h0:h0 + HG, :].rearrange("p a c -> p (a c)")
                lamc_sl = c_lamc[:, h0:h0 + HG, :].rearrange("p a c -> p (a c)")
                dTsl = c_dT[:, h0:h0 + HG, :].rearrange("p a c -> p (a c)")
                b.wait("tensor", "dv", rot_dv)
                b.wait("tensor", "dv2", v_dv)
                sc_free = 0
                su_done = b.cnt["dv"]
                o_free = 0
                mm1_done = 0
                for c in range(NT):
                    csl = slice(c * CH, (c + 1) * CH)
                    if is_own:
                        b.wait("vector", "pe", mm1_done)
                        b.opi("vector", lambda e, csl=csl, decq_sl=decq_sl: e.tensor_tensor(
                            out=scrq[:].rearrange("p (a c) -> p a c", a=HG),
                            in0=qsb[:, :, csl],
                            in1=decq_sl.rearrange("p (a c) -> p a c", a=HG), op=ALU.mult), "dv")
                        qd_dv = b.cnt["dv"]
                        b.wait("tensor", "dv", qd_dv)
                        b.wait("tensor", "dv2", o_free)
                        for hh in range(HG):
                            b.opi("tensor", lambda e, hh=hh, h0=h0: e.matmul(
                                bank(hh)[:, 0:CH],
                                Sbf[:, h0 + hh, :],
                                scrq[:, hh * CH:(hh + 1) * CH],
                                start=True, stop=False), "pe")
                        mm1_done = b.cnt["pe"]
                        b.wait("tensor", "dv", sc_free)
                        for hh in range(HG):
                            b.opi("tensor", lambda e, hh=hh, csl=csl: e.matmul(
                                bank(2)[:, hh * CH:(hh + 1) * CH],
                                ksb[:, hh, csl], qsb[:, hh, csl], start=True, stop=True), "pe")
                        b.wait("vector", "pe")
                        b.opi("vector", lambda e, dTsl=dTsl: e.tensor_tensor(
                            out=scrs[:], in0=bank(2)[:, 0:HG * CH], in1=dTsl, op=ALU.mult), "dv")
                        sc_free = b.cnt["dv"]
                        b.wait("tensor", "dv", sc_free)
                        for hh in range(HG):
                            b.opi("tensor", lambda e, hh=hh, c=c: e.matmul(
                                bank(hh)[:, 0:CH],
                                vsb[:, c, hh * HD:(hh + 1) * HD], scrs[:, hh * CH:(hh + 1) * CH],
                                start=False, stop=True), "pe")
                        b.wait("vector", "pe")
                        o_free = b.opi("vector", lambda e, c=c, h0=h0: e.tensor_copy(
                            out=o_nrm[:, h0, c * CH:(c + 1) * CH], in_=bank(0)[:, 0:CH]), "dv2")
                        o_free = b.opi("vector", lambda e, c=c, h0=h0: e.tensor_copy(
                            out=o_nrm[:, h0 + 1, c * CH:(c + 1) * CH], in_=bank(1)[:, 0:CH]), "dv2")
                    b.wait("tensor", "dv", su_done)
                    for hh in range(HG):
                        b.opi("tensor", lambda e, hh=hh, csl=csl: e.transpose(
                            bank(3, BF16)[:, hh * CH:(hh + 1) * CH],
                            ksb[:, hh, csl], c_idb[:]), "pe")
                    b.wait("vector", "pe")
                    b.opi("vector", lambda e, deck_sl=deck_sl: e.tensor_tensor(
                        out=scrk[:], in0=bank(3, BF16)[:, 0:HG * CH], in1=deck_sl, op=ALU.mult), "dv")
                    kd_dv = b.cnt["dv"]
                    b.wait("tensor", "dv", kd_dv)
                    for hh in range(HG):
                        b.opi("tensor", lambda e, hh=hh, c=c: e.matmul(
                            bank(4)[:, hh * CH:(hh + 1) * CH],
                            scrk[:, hh * CH:(hh + 1) * CH], vsb[:, c, hh * HD:(hh + 1) * HD],
                            start=True, stop=True), "pe")
                    b.wait("vector", "pe")
                    b.op("vector", lambda e, Ssl32=Ssl32, lamc_sl=lamc_sl: e.tensor_tensor(
                        out=Ssl32, in0=Ssl32, in1=lamc_sl, op=ALU.mult))
                    b.op("vector", lambda e, Ssl32=Ssl32: e.tensor_tensor(
                        out=Ssl32, in0=Ssl32, in1=bank(4)[:, 0:HG * CH], op=ALU.add))
                    b.opi("vector", lambda e, Ssl32=Ssl32, Sslbf=Sslbf: e.tensor_copy(
                        out=Sslbf, in_=Ssl32), "dv")
                    su_done = b.cnt["dv"]
                b.wait("tensor", "dv", su_done)

        attn_pass(False)
        b.op("vector", lambda e: e.tensor_scalar(
            out=S32[:].rearrange("p a c -> p (a c)"), in0=S32[:].rearrange("p a c -> p (a c)"),
            scalar1=c_mask[:, 0:1], scalar2=None, op0=ALU.mult))
        b.opi("vector", lambda e: e.tensor_copy(
            out=Sbf[:].rearrange("p a c -> p (a c)"), in_=S32[:].rearrange("p a c -> p (a c)")), "dv")
        attn_pass(True)
        dbg_dump("o", o_nrm)
        dbg_dump("g1T", g1T)

        b.wait("tensor", "dv")
        b.wait("tensor", "dv2")
        p4_pe_gate = 0
        for h in range(HEADS):
            b.wait("tensor", "dv", p4_pe_gate)
            for th in range(2):
                b.opi("tensor", lambda e, h=h, th=th: e.matmul(
                    bank(th)[:, :], c_wg2[:, h * HD:(h + 1) * HD], g1T[:, th * 512:(th + 1) * 512],
                    start=True, stop=True), "pe")
            b.wait("scalar", "pe")
            for th in range(2):
                b.opi("scalar", lambda e, th=th: e.activation(
                    out=sgt[:, 0, th * 512:(th + 1) * 512], in_=bank(th)[:, :], func=AF.Sigmoid), "ac")
            b.wait("scalar", "ac")
            b.opi("scalar", lambda e: e.nop(), "st")
            b.wait("vector", "st")
            b.opi("vector", lambda e, h=h: e.tensor_tensor(
                out=sgt[:, 1, :], in0=o_nrm[:, h, :], in1=sgt[:, 0, :], op=ALU.mult), "st")
            b.wait("vector", "st")
            b.opi("vector", lambda e: e.tensor_tensor(
                out=scr2[:, 0:T], in0=sgt[:, 1, :], in1=sgt[:, 1, :], op=ALU.mult), "dv")
            sq_dv = b.cnt["dv"]
            p4_pe_gate = sq_dv
            b.wait("tensor", "dv", sq_dv)
            b.wait("tensor", "ac")
            for th in range(2):
                b.opi("tensor", lambda e, th=th: e.matmul(
                    PS[0:1, 2 + th, :], c_1rms[:], scr2[:, th * 512:(th + 1) * 512],
                    start=True, stop=True), "pe")
            b.wait("scalar", "pe")
            b.op("scalar", lambda e: e.activation(out=rms1[0:1, 0:512], in_=PS[0:1, 2, :],
                                                  func=AF.Ln, bias=c_actk[0:1, 1:2]))
            b.opi("scalar", lambda e: e.activation(out=rms1[0:1, 512:1024], in_=PS[0:1, 3, :],
                                                   func=AF.Ln, bias=c_actk[0:1, 1:2]), "st")
            b.wait("scalar", "st")
            b.opi("scalar", lambda e: e.activation(out=rms1[0:1, 0:1024], in_=rms1[0:1, 0:1024],
                                                   func=AF.Exp, scale=c_actk[0:1, 2:3]), "ac")
            b.wait("scalar", "ac")
            b.opi("scalar", lambda e: e.nop(), "st")
            b.wait("tensor", "st")
            for th in range(2):
                b.opi("tensor", lambda e, th=th: e.matmul(
                    bank(4 + th)[:, :], c_1row[0:1, :], rms1[0:1, th * 512:(th + 1) * 512],
                    start=True, stop=True), "pe")
            b.wait("vector", "pe")
            for th in range(2):
                b.opi("vector", lambda e, th=th: e.tensor_tensor(
                    out=scr2[:, T + th * 512:T + (th + 1) * 512],
                    in0=sgt[:, 1, th * 512:(th + 1) * 512],
                    in1=bank(4 + th)[:, :], op=ALU.mult), "st")
            b.wait("vector", "st")
            b.opi("vector", lambda e, h=h: e.tensor_scalar(
                out=o_nrm[:, h, :], in0=scr2[:, T:2 * T], scalar1=c_nw[:, h:h + 1], scalar2=None,
                op0=ALU.mult), "dv")
            p4_pe_gate = b.cnt["dv"]

        dbg_dump("normed", o_nrm)

        b.barrier()
        xre_free = [0]
        tr_free = {0: 0, 1: 0}
        for tc in range(NT):
            b.wait("scalar", "pe2", xre_free[0])
            b.dma("scalar", xstg2[:, 0, :], d["xin"][(NT + tc) * P:(NT + tc + 1) * P, :], "sl")
            b.wait("tensor", "sl")
            for g4 in range(4):
                bk = g4 % 2
                b.wait("tensor", "dv2", tr_free[bk])
                for k4 in range(4):
                    fc = g4 * 4 + k4
                    m, par = fc // 2, fc % 2
                    srct = xstg2[:, 0, 256 * m + par: 256 * (m + 1): 2]
                    b.opi("tensor", lambda e, bk=bk, k4=k4, s_=srct: e.transpose(
                        bank(bk)[:, k4 * P:(k4 + 1) * P], s_, c_idf[:]), "pe")
                b.wait("vector", "pe")
                tr_free[bk] = b.opi("vector", lambda e, bk=bk, g4=g4, tc=tc: e.tensor_copy(
                    out=xT[:, g4 * 4:(g4 + 1) * 4, tc * P:(tc + 1) * P],
                    in_=bank(bk)[:, :].rearrange("p (a c) -> p a c", a=4)), "dv2")
            xre_free[0] = b.opi("tensor", lambda e: e.nop(), "pe2")
        xT_dv = b.cnt["dv2"]

        wo_state = {"conv": {0: 0, 1: 0}, "peuse": {0: 0, 1: 0}}
        res_free = {6: 0, 7: 0}
        b.wait("tensor", "dv2", xT_dv)
        for m in range(8):
            buf = m % 2
            b.wait("sync", "dv", wo_state["conv"][buf])
            src = AP(tensor=d["wo"].tensor, offset=256 * m,
                     ap=[[D, 128], [P * D, NF], [1, 256]])
            b.dma("sync", wostg[:, buf, :, :], src, "wl")
            b.wait("vector", "wl")
            b.wait("vector", "pe", wo_state["peuse"][buf])
            cv = b.opi("vector", lambda e, buf=buf: e.tensor_copy(
                out=wobf[:, buf, :, :].rearrange("p a c -> p (a c)"),
                in_=wostg[:, buf, :, :].rearrange("p a c -> p (a c)")), "dv")
            wo_state["conv"][buf] = cv
            b.wait("tensor", "dv", cv)
            for par in range(2):
                fc = 2 * m + par
                for th in range(2):
                    bk = 6 + th
                    b.wait("tensor", "dv", res_free[bk])
                    for ec in range(NF):
                        b.opi("tensor", lambda e, buf=buf, ec=ec, par=par, th=th, bk=bk: e.matmul(
                            bank(bk)[:, :], wobf[:, buf, ec, par:256:2],
                            o_nrm[:, ec, th * 512:(th + 1) * 512],
                            start=(ec == 0), stop=(ec == NF - 1)), "pe")
                    b.wait("vector", "pe")
                    res_free[bk] = b.opi("vector", lambda e, fc=fc, th=th, bk=bk: e.tensor_tensor(
                        out=xT[:, fc, th * 512:(th + 1) * 512],
                        in0=xT[:, fc, th * 512:(th + 1) * 512], in1=bank(bk)[:, :], op=ALU.add), "dv")
            wo_state["peuse"][buf] = b.cnt["pe"]
        b.wait("tensor", "dv", max(res_free.values()))

        dbg_dump("xa", xT[:, 0:8, :])
        dbg_dump("xa2", xT[:, 8:16, :])

        lnmm = 0
        for fc in range(NF):
            b.wait("vector", "pe", lnmm)
            b.op("vector", lambda e, fc=fc: e.tensor_copy(out=xbs[:], in_=xT[:, fc, :]))
            b.opi("vector", lambda e: e.tensor_tensor(out=sqs[:], in0=xbs[:], in1=xbs[:],
                                                      op=ALU.mult), "dv")
            b.wait("tensor", "dv")
            for th in range(2):
                b.opi("tensor", lambda e, fc=fc, th=th: e.matmul(
                    PS[0:1, th, :], c_1ln[:], xbs[:, th * 512:(th + 1) * 512],
                    start=(fc == 0), stop=(fc == NF - 1)), "pe")
                b.opi("tensor", lambda e, fc=fc, th=th: e.matmul(
                    PS[0:1, 2 + th, :], c_1ln[:], sqs[:, th * 512:(th + 1) * 512],
                    start=(fc == 0), stop=(fc == NF - 1)), "pe")
            lnmm = b.cnt["pe"]
        b.wait("vector", "pe", lnmm)
        b.op("vector", lambda e: e.tensor_copy(out=lnsc[0:1, 0:512], in_=PS[0:1, 0, :]))
        b.opi("vector", lambda e: e.tensor_copy(out=lnsc[0:1, 512:1024], in_=PS[0:1, 1, :]), "st")
        b.wait("vector", "st")
        b.op("vector", lambda e: e.tensor_tensor(out=rms2[0:1, 0:512], in0=lnsc[0:1, 0:512],
                                                 in1=lnsc[0:1, 0:512], op=ALU.mult))
        b.opi("vector", lambda e: e.tensor_tensor(out=rms2[0:1, 512:1024], in0=lnsc[0:1, 512:1024],
                                                  in1=lnsc[0:1, 512:1024], op=ALU.mult), "st")
        b.wait("vector", "st")
        b.op("vector", lambda e: e.tensor_tensor(out=rms2[0:1, 0:512], in0=PS[0:1, 2, :],
                                                 in1=rms2[0:1, 0:512], op=ALU.subtract))
        b.opi("vector", lambda e: e.tensor_tensor(out=rms2[0:1, 512:1024], in0=PS[0:1, 3, :],
                                                  in1=rms2[0:1, 512:1024], op=ALU.subtract), "dv")
        b.wait("vector", "dv")
        b.opi("vector", lambda e: e.engine_nop(), "st")
        b.wait("scalar", "st")
        b.opi("scalar", lambda e: e.activation(out=rms2[0:1, 0:1024], in_=rms2[0:1, 0:1024],
                                               func=AF.Ln, bias=c_actk[0:1, 0:1]), "st")
        b.wait("scalar", "st")
        b.opi("scalar", lambda e: e.activation(out=rms2[0:1, 0:1024], in_=rms2[0:1, 0:1024],
                                               func=AF.Exp, scale=c_actk[0:1, 2:3]), "ac")
        b.wait("scalar", "ac")
        b.opi("scalar", lambda e: e.nop(), "st")
        b.wait("vector", "st")
        b.opi("vector", lambda e: e.scalar_tensor_tensor(
            out=lnsc[0:1, 0:1024], in0=lnsc[0:1, 0:1024], scalar=-1.0, in1=rms2[0:1, 0:1024],
            op0=ALU.mult, op1=ALU.mult), "dv")
        b.wait("vector", "dv")
        b.opi("vector", lambda e: e.engine_nop(), "st")
        b.wait("tensor", "st")
        b.wait("tensor", "dv")
        for th in range(2):
            b.opi("tensor", lambda e, th=th: e.matmul(
                bank(4 + th)[:, :], c_1row[0:1, :], rms2[0:1, th * 512:(th + 1) * 512],
                start=True, stop=True), "pe")
            b.opi("tensor", lambda e, th=th: e.matmul(
                bank(6 + th)[:, :], c_1row[0:1, :], lnsc[0:1, th * 512:(th + 1) * 512],
                start=True, stop=True), "pe")
        b.wait("vector", "pe")
        for fc in range(NF):
            for th in range(2):
                b.opi("vector", lambda e, fc=fc, th=th: e.tensor_tensor(
                    out=rms2[:, th * 512:(th + 1) * 512],
                    in0=xT[:, fc, th * 512:(th + 1) * 512], in1=bank(4 + th)[:, :], op=ALU.mult), "st")
                b.wait("vector", "st")
                b.opi("vector", lambda e, fc=fc, th=th: e.tensor_tensor(
                    out=xln2[:, fc, th * 512:(th + 1) * 512],
                    in0=rms2[:, th * 512:(th + 1) * 512], in1=bank(6 + th)[:, :], op=ALU.add), "dv")
        xln2_dv = b.cnt["dv"]
        dbg_dump("xln2", xln2)
        b.barrier()

        scb_war = {"gu": 0, "dn": 0}

        def dec_batch(v16, shlo, dst_lo, dst_hi, scf, scb, nj, key, sem=None):
            b.op("vector", lambda e, scf=scf, scb=scb: e.tensor_scalar(
                out=scb, in0=scf, scalar1=float(2.0 ** 126), scalar2=None, op0=ALU.mult))
            last = None
            for k, (sh, dst) in enumerate(((shlo, dst_lo), (shlo + 4, dst_hi))):
                t = scrC[:, 0:nj, :]
                if sh == 0:
                    b.op("vector", lambda e, v16=v16, t=t: e.tensor_scalar(
                        out=t, in0=v16, scalar1=0x000F, scalar2=None, op0=ALU.bitwise_and))
                else:
                    b.op("vector", lambda e, v16=v16, t=t, sh=sh: e.tensor_scalar(
                        out=t, in0=v16, scalar1=sh, scalar2=0x000F,
                        op0=ALU.logical_shift_right, op1=ALU.bitwise_and))
                b.op("vector", lambda e, t=t: e.tensor_scalar(
                    out=t, in0=t, scalar1=0x1040, scalar2=None, op0=ALU.mult))
                b.op("vector", lambda e, t=t: e.tensor_scalar(
                    out=t, in0=t, scalar1=0x81C0, scalar2=None, op0=ALU.bitwise_and))
                fn = lambda e, t=t, dst=dst, scb=scb: e.tensor_tensor(
                    out=dst, in0=t.bitcast(BF16), in1=scb, op=ALU.mult)
                if k == 1 and sem is not None:
                    last = b.opi("vector", fn, sem)
                else:
                    b.op("vector", fn)
            scb_war[key] = last if last is not None else b.cnt["dv"]
            return last

        pk_done = {}

        def pk_dma(mat, w):
            arr = gpk if mat == "g" else upk
            src = AP(tensor=d[mat + "_pk"].tensor, offset=512 * w,
                     ap=[[MLP, 128], [128 * MLP, 8], [1, 512]])
            pk_done[(mat, w)] = b.dma("sync", arr[:, w % 2, :, :], src, "wl")

        dpk_done = {}

        def dpk_dma(s, q):
            src = AP(tensor=d["d_pk"].tensor, offset=(4 * s) * 128 * D + 512 * q,
                     ap=[[D, 128], [128 * D, 4], [1, 512]])
            dpk_done[(s, q)] = b.dma("gpsimd", dpk[:, :, 512 * q:512 * q + 512], src, "gl")

        b.wait("sync", "dv", max(res_free.values()))
        pk_dma("g", 0)
        pk_dma("u", 0)
        decuse = {"g": 0, "u": 0}
        pkuse = {("g", 0): 0, ("g", 1): 0, ("u", 0): 0, ("u", 1): 0}
        gu_bank_free = {0: 0, 1: 0, 2: 0, 3: 0}
        dn_bank_free = {4: 0, 5: 0}
        wd_use = {0: 0, 1: 0}
        inter_tt = [0]
        dscuse = [0]

        for s_i in range(NS):
            if s_i == 0:
                b.wait("gpsimd", "dv", xln2_dv)
                for q in range(4):
                    dpk_dma(0, q)
            for ww in range(2):
                w = 2 * s_i + ww
                wbuf = w % 2
                if w + 1 < NWIN:
                    b.wait("sync", "dv", pkuse[("g", (w + 1) % 2)])
                    pk_dma("g", w + 1)
                    b.wait("sync", "dv", pkuse[("u", (w + 1) % 2)])
                    pk_dma("u", w + 1)
                for npar in range(2):
                    for mat in ("g", "u"):
                        dec = gdec if mat == "g" else udec
                        arr = gpk if mat == "g" else upk
                        nm = f"{mat}_sc_{'e' if npar == 0 else 'o'}"
                        b.wait("sync", "dv", scb_war["gu"])
                        scd = b.dma("sync", gsc[:],
                                    AP(tensor=d[nm].tensor, offset=256 * w,
                                       ap=[[MLP // 2, 128], [128 * (MLP // 2), 8], [1, 256]]),
                                    "wl")
                        b.wait("vector", "wl", scd)
                        b.wait("vector", "wl", pk_done[(mat, w)])
                        b.wait("vector", "pe", decuse[mat])
                        v16 = arr[:, wbuf, :, :].bitcast(U16)
                        dv_dec = dec_batch(v16, 8 * npar, dec[:, 0, :, :], dec[:, 1, :, :],
                                           gsc[:, :, :], gscb[:, :, :], 8, "gu", sem="dv")
                        pkuse[(mat, wbuf)] = dv_dec
                        b.wait("tensor", "dv", dv_dec)
                        if s_i == 0 and ww == 0 and mat == "g" and npar == 0:
                            b.wait("tensor", "dv", xln2_dv)
                        if mat == "g":
                            b.wait("scalar", "dv", inter_tt[0])
                        for ci in range(2):
                            islot = 4 * npar + 2 * ww + ci
                            for th in range(2):
                                bk = (ci * 2 + th) % 4
                                b.wait("tensor", "ac", gu_bank_free[bk])
                                for kc in range(NF):
                                    b.opi("tensor", lambda e, kc=kc, ci=ci, th=th, bk=bk, dec=dec: e.matmul(
                                        bank(bk)[:, :], dec[:, kc % 2, kc // 2, ci * 128:(ci + 1) * 128],
                                        xln2[:, kc, th * 512:(th + 1) * 512],
                                        start=(kc == 0), stop=(kc == NF - 1)), "pe")
                                mme = b.cnt["pe"]
                                sl = ci * 2 + th
                                b.wait("scalar", "pe", mme)
                                if mat == "g":
                                    gu_bank_free[bk] = b.opi("scalar", lambda e, bk=bk, sl=sl: e.activation(
                                        out=gl[:, sl * 512:(sl + 1) * 512], in_=bank(bk)[:, :],
                                        func=AF.Gelu_apprx_tanh), "ac")
                                else:
                                    gu_bank_free[bk] = b.opi("scalar", lambda e, bk=bk, sl=sl: e.activation(
                                        out=up_sb[:, sl * 512:(sl + 1) * 512], in_=bank(bk)[:, :],
                                        func=AF.Copy), "ac")
                        decuse[mat] = b.cnt["pe"]
                        if mat == "u":
                            base = 4 * npar + 2 * ww
                            b.wait("vector", "ac")
                            inter_tt[0] = b.opi("vector", lambda e, base=base: e.tensor_tensor(
                                out=inter[:, base:base + 2, :],
                                in0=gl[:].rearrange("p (a c) -> p a c", a=2),
                                in1=up_sb[:].rearrange("p (a c) -> p a c", a=2),
                                op=ALU.mult), "dv")
            inter_done = b.cnt["dv"]
            if s_i == 0:
                dbg_dump("inter0", inter)
            for q in range(4):
                for hf in range(2):
                    blk = q * 2 + hf
                    wdb = wddec if blk % 2 == 0 else wddec2
                    nm = "d_sc_a" if hf == 0 else "d_sc_b"
                    src = AP(tensor=d[nm].tensor, offset=(512 * s_i) * 1024 + 256 * q,
                             ap=[[1024, 128], [128 * 1024, 4], [1, 256]])
                    b.wait("sync", "dv", scb_war["dn"])
                    dsd = b.dma("sync", dsc[:], src, "wl")
                    b.wait("vector", "wl", dsd)
                    b.wait("vector", "gl", dpk_done[(s_i, q)])
                    b.wait("vector", "pe", wd_use[blk % 2])
                    v16 = dpk[:, :, 512 * q:512 * q + 512].bitcast(U16)
                    wd_dv = dec_batch(v16, 8 * hf, wdb[:, 0:4, :], wdb[:, 4:8, :],
                                      dsc[:, :, :], dscb[:, :, :], 4, "dn", sem="dv")
                    if hf == 1 and s_i + 1 < NS:
                        b.wait("gpsimd", "dv", wd_dv)
                        dpk_dma(s_i + 1, q)
                    b.wait("tensor", "dv", wd_dv)
                    if q == 0 and hf == 0:
                        b.wait("tensor", "dv", inter_done)
                    for hci in range(2):
                        hc = hf * 8 + 2 * q + hci
                        for th in range(2):
                            bk = 4 + ((hci * 2 + th) % 2)
                            b.wait("tensor", "dv", dn_bank_free[bk])
                            for sl in range(8):
                                jc, nib = sl % 4, sl // 4
                                islot = 4 * nib + jc
                                b.opi("tensor", lambda e, sl=sl, hci=hci, th=th, bk=bk, islot=islot, wdb=wdb: e.matmul(
                                    bank(bk)[:, :], wdb[:, sl, hci * 128:(hci + 1) * 128],
                                    inter[:, islot, th * 512:(th + 1) * 512],
                                    start=(sl == 0), stop=(sl == 7)), "pe")
                            b.wait("vector", "pe")
                            dn_bank_free[bk] = b.opi("vector", lambda e, hc=hc, th=th, bk=bk: e.tensor_tensor(
                                out=xT[:, hc, th * 512:(th + 1) * 512],
                                in0=xT[:, hc, th * 512:(th + 1) * 512], in1=bank(bk)[:, :],
                                op=ALU.add), "dv")
                    wd_use[blk % 2] = b.cnt["pe"]

        b.wait("tensor", "dv", max(dn_bank_free.values()))
        b.barrier()

        st_done = {0: 0, 1: 0}
        tr_free = {0: 0, 1: 0}
        for tc in range(NT):
            buf = tc % 2
            b.wait("vector", "sl", st_done[buf])
            evacs = []
            for g4 in range(4):
                bk = g4 % 2
                b.wait("tensor", "dv2", tr_free[bk])
                for k4 in range(4):
                    fc = g4 * 4 + k4
                    b.opi("tensor", lambda e, bk=bk, k4=k4, fc=fc, tc=tc: e.transpose(
                        bank(bk)[:, k4 * P:(k4 + 1) * P], xT[:, fc, tc * P:(tc + 1) * P],
                        c_idf[:]), "pe")
                b.wait("vector", "pe")
                last = 0
                for k4 in range(4):
                    fc = g4 * 4 + k4
                    m, par = fc // 2, fc % 2
                    dstv = ostg[:, buf, 256 * m + par: 256 * (m + 1): 2]
                    last = b.opi("vector", lambda e, bk=bk, k4=k4, dstv=dstv: e.tensor_copy(
                        out=dstv, in_=bank(bk)[:, k4 * P:(k4 + 1) * P]), "dv2")
                tr_free[bk] = last
            b.wait("scalar", "dv2")
            st_done[buf] = b.dma("scalar", d_out[tc * P:(tc + 1) * P, :], ostg[:, buf, :], "sl")
        b.wait("scalar", "sl")

        import os
        if os.environ.get("K_PRINT_CNT"):
            print("SEM COUNTS:", {k: v for k, v in b.cnt.items()})
            print("Q LENGTHS:", {k: len(v) for k, v in b.q.items()})
        @block.sync
        def _(eng):
            for f in b.q["sync"]:
                f(eng)

        @block.vector
        def _(eng):
            for f in b.q["vector"]:
                f(eng)

        @block.scalar
        def _(eng):
            for f in b.q["scalar"]:
                f(eng)

        @block.tensor
        def _(eng):
            for f in b.q["tensor"]:
                f(eng)

        @block.gpsimd
        def _(eng):
            for f in b.q["gpsimd"]:
                f(eng)

    return nc


_NC_CACHE = [None]


def kernel(**inputs):
    in_maps = _host_prep(inputs)
    if _NC_CACHE[0] is None:
        _NC_CACHE[0] = build_nc()
    nc = _NC_CACHE[0]
    res = run_bass_kernel_spmd(nc, in_maps, core_ids=list(range(8)))
    out = np.empty((B, S, D), np.float32)
    for core in range(8):
        bb, h = core // 2, core % 2
        out[bb, h * T:(h + 1) * T] = res.results[core]["out"]
    return out
